# version 30
# baseline (speedup 1.0000x reference)
import numpy as np
import ml_dtypes
import contextlib
import concourse.bass as bass
import concourse.mybir as mybir
from concourse.bass_utils import run_bass_kernel_spmd

F32 = mybir.dt.float32
BF16 = mybir.dt.bfloat16
I32 = mybir.dt.int32
AF = mybir.ActivationFunctionType
ALU = mybir.AluOpType
AX = mybir.AxisListType

B, S, D = 4, 4096, 1024
T = B * S
TOK = 2048
E, DFF = 8, 4096
CAP = 2560
NCORE = 8
KT = 8
NTT = 16

DEV_STAGE = 4

ENGS = ("sync", "tensor", "vector", "scalar", "gpsimd")


class Prog:
    def __init__(self):
        self.ops = {e: [] for e in ENGS}

    def install(self, block):
        for name in ENGS:
            def body(eng, _name=name):
                for f in self.ops[_name]:
                    f(eng)
            getattr(block, name)(body)


class Sem:
    def __init__(self, handle):
        self.h = handle
        self.n = 0

    def bump(self, k=1):
        self.n += k
        return self.n


def build_nc():
    nc = bass.Bass("TRN2", target_bir_lowering=False, debug=False)

    def ein(name, shape, dt=F32):
        return nc.dram_tensor(name, shape, dt, kind="ExternalInput").ap()

    x0_my = ein("x0_my", [D, TOK])
    wqkv_my = ein("wqkv_my", [D, 1536])
    w_out_in = ein("w_out", [D, D])
    w1_my = ein("w1_my", [D, DFF])
    w2_my = ein("w2_my", [DFF, D])
    wbd_my = ein("wbd_my", [12 * 128, 128])
    SMALL = {
        "bqkv_col": ([128, 8], F32), "bqkv_vrow": ([1, 512], F32),
        "bout_col": ([128, 8], F32), "wg_t": ([128, 64], F32),
        "b1_col": ([128, 32], F32), "b2_row": ([1, D], F32),
        "consts": ([128, 8], F32), "tri128i_f": ([128, 128], F32),
        "tri128i_b": ([128, 128], BF16), "id128_b": ([128, 128], BF16),
        "id128_f": ([128, 128], F32), "stri8": ([8, 8], F32),
        "tri8i": ([8, 8], F32), "ones_c_f": ([128, 1], F32),
        "ones_c_b": ([128, 1], BF16), "ones_r_f": ([1, 128], F32),
        "ones_r_b": ([1, 128], BF16), "colidx8": ([128, 8], F32),
        "rank1h": ([8, 1], F32), "rank2048": ([8, 1], F32),
        "c_iota": ([128, 20], F32), "tileoff_m": ([128, 128], F32),
        "ones_sq": ([128, 128], F32),
        "esum_m": ([128, 8], F32),
        "ln1_idx": ([128, 16], I32), "o_idx": ([128, 8], I32),
        "ln3_idx": ([128, 8], I32), "h_idx": ([128, 8], I32),
        "sc1_idx": ([128, 8], I32), "scO_idx": ([128, 8], I32),
        "sch_idx": ([128, 4], I32),
    }
    ext_small = {nm: ein(nm, shp, dt) for nm, (shp, dt) in SMALL.items()}

    out_x = nc.dram_tensor("out_x", [D, TOK], F32, kind="ExternalOutput").ap()

    def dr(name, shape, dt=BF16, shared=False):
        if shared:
            return nc.dram_tensor(name, shape, dt, addr_space="Shared").ap()
        return nc.dram_tensor(name, shape, dt).ap()

    cc_ln1p = dr("cc_ln1p", [2 * D, TOK], shared=True)
    bar_src = nc.dram_tensor("bar_src", [1, 8], F32).ap()
    BARS = [nc.dram_tensor(f"bar{i}", [2, 8], F32).ap() for i in range(6)]
    PAIR_RG = [[0, 1], [2, 3], [4, 5], [6, 7]]
    cc_Op = dr("cc_Op", [1024, S], shared=True)
    cc_ln3p = dr("cc_ln3p", [2 * D, TOK], shared=True)
    cc_hp = dr("cc_hp", [1024, S], shared=True)
    bar_full = nc.dram_tensor("bar_full", [8, 8], F32, addr_space="Shared").ap()
    w1bf = dr("w1bf", [D, DFF])
    w2bf = dr("w2bf", [DFF, D])
    FBD = []
    for a in range(2):
        FBD.append(dict(
            cnt_src=dr(f"cc_cnt_src{a}", [1, 8], F32),
            cnt=dr(f"cc_cnt{a}", [8, 8], F32, shared=True),
            g_src=dr(f"cc_g_src{a}", [TOK, D]),
            g=dr(f"cc_g{a}", [T, D], shared=True),
            eo_src=dr(f"cc_eo_src{a}", [CAP, D]),
            eo=dr(f"cc_eo{a}", [NCORE * CAP, D], shared=True),
        ))

    prog = Prog()
    st = contextlib.ExitStack()
    with st:
        AR_COLS = 82944
        AR = st.enter_context(nc.sbuf_tensor("AR", [128, AR_COLS], BF16))

        def abf(c0, n):
            assert 0 <= c0 and c0 + n <= AR_COLS, (c0, n)
            return AR[:, c0:c0 + n]

        def af32(c0, n):
            assert c0 % 2 == 0 and c0 + 2 * n <= AR_COLS, (c0, n)
            return AR[:, c0:c0 + 2 * n].bitcast(F32)

        def t3d(ap2d):
            return ap2d.rearrange("(kt p) t -> p kt t", p=128)

        def s3d(apslice, inner):
            return apslice.rearrange("p (kt t) -> p kt t", t=inner)

        P_ = {nm: st.enter_context(nc.sbuf_tensor("sb_" + nm, shp, dt))
              for nm, (shp, dt) in SMALL.items()}
        for nm, shp, dt in [
            ("bvrow_b", [1, 512], BF16), ("b2row_b", [1, D], BF16),
            ("stats", [128, 16], F32), ("lnscr", [128, 1536], F32),
            ("mrow", [1, TOK], F32), ("rrow", [1, TOK], F32),
            ("route", [128, 176], F32), ("cnt_all", [8, 8], F32),
            ("sm8", [8, 48], F32), ("row8", [1, 160], F32),
            ("wbd", [128, 12 * 128], BF16), ("ri", [128, 64], I32),
        ]:
            P_[nm] = st.enter_context(nc.sbuf_tensor(nm, shp, dt))

        PS = [st.enter_context(nc.psum_tensor(f"ps{i}", [128, 1024], F32)) for i in range(4)]
        PSB = [p.ap().bitcast(BF16) for p in PS]

        semnames = ["sQ", "sG", "sW0", "sW1", "sCC", "sT", "sV", "sS", "sG2",
                    "sCONST", "sXA", "sXB", "sWC", "sLN1C", "sLN1G", "sOC",
                    "sP3", "sCNT", "sSCAT", "sEIN", "sEO0", "sEO1", "sCMB0",
                    "sCMB1", "sLN3C", "sL3G", "sHC", "sHG", "sOUT", "sOC1", "sOC2", "sOC3", "sW2", "sW3", "sP"]
        sems = {nm: Sem(st.enter_context(nc.semaphore(nm))) for nm in semnames}
        (sQ, sG, sW0, sW1, sCC, sT, sV, sS, sG2, sCONST, sXA, sXB, sWC, sLN1C,
         sLN1G, sOC, sP3, sCNT, sSCAT, sEIN, sEO0, sEO1, sCMB0, sCMB1, sLN3C,
         sL3G, sHC, sHG, sOUT, sOC1, sOC2, sOC3, sW2, sW3, sP) = (sems[n] for n in semnames)

        block = st.enter_context(nc.Block())

        def on(eng_name):
            def deco(fn):
                prog.ops[eng_name].append(fn)
            return deco

        def barrier():
            vals = [(s, s.n) for s in sems.values()]
            for en in ENGS:
                @on(en)
                def _(eng, vals=tuple(vals)):
                    for s, v in vals:
                        if v:
                            eng.wait_ge(s.h, v)

        CB = P_["consts"]
        RT = P_["route"]
        SR = P_["stats"]

        @on("sync")
        def _(eng):
            for nm in SMALL:
                eng.dma_start(out=P_[nm][:], in_=ext_small[nm]).then_inc(sCONST.h, 16)
            eng.wait_ge(sCONST.h, 16 * len(SMALL))
            eng.dma_start(out=bar_src, in_=P_["consts"][0:1, 0:8]).then_inc(sCONST.h, 16)
        consts_done = sCONST.bump(16 * len(SMALL))
        barsrc_done = sCONST.bump(16)

        @on("vector")
        def _(eng):
            eng.wait_ge(sCONST.h, consts_done)
            eng.tensor_copy(P_["bvrow_b"][:], P_["bqkv_vrow"][:])
            last = eng.tensor_copy(P_["b2row_b"][:], P_["b2_row"][:])
            last.then_inc(sV.h, 1)
        rowcast_done = sV.bump()

        BR = {}
        @on("gpsimd")
        def _(eng):
            for v in (TOK - 1, D - 1, NCORE * D - 1, T - 1, NCORE * CAP - 1):
                BR[v] = eng.to_reg(v)
            eng.dma_start(out=w1bf, in_=w1_my).then_inc(sWC.h, 16)
            eng.dma_start(out=w2bf, in_=w2_my).then_inc(sWC.h, 16)
            eng.dma_start(out=s3d(P_["wbd"][:], 128),
                          in_=wbd_my.rearrange("(t p) c -> p t c", p=128)).then_inc(sWC.h, 16)
        wcache_done = sWC.bump(48)
        @on("gpsimd")
        def _(eng, bd=barsrc_done):
            eng.wait_ge(sCONST.h, bd)
            eng.collective_compute("AllGather", ALU.bypass,
                                   replica_groups=[list(range(NCORE))],
                                   ins=[bar_src.opt()], outs=[bar_full.opt()]
                                   ).then_inc(sCC.h, 1)
        sync0 = sCC.bump()

        XRES0 = AR_COLS - 32768
        xres = [af32(XRES0 + 4096 * kt, TOK) for kt in range(KT)]

        def emit_layernorm(xsrc, lnout, scr0, need_rows=False):
            xbf = [abf(scr0 + 512 * kt, 512) for kt in range(KT)]
            xsq = [abf(scr0 + 4096 + 512 * kt, 512) for kt in range(KT)]
            sx_slot = [PS[2][0:1, 0:512], PS[2][0:1, 512:1024],
                       PS[3][0:1, 0:512], PS[3][0:1, 512:1024]]
            sq_slot = [PS[0][0:1, 0:512], PS[0][0:1, 512:1024],
                       PS[1][0:1, 0:512], PS[1][0:1, 512:1024]]
            MR, RR = P_["mrow"], P_["rrow"]
            mm_done = {}
            cp_done = {}
            sq_done = {}
            for c in range(4):
                c0 = 512 * c
                wmm = mm_done.get(c - 1, 0)
                @on("vector")
                def _(eng, c0=c0, wmm=wmm):
                    if wmm:
                        eng.wait_ge(sT.h, wmm)
                    for kt in range(KT):
                        last = eng.tensor_copy(xbf[kt], xsrc[kt][:, c0:c0 + 512])
                    last.then_inc(sV.h, 1)
                cp_done[c] = sV.bump()
                @on("scalar")
                def _(eng, c0=c0, wmm=wmm):
                    if wmm:
                        eng.wait_ge(sT.h, wmm)
                    for kt in range(KT):
                        last = eng.activation(xsq[kt], xsrc[kt][:, c0:c0 + 512], AF.Square)
                    last.then_inc(sS.h, 1)
                sq_done[c] = sS.bump()
                @on("tensor")
                def _(eng, c=c, vdone=cp_done[c], sdone=sq_done[c]):
                    eng.wait_ge(sV.h, vdone)
                    eng.wait_ge(sS.h, sdone)
                    for kt in range(KT):
                        eng.matmul(sx_slot[c], P_["ones_c_b"][:], xbf[kt],
                                   start=(kt == 0), stop=(kt == KT - 1))
                    for kt in range(KT):
                        last = eng.matmul(sq_slot[c], P_["ones_c_b"][:], xsq[kt],
                                   start=(kt == 0), stop=(kt == KT - 1))
                    last.then_inc(sT.h, 1)
                mm_done[c] = sT.bump()
            all_mm = mm_done[3]
            @on("vector")
            def _(eng, am=all_mm):
                eng.wait_ge(sT.h, am)
                eng.tensor_scalar_mul(MR[0:1, 0:1024], PS[2][0:1, 0:1024], 1.0 / 1024.0)
                last = eng.tensor_scalar_mul(MR[0:1, 1024:2048], PS[3][0:1, 0:1024],
                                             1.0 / 1024.0)
                last.then_inc(sV.h, 1)
            v_m = sV.bump()
            @on("vector")
            def _(eng, v_m=v_m):
                eng.wait_ge(sV.h, v_m)
                eng.tensor_mul(RR[0:1, 0:2048], MR[0:1, 0:2048], MR[0:1, 0:2048]).then_inc(sV.h, 1)
            v_m2 = sV.bump()
            @on("vector")
            def _(eng, v_m2=v_m2):
                eng.wait_ge(sV.h, v_m2)
                eng.scalar_tensor_tensor(RR[0:1, 0:1024], PS[0][0:1, 0:1024],
                                         1.0 / 1024.0, RR[0:1, 0:1024],
                                         ALU.mult, ALU.subtract)
                last = eng.scalar_tensor_tensor(RR[0:1, 1024:2048], PS[1][0:1, 0:1024],
                                         1.0 / 1024.0, RR[0:1, 1024:2048],
                                         ALU.mult, ALU.subtract)
                last.then_inc(sV.h, 1)
            v_var = sV.bump()
            @on("scalar")
            def _(eng, v_var=v_var):
                eng.wait_ge(sV.h, v_var)
                eng.activation(RR[0:1, 0:2048], RR[0:1, 0:2048], AF.Ln,
                               bias=CB[0:1, 0:1]).then_inc(sS.h, 1)
            s_ln = sS.bump()
            @on("scalar")
            def _(eng, s_ln=s_ln):
                eng.wait_ge(sS.h, s_ln)
                eng.activation(RR[0:1, 0:2048], RR[0:1, 0:2048], AF.Exp,
                               scale=-0.5).then_inc(sS.h, 1)
            s_r = sS.bump()
            norm_done = {}
            for c in range(4):
                c0 = 512 * c
                bps = PS[2] if c % 2 == 0 else PS[3]
                wn = norm_done.get(c - 2, 0)
                @on("tensor")
                def _(eng, c0=c0, bps=bps, s_r=s_r, wn=wn):
                    eng.wait_ge(sS.h, s_r)
                    if wn:
                        eng.wait_ge(sV.h, wn)
                    eng.matmul(bps[:, 0:512], P_["ones_r_f"][0:1, :],
                               MR[0:1, c0:c0 + 512], start=True, stop=True)
                    last = eng.matmul(bps[:, 512:1024], P_["ones_r_f"][0:1, :],
                               RR[0:1, c0:c0 + 512], start=True, stop=True)
                    last.then_inc(sT.h, 1)
                t_bc = sT.bump()
                for kt in range(KT):
                    @on("vector")
                    def _(eng, kt=kt, c0=c0, bps=bps, t_bc=t_bc, vprev=sV.n):
                        if kt == 0:
                            eng.wait_ge(sT.h, t_bc)
                        eng.wait_ge(sV.h, vprev)
                        eng.tensor_sub(P_["lnscr"][:, 0:512], xsrc[kt][:, c0:c0 + 512],
                                       bps[:, 0:512]).then_inc(sV.h, 1)
                    va = sV.bump()
                    @on("vector")
                    def _(eng, kt=kt, c0=c0, bps=bps, va=va):
                        eng.wait_ge(sV.h, va)
                        eng.tensor_mul(lnout[kt][:, c0:c0 + 512], P_["lnscr"][:, 0:512],
                                       bps[:, 512:1024]).then_inc(sV.h, 1)
                    sV.bump()
                norm_done[c] = sV.n
            vX = sV.n
            @on("tensor")
            def _(eng, vX=vX):
                eng.wait_ge(sV.h, vX)

        @on("sync")
        def _(eng):
            eng.dma_start(out=AR[:, XRES0:XRES0 + 32768].bitcast(F32)
                          .rearrange("p (kt t) -> p kt t", t=2048),
                          in_=x0_my.rearrange("(kt p) t -> p kt t", p=128)
                          ).then_inc(sXA.h, 16)
        x0_done = sXA.bump(16)

        ln1out = [abf(TOK * kt, TOK) for kt in range(KT)]
        for en in ("vector", "scalar", "tensor"):
            @on(en)
            def _(eng):
                eng.wait_ge(sXA.h, x0_done)
                eng.wait_ge(sCONST.h, consts_done)
        emit_layernorm(xres, ln1out, 16384)
        ln1_v = sV.n

        @on("gpsimd")
        def _(eng, lv=ln1_v):
            eng.wait_ge(sV.h, lv)
            for j in range(KT):
                eng.indirect_dma_start(
                    out=cc_ln1p,
                    out_offset=bass.IndirectOffsetOnAxis(P_["sc1_idx"][:, j:j + 1], 0),
                    in_=ln1out[j], in_offset=None,
                    bounds_check=BR[TOK - 1], oob_is_err=False,
                ).then_inc(sLN1C.h, 16)
        c1 = sLN1C.bump(16 * KT)
        @on("gpsimd")
        def _(eng, c1=c1, bd=barsrc_done):
            eng.wait_ge(sLN1C.h, c1)
            eng.wait_ge(sCONST.h, bd)
            eng.collective_compute("AllGather", ALU.bypass,
                                   replica_groups=PAIR_RG,
                                   ins=[bar_src.opt()], outs=[BARS[0].opt()]
                                   ).then_inc(sCC.h, 1)
        ag_ln1 = sCC.bump()

        ln1h = [abf(2048 * kt, 2048) for kt in range(KT)]
        wqkv_sb = abf(16384, 12288)
        qk_t = [abf(28672 + 4096 * i, 4096) for i in range(8)]
        v_sb = [abf(61440 + 520 * tt, 520) for tt in range(32)]

        @on("gpsimd")
        def _(eng):
            eng.wait_ge(sCONST.h, consts_done)
            eng.dma_start(out=s3d(wqkv_sb, 1536),
                          in_=wqkv_my.rearrange("(kt p) c -> p kt c", p=128)
                          ).then_inc(sG2.h, 16)
        wqkv_done = sG2.bump(16)

        @on("vector")
        def _(eng):
            for tt in range(32):
                vv = v_sb[tt].rearrange("p (h c) -> p h c", c=65)
                last = eng.memset(vv[:, :, 64:65], 1.0)
            last.then_inc(sV.h, 1)
        vones_done = sV.bump()

        last_evac = {0: 0, 1: 0, 2: 0}

        def emit_qkv_half(half, war_v):
            h0 = half * 2048
            @on("gpsimd")
            def _(eng, half=half, war_v=war_v, ag=ag_ln1):
                eng.wait_ge(sCC.h, ag)
                if war_v is not None:
                    eng.wait_ge(sV.h, war_v)
                for j in range(8):
                    eng.indirect_dma_start(
                        out=ln1h[j], out_offset=None, in_=cc_ln1p,
                        in_offset=bass.IndirectOffsetOnAxis(
                            P_["ln1_idx"][:, 8 * half + j:8 * half + j + 1], 0),
                        bounds_check=BR[TOK - 1], oob_is_err=False,
                    ).then_inc(sLN1G.h, 16)
            ld = sLN1G.bump(16 * 8)
            for ct in range(8):
                for fc in range(4):
                    pidx = (ct * 4 + fc) % 2
                    ps = PS[pidx]
                    lev = last_evac[pidx]
                    @on("tensor")
                    def _(eng, ct=ct, fc=fc, ps=ps, ld=ld, lev=lev, wq=wqkv_done):
                        eng.wait_ge(sLN1G.h, ld)
                        eng.wait_ge(sG2.h, wq)
                        if lev:
                            eng.wait_ge(sV.h, lev)
                        for kt in range(KT):
                            last = eng.matmul(ps[:, 0:512],
                                       wqkv_sb[:, 1536 * kt + 128 * ct:1536 * kt + 128 * ct + 128],
                                       ln1h[kt][:, 512 * fc:512 * fc + 512],
                                       start=(kt == 0), stop=(kt == KT - 1))
                        last.then_inc(sT.h, 1)
                    td = sT.bump()
                    @on("vector")
                    def _(eng, ct=ct, fc=fc, ps=ps, td=td, h0=h0):
                        eng.wait_ge(sT.h, td)
                        eng.tensor_scalar(qk_t[ct][:, h0 + 512 * fc:h0 + 512 * fc + 512],
                                          ps[:, 0:512], P_["bqkv_col"][:, ct:ct + 1],
                                          None, ALU.add).then_inc(sV.h, 1)
                    last_evac[pidx] = sV.bump()
            for tt_loc in range(16):
                tt = half * 16 + tt_loc
                lev = last_evac[2]
                @on("tensor")
                def _(eng, tt_loc=tt_loc, ld=ld, lev=lev, von=vones_done,
                      rc=rowcast_done):
                    eng.wait_ge(sV.h, max(von, rc))
                    eng.wait_ge(sLN1G.h, ld)
                    if lev:
                        eng.wait_ge(sV.h, lev)
                    last = eng.matmul(PS[2][:, 0:512], P_["ones_r_b"][0:1, :],
                               P_["bvrow_b"][0:1, :], start=True, stop=False)
                    for kt in range(KT):
                        last = eng.matmul(PS[2][:, 0:512],
                                   ln1h[kt][:, 128 * tt_loc:128 * tt_loc + 128],
                                   wqkv_sb[:, 1536 * kt + 1024:1536 * kt + 1536],
                                   start=False, stop=(kt == KT - 1))
                    last.then_inc(sT.h, 1)
                td = sT.bump()
                @on("vector")
                def _(eng, tt=tt, td=td):
                    eng.wait_ge(sT.h, td)
                    vv = v_sb[tt].rearrange("p (h c) -> p h c", c=65)
                    for hu in range(8):
                        last = eng.tensor_copy(vv[:, hu, 0:64],
                                               PS[2][:, 64 * hu:64 * hu + 64])
                    last.then_inc(sV.h, 1)
                last_evac[2] = sV.bump()
            return sV.n

        v_half0 = emit_qkv_half(0, None)
        emit_qkv_half(1, v_half0)
        barrier()

        p_sb = [abf(16384 + 1024 * i, 1024) for i in range(3)]
        o_sb = af32(20480, 1024)
        rrec = af32(24576, 1024)
        oacc = [[abf(8192 * p + 4096 * i, 4096) for i in range(2)]
                for p in range(2)]
        odma_done = {0: 0, 1: 0}

        exp_par = {0: 0, 1: 0, 2: 0}
        pv_par = {0: 0, 1: 0, 2: 0}
        PSS = [PS[0], PS[1], PS[3]]
        v_oc_last = 0

        def emit_st(hp, g, kk, qtile, ktile):
            par = kk % 3
            ps_s = PSS[par]
            q0 = 512 * g
            w_exp = exp_par[par]
            @on("tensor")
            def _(eng, kk=kk, ps_s=ps_s, qtile=qtile, ktile=ktile, q0=q0, w_exp=w_exp):
                if w_exp:
                    eng.wait_ge(sS.h, w_exp)
                eng.matmul(ps_s[:, 0:512],
                           ktile[0:64, 128 * kk:128 * kk + 128],
                           qtile[0:64, q0:q0 + 512], start=True, stop=True)
                last = eng.matmul(ps_s[:, 512:1024],
                           ktile[64:128, 128 * kk:128 * kk + 128],
                           qtile[64:128, q0:q0 + 512], start=True, stop=True)
                last.then_inc(sT.h, 1)
            return sT.bump()

        def emit_exp_mask(hp, g, kk, t_st):
            par = kk % 3
            ps_s = PSS[par]
            w_pv = pv_par[par]
            @on("scalar")
            def _(eng, par=par, ps_s=ps_s, t_st=t_st, w_pv=w_pv):
                eng.wait_ge(sT.h, t_st)
                if w_pv:
                    eng.wait_ge(sT.h, w_pv)
                eng.activation(p_sb[par], ps_s[:, 0:1024], AF.Exp,
                               scale=0.125).then_inc(sS.h, 1)
            exp_par[par] = s_exp = sS.bump()
            v_mask = None
            if kk >= 4 * g:
                j = kk - 4 * g
                @on("vector")
                def _(eng, par=par, j=j, s_exp=s_exp):
                    eng.wait_ge(sS.h, s_exp)
                    if j > 0:
                        eng.memset(p_sb[par][:, 0:128 * j], 0.0)
                        last = eng.memset(p_sb[par][:, 512:512 + 128 * j], 0.0)
                        last.then_inc(sV.h, 1)
                    else:
                        eng.engine_nop().then_inc(sV.h, 1)
                vm1 = sV.bump()
                @on("vector")
                def _(eng, par=par, j=j, vm1=vm1):
                    eng.wait_ge(sV.h, vm1)
                    eng.tensor_mul(p_sb[par][:, 128 * j:128 * j + 128],
                                   p_sb[par][:, 128 * j:128 * j + 128],
                                   P_["tri128i_b"][:])
                    last = eng.tensor_mul(p_sb[par][:, 512 + 128 * j:512 + 128 * j + 128],
                                   p_sb[par][:, 512 + 128 * j:512 + 128 * j + 128],
                                   P_["tri128i_b"][:])
                    last.then_inc(sV.h, 1)
                v_mask = sV.bump()
            return s_exp, v_mask

        def emit_pv(hp, g, kk, nk, s_exp, v_mask, first_wait=0):
            par = kk % 3
            @on("tensor")
            def _(eng, hp=hp, kk=kk, nk=nk, par=par, s_exp=s_exp, v_mask=v_mask,
                  fw=first_wait):
                eng.wait_ge(sS.h, s_exp)
                if v_mask:
                    eng.wait_ge(sV.h, v_mask)
                if fw:
                    eng.wait_ge(sV.h, fw)
                vv = v_sb[kk].rearrange("p (h c) -> p h c", c=65)
                eng.matmul(PS[2][0:65, 0:512], vv[:, 2 * hp, :], p_sb[par][:, 0:512],
                           start=(kk == 0), stop=(kk == nk - 1))
                last = eng.matmul(PS[2][0:65, 512:1024], vv[:, 2 * hp + 1, :],
                           p_sb[par][:, 512:1024],
                           start=(kk == 0), stop=(kk == nk - 1))
                last.then_inc(sT.h, 1)
            pv_par[par] = v = sT.bump()
            return v

        for hp in range(4):
            qtile = qk_t[hp]
            ktile = qk_t[4 + hp]
            for g in range(8):
                nk = 4 * g + 4
                steps = {}
                for kk in range(min(3, nk)):
                    t_st = emit_st(hp, g, kk, qtile, ktile)
                    steps[kk] = emit_exp_mask(hp, g, kk, t_st)
                t_pv = None
                pend_st = {}
                for kk in range(nk):
                    if kk + 3 < nk:
                        pend_st[kk + 3] = emit_st(hp, g, kk + 3, qtile, ktile)
                    s_exp, v_mask = steps.pop(kk)
                    t_pv = emit_pv(hp, g, kk, nk, s_exp, v_mask,
                                   first_wait=(v_oc_last if kk == 0 else 0))
                    if kk + 3 < nk:
                        steps[kk + 3] = emit_exp_mask(hp, g, kk + 3, pend_st.pop(kk + 3))
                @on("vector")
                def _(eng, t_pv=t_pv):
                    eng.wait_ge(sT.h, t_pv)
                    eng.reciprocal(rrec[64:65, 0:1024], PS[2][64:65, 0:1024]).then_inc(sV.h, 1)
                vr1 = sV.bump()
                @on("scalar")
                def _(eng, t_pv=t_pv):
                    eng.wait_ge(sT.h, t_pv)
                    eng.copy(o_sb[0:64, 0:1024], PS[2][0:64, 0:1024]).then_inc(sS.h, 1)
                s_oc = sS.bump()
                @on("tensor")
                def _(eng, vr1=vr1, s_oc=s_oc):
                    eng.wait_ge(sV.h, vr1)
                    eng.wait_ge(sS.h, s_oc)
                    eng.matmul(PS[2][0:64, 0:512], P_["ones_sq"][64:65, 0:64],
                               rrec[64:65, 0:512], start=True, stop=True)
                    last = eng.matmul(PS[2][0:64, 512:1024], P_["ones_sq"][64:65, 0:64],
                               rrec[64:65, 512:1024], start=True, stop=True)
                    last.then_inc(sT.h, 1)
                t_bc = sT.bump()
                par2 = hp % 2
                osem = sOC if par2 == 0 else sOC1
                dmaprev = odma_done[par2]
                @on("vector")
                def _(eng, t_bc=t_bc, par2=par2, g=g, dmaprev=dmaprev, osem=osem):
                    eng.wait_ge(sT.h, t_bc)
                    if g == 0 and dmaprev:
                        eng.wait_ge(osem.h, dmaprev)
                    eng.tensor_mul(oacc[par2][0][0:64, 512 * g:512 * g + 512],
                                   o_sb[0:64, 0:512],
                                   PS[2][0:64, 0:512])
                    eng.tensor_mul(oacc[par2][1][0:64, 512 * g:512 * g + 512],
                                   o_sb[0:64, 512:1024],
                                   PS[2][0:64, 512:1024]).then_inc(sV.h, 1)
                v_oc = sV.bump()
                if g == 7:
                    @on("gpsimd")
                    def _(eng, hp=hp, par2=par2, v_oc=v_oc, osem=osem):
                        eng.wait_ge(sV.h, v_oc)
                        he, ho = 2 * hp, 2 * hp + 1
                        eng.indirect_dma_start(
                            out=cc_Op,
                            out_offset=bass.IndirectOffsetOnAxis(
                                P_["scO_idx"][0:64, he:he + 1], 0),
                            in_=oacc[par2][0][0:64, :], in_offset=None,
                            bounds_check=BR[D - 1], oob_is_err=False,
                        ).then_inc(osem.h, 16)
                        eng.indirect_dma_start(
                            out=cc_Op,
                            out_offset=bass.IndirectOffsetOnAxis(
                                P_["scO_idx"][0:64, ho:ho + 1], 0),
                            in_=oacc[par2][1][0:64, :], in_offset=None,
                            bounds_check=BR[D - 1], oob_is_err=False,
                        ).then_inc(osem.h, 16)
                    odma_done[par2] = osem.bump(32)
                v_oc_last = v_oc
        oc_dones = [(sm, sm.n) for sm in (sOC, sOC1)]

        @on("gpsimd")
        def _(eng, ods=tuple(oc_dones)):
            for sm, v in ods:
                eng.wait_ge(sm.h, v)
            eng.collective_compute("AllGather", ALU.bypass,
                                   replica_groups=PAIR_RG,
                                   ins=[bar_src.opt()], outs=[BARS[1].opt()]
                                   ).then_inc(sCC.h, 1)
        ag_O = sCC.bump()
        barrier()

        attn_fm = [abf(2048 * kt, 2048) for kt in range(KT)]
        wout_sb = abf(16384, 8192)

        @on("sync")
        def _(eng):
            eng.dma_start(out=AR[:, XRES0:XRES0 + 32768].bitcast(F32)
                          .rearrange("p (kt t) -> p kt t", t=2048),
                          in_=x0_my.rearrange("(kt p) t -> p kt t", p=128)
                          ).then_inc(sXB.h, 16)
        x0_re = sXB.bump(16)
        @on("gpsimd")
        def _(eng):
            eng.dma_start(out=s3d(wout_sb, 1024),
                          in_=w_out_in.rearrange("(kt p) c -> p kt c", p=128)
                          ).then_inc(sP3.h, 16)
        wout_done = sP3.bump(16)
        ccO_v = cc_Op.rearrange("r (h t) -> (r h) t", h=2)
        @on("gpsimd")
        def _(eng, wd=wout_done):
            eng.wait_ge(sP3.h, wd)
            for j in range(8):
                eng.indirect_dma_start(
                    out=attn_fm[j], out_offset=None, in_=ccO_v,
                    in_offset=bass.IndirectOffsetOnAxis(P_["o_idx"][:, j:j + 1], 0),
                    bounds_check=BR[TOK - 1], oob_is_err=False,
                ).then_inc(sP3.h, 16)
        attn_g = sP3.bump(16 * 8)

        for oc in range(8):
            for fc in range(4):
                pidx = (oc * 4 + fc) % 2
                ps = PS[pidx]
                lev = last_evac[pidx]
                @on("tensor")
                def _(eng, oc=oc, fc=fc, ps=ps, ag=attn_g, x0r=x0_re, lev=lev):
                    eng.wait_ge(sP3.h, ag)
                    eng.wait_ge(sXB.h, x0r)
                    if lev:
                        eng.wait_ge(sV.h, lev)
                    for kt in range(KT):
                        last = eng.matmul(ps[:, 0:512],
                                   wout_sb[:, 1024 * kt + 128 * oc:1024 * kt + 128 * oc + 128],
                                   attn_fm[kt][:, 512 * fc:512 * fc + 512],
                                   start=(kt == 0), stop=(kt == KT - 1))
                    last.then_inc(sT.h, 1)
                td = sT.bump()
                @on("vector")
                def _(eng, oc=oc, fc=fc, ps=ps, td=td):
                    eng.wait_ge(sT.h, td)
                    eng.scalar_tensor_tensor(xres[oc][:, 512 * fc:512 * fc + 512],
                                             ps[:, 0:512], P_["bout_col"][:, oc:oc + 1],
                                             xres[oc][:, 512 * fc:512 * fc + 512],
                                             ALU.add, ALU.add).then_inc(sV.h, 1)
                last_evac[pidx] = sV.bump()
        barrier()

        def emit_output():
            barrier()
            @on("sync")
            def _(eng):
                eng.dma_start(out=out_x.rearrange("(kt p) t -> p kt t", p=128),
                              in_=AR[:, XRES0:XRES0 + 32768].bitcast(F32)
                              .rearrange("p (kt t) -> p kt t", t=2048)
                              ).then_inc(sOUT.h, 16)
            od = sOUT.bump(16)
            @on("sync")
            def _(eng, od=od):
                eng.wait_ge(sOUT.h, od)

        if DEV_STAGE == 1:
            emit_output()
            prog.install(block)
            return nc

        def emit_ffn(app):
            FA = FBD[app]
            lnbf = [abf(2048 * kt, 2048) for kt in range(KT)]
            lnTM = [abf(24576 + 1024 * tt, 1024) for tt in range(NTT)]
            lnf32 = [af32(40960 + 1024 * kt, 512) for kt in range(KT)]
            logitsTM = af32(16384, 128)
            ohall = af32(16640, 128)
            cums = af32(16896, 128)
            selscr = af32(17152, 128)
            emit_layernorm(xres, lnbf, 18432, need_rows=True)
            barrier()
            tp_evac = 0
            for kt in range(KT):
                for tt in range(NTT):
                    par = (kt * NTT + tt) % 2
                    @on("tensor")
                    def _(eng, kt=kt, tt=tt, par=par, tp_evac=tp_evac):
                        if tp_evac:
                            eng.wait_ge(sV.h, tp_evac)
                        last = eng.transpose(PSB[par][:, 0:128],
                                      lnbf[kt][:, 128 * tt:128 * tt + 128],
                                      P_["id128_b"][:])
                        last.then_inc(sT.h, 1)
                    t_tp = sT.bump()
                    @on("vector")
                    def _(eng, kt=kt, tt=tt, par=par, t_tp=t_tp):
                        eng.wait_ge(sT.h, t_tp)
                        eng.tensor_copy(lnTM[tt][:, 128 * kt:128 * kt + 128],
                                        PSB[par][:, 0:128]).then_inc(sV.h, 1)
                    newv = sV.bump()
                    if par == 1:
                        tp_evac = newv
            lnbf_free = sT.n

            for c in range(4):
                c0 = 512 * c
                @on("tensor")
                def _(eng, c0=c0):
                    last = eng.matmul(PS[3][:, 0:512], P_["ones_r_f"][0:1, :],
                               P_["mrow"][0:1, c0:c0 + 512], start=True, stop=True)
                    last = eng.matmul(PS[3][:, 512:1024], P_["ones_r_f"][0:1, :],
                               P_["rrow"][0:1, c0:c0 + 512], start=True, stop=True)
                    last.then_inc(sT.h, 1)
                tb = sT.bump()
                for kt in range(KT):
                    @on("vector")
                    def _(eng, kt=kt, c0=c0, tb=tb, vprev=sV.n):
                        if kt == 0:
                            eng.wait_ge(sT.h, tb)
                        eng.wait_ge(sV.h, vprev)
                        eng.tensor_sub(P_["lnscr"][:, 0:512], xres[kt][:, c0:c0 + 512],
                                       PS[3][:, 0:512]).then_inc(sV.h, 1)
                    va = sV.bump()
                    @on("vector")
                    def _(eng, kt=kt, va=va):
                        eng.wait_ge(sV.h, va)
                        eng.tensor_mul(lnf32[kt], P_["lnscr"][:, 0:512],
                                       PS[3][:, 512:1024]).then_inc(sV.h, 1)
                    sV.bump()
                vls = sV.n
                @on("tensor")
                def _(eng, vls=vls):
                    eng.wait_ge(sV.h, vls)
                    for kt in range(KT):
                        last = eng.matmul(PS[2][0:8, 0:512], P_["wg_t"][:, 8 * kt:8 * kt + 8],
                                   lnf32[kt], start=(kt == 0), stop=(kt == KT - 1))
                    last.then_inc(sT.h, 1)
                t_log = sT.bump()
                @on("vector")
                def _(eng, t_log=t_log):
                    eng.wait_ge(sT.h, t_log)
                    eng.tensor_copy(P_["lnscr"][0:8, 0:512], PS[2][0:8, 0:512]).then_inc(sV.h, 1)
                v_log = sV.bump()
                for tl in range(4):
                    tt = 4 * c + tl
                    @on("tensor")
                    def _(eng, tl=tl, v_log=v_log, vprev=sV.n):
                        eng.wait_ge(sV.h, v_log)
                        eng.wait_ge(sV.h, vprev)
                        last = eng.transpose(PS[2][:, 512:520],
                                      P_["lnscr"][0:8, 128 * tl:128 * tl + 128],
                                      P_["id128_f"][0:8, 0:8])
                        last.then_inc(sT.h, 1)
                    t_tp = sT.bump()
                    @on("vector")
                    def _(eng, tt=tt, t_tp=t_tp):
                        eng.wait_ge(sT.h, t_tp)
                        eng.tensor_copy(logitsTM[:, 8 * tt:8 * tt + 8],
                                        PS[2][:, 512:520]).then_inc(sV.h, 1)
                    sV.bump()
            logits_v = sV.n

            lv3 = logitsTM.rearrange("p (t e) -> p t e", e=8)
            @on("vector")
            def _(eng, lv=logits_v):
                eng.wait_ge(sV.h, lv)
                eng.reduce_max(RT[:, 0:16], lv3, AX.X).then_inc(sV.h, 1)
            v_m = sV.bump()
            @on("vector")
            def _(eng, v_m=v_m):
                eng.wait_ge(sV.h, v_m)
                for tt in range(NTT):
                    last = eng.tensor_scalar(ohall[:, 8 * tt:8 * tt + 8],
                                             logitsTM[:, 8 * tt:8 * tt + 8],
                                             RT[:, tt:tt + 1], None, ALU.is_ge)
                last.then_inc(sV.h, 1)
            v_oh = sV.bump()
            @on("scalar")
            def _(eng, lv=logits_v, v_m=v_m):
                eng.wait_ge(sV.h, v_m)
                eng.activation(selscr[:, 0:128], logitsTM[:, 0:128], AF.Exp).then_inc(sS.h, 1)
            s_el = sS.bump()
            @on("scalar")
            def _(eng, v_m=v_m):
                eng.wait_ge(sV.h, v_m)
                eng.activation(RT[:, 16:32], RT[:, 0:16], AF.Exp).then_inc(sS.h, 1)
            s_em = sS.bump()
            @on("vector")
            def _(eng, s_el=s_el):
                eng.wait_ge(sS.h, s_el)
                eng.reduce_sum(RT[:, 32:48], selscr.rearrange("p (t e) -> p t e", e=8),
                               AX.X).then_inc(sV.h, 1)
            v_se = sV.bump()
            @on("vector")
            def _(eng, v_se=v_se):
                eng.wait_ge(sV.h, v_se)
                eng.reciprocal(RT[:, 48:64], RT[:, 32:48]).then_inc(sV.h, 1)
            v_rs = sV.bump()
            @on("vector")
            def _(eng, v_rs=v_rs, s_em=s_em):
                eng.wait_ge(sV.h, v_rs)
                eng.wait_ge(sS.h, s_em)
                eng.tensor_mul(RT[:, 64:80], RT[:, 16:32], RT[:, 48:64]).then_inc(sV.h, 1)
            v_g1 = sV.bump()

            @on("tensor")
            def _(eng, v_oh=v_oh):
                eng.wait_ge(sV.h, v_oh)
                for tt in range(NTT):
                    last = eng.matmul(PS[2][:, 512 + 8 * tt:512 + 8 * tt + 8],
                               P_["tri128i_f"][:], ohall[:, 8 * tt:8 * tt + 8],
                               start=True, stop=True)
                last = eng.matmul(PS[3][0:1, 0:128], P_["ones_c_f"][:], ohall[:, 0:128],
                           start=True, stop=True)
                last.then_inc(sT.h, 1)
            t_cs = sT.bump()
            @on("vector")
            def _(eng, t_cs=t_cs):
                eng.wait_ge(sT.h, t_cs)
                eng.tensor_copy(cums[:, 0:128], PS[2][:, 512:640]).then_inc(sV.h, 1)
            v_cums = sV.bump()
            @on("vector")
            def _(eng, t_cs=t_cs):
                eng.wait_ge(sT.h, t_cs)
                eng.tensor_copy(P_["row8"][0:1, 0:128], PS[3][0:1, 0:128]).then_inc(sV.h, 1)
            v_ts = sV.bump()
            @on("tensor")
            def _(eng, v_ts=v_ts):
                eng.wait_ge(sV.h, v_ts)
                last = eng.matmul(PS[3][:, 0:1], P_["row8"][0:1, 0:128], CB[0:1, 3:4],
                           start=True, stop=True)
                last.then_inc(sT.h, 1)
            t_tc = sT.bump()
            @on("vector")
            def _(eng, t_tc=t_tc):
                eng.wait_ge(sT.h, t_tc)
                eng.tensor_copy(SR[:, 0:1], PS[3][:, 0:1]).then_inc(sV.h, 1)
            v_tc = sV.bump()
            @on("tensor")
            def _(eng, v_tc=v_tc):
                eng.wait_ge(sV.h, v_tc)
                last = eng.matmul(PS[3][:, 4:5], P_["tileoff_m"][:], SR[:, 0:1],
                           start=True, stop=True)
                last = eng.matmul(PS[3][0:1, 8:16], SR[:, 0:1], P_["esum_m"][:],
                           start=True, stop=True)
                last.then_inc(sT.h, 1)
            t_off = sT.bump()
            @on("vector")
            def _(eng, t_off=t_off):
                eng.wait_ge(sT.h, t_off)
                eng.tensor_copy(SR[:, 1:2], PS[3][:, 4:5])
                last = eng.tensor_copy(P_["row8"][0:1, 128:136], PS[3][0:1, 8:16])
                last.then_inc(sV.h, 1)
            v_off = sV.bump()
            @on("tensor")
            def _(eng, v_off=v_off):
                eng.wait_ge(sV.h, v_off)
                last = eng.matmul(PS[3][0:1, 16:144], SR[:, 1:2], P_["id128_f"][:],
                           start=True, stop=True)
                last.then_inc(sT.h, 1)
            t_or = sT.bump()
            @on("vector")
            def _(eng, t_or=t_or):
                eng.wait_ge(sT.h, t_or)
                eng.tensor_copy(P_["row8"][0:1, 0:128], PS[3][0:1, 16:144]).then_inc(sV.h, 1)
            v_or = sV.bump()
            @on("tensor")
            def _(eng, v_or=v_or):
                eng.wait_ge(sV.h, v_or)
                last = eng.matmul(PS[3][:, 512:640], P_["ones_r_f"][0:1, :],
                           P_["row8"][0:1, 0:128], start=True, stop=True)
                last.then_inc(sT.h, 1)
            t_ob = sT.bump()
            @on("vector")
            def _(eng, v_cums=v_cums, v_oh=v_oh):
                eng.wait_ge(sV.h, v_cums)
                eng.tensor_mul(selscr[:, 0:128], ohall[:, 0:128], cums[:, 0:128]).then_inc(sV.h, 1)
            v1 = sV.bump()
            @on("vector")
            def _(eng, v1=v1):
                eng.wait_ge(sV.h, v1)
                eng.reduce_sum(RT[:, 80:96], selscr.rearrange("p (t e) -> p t e", e=8),
                               AX.X).then_inc(sV.h, 1)
            v2 = sV.bump()
            @on("vector")
            def _(eng, t_ob=t_ob, vprev=sV.n):
                eng.wait_ge(sT.h, t_ob)
                eng.wait_ge(sV.h, vprev)
                eng.tensor_mul(selscr[:, 0:128], ohall[:, 0:128],
                               PS[3][:, 512:640]).then_inc(sV.h, 1)
            v3 = sV.bump()
            @on("vector")
            def _(eng, v3=v3):
                eng.wait_ge(sV.h, v3)
                eng.reduce_sum(RT[:, 96:112], selscr.rearrange("p (t e) -> p t e", e=8),
                               AX.X).then_inc(sV.h, 1)
            v4 = sV.bump()
            @on("vector")
            def _(eng, v2=v2, v4=v4):
                eng.wait_ge(sV.h, max(v2, v4))
                eng.tensor_add(RT[:, 112:128], RT[:, 80:96], RT[:, 96:112]).then_inc(sV.h, 1)
            v5 = sV.bump()
            @on("vector")
            def _(eng, v5=v5):
                eng.wait_ge(sV.h, v5)
                eng.tensor_scalar_add(RT[:, 112:128], RT[:, 112:128], -1.0).then_inc(sV.h, 1)
            v_posmy = sV.bump()
            @on("vector")
            def _(eng, v_oh=v_oh, vprev=sV.n):
                eng.wait_ge(sV.h, vprev)
                for tt in range(NTT):
                    last = eng.tensor_mul(selscr[:, 8 * tt:8 * tt + 8],
                                          ohall[:, 8 * tt:8 * tt + 8], P_["colidx8"][:])
                last.then_inc(sV.h, 1)
            v6 = sV.bump()
            @on("vector")
            def _(eng, v6=v6):
                eng.wait_ge(sV.h, v6)
                eng.reduce_sum(RT[:, 128:144], selscr.rearrange("p (t e) -> p t e", e=8),
                               AX.X).then_inc(sV.h, 1)
            v_e = sV.bump()

            @on("gpsimd")
            def _(eng, v_off=v_off):
                eng.wait_ge(sV.h, v_off)
                eng.dma_start(out=FA["cnt_src"], in_=P_["row8"][0:1, 128:136]).then_inc(sCNT.h, 16)
            g1_ = sCNT.bump(16)
            @on("gpsimd")
            def _(eng, g1_=g1_):
                eng.wait_ge(sCNT.h, g1_)
                eng.collective_compute("AllGather", ALU.bypass,
                                       replica_groups=[list(range(NCORE))],
                                       ins=[FA["cnt_src"].opt()], outs=[FA["cnt"].opt()]
                                       ).then_inc(sCC.h, 1)
            ag_cnt = sCC.bump()
            @on("gpsimd")
            def _(eng, ag=ag_cnt):
                eng.wait_ge(sCC.h, ag)
                eng.dma_start(out=P_["cnt_all"][:], in_=FA["cnt"]).then_inc(sCNT.h, 16)
            cnt_ld = sCNT.bump(16)

            SM = P_["sm8"]
            @on("tensor")
            def _(eng, v_off=v_off, vprev=sV.n):
                eng.wait_ge(sV.h, vprev)
                last = eng.matmul(PS[3][0:8, 24:25], P_["row8"][0:1, 128:136],
                                  CB[0:1, 3:4], start=True, stop=True)
                last.then_inc(sT.h, 1)
            t_l1 = sT.bump()
            @on("vector")
            def _(eng, t_l1=t_l1):
                eng.wait_ge(sT.h, t_l1)
                eng.tensor_copy(SM[:, 34:35], PS[3][0:8, 24:25]).then_inc(sV.h, 1)
            v_l1 = sV.bump()
            @on("tensor")
            def _(eng, v_l1=v_l1):
                eng.wait_ge(sV.h, v_l1)
                last = eng.matmul(PS[3][0:1, 26:34], SM[:, 34:35], P_["stri8"][:],
                                  start=True, stop=True)
                last.then_inc(sT.h, 1)
            t_l2 = sT.bump()
            @on("vector")
            def _(eng, t_l2=t_l2):
                eng.wait_ge(sT.h, t_l2)
                eng.tensor_copy(P_["row8"][0:1, 144:152], PS[3][0:1, 26:34]).then_inc(sV.h, 1)
            v_l2 = sV.bump()
            @on("tensor")
            def _(eng, v_l2=v_l2, vprev=sV.n):
                eng.wait_ge(sV.h, v_l2)
                eng.wait_ge(sV.h, vprev)
                last = eng.matmul(PS[2][:, 664:672], P_["ones_r_f"][0:1, :],
                                  P_["row8"][0:1, 144:152], start=True, stop=True)
                last.then_inc(sT.h, 1)
            t_rb = sT.bump()
            @on("vector")
            def _(eng, t_rb=t_rb):
                eng.wait_ge(sT.h, t_rb)
                eng.tensor_copy(RT[:, 144:152], PS[2][:, 664:672]).then_inc(sV.h, 1)
            v_rb = sV.bump()
            @on("vector")
            def _(eng, v_rb=v_rb, vprev=sV.n):
                eng.wait_ge(sV.h, max(v_rb, vprev))
                for tt in range(NTT):
                    last = eng.tensor_mul(selscr[:, 8 * tt:8 * tt + 8],
                                          ohall[:, 8 * tt:8 * tt + 8], RT[:, 144:152])
                last.then_inc(sV.h, 1)
            vr1_ = sV.bump()
            @on("vector")
            def _(eng, vr1_=vr1_):
                eng.wait_ge(sV.h, vr1_)
                eng.reduce_sum(RT[:, 32:48], selscr.rearrange("p (t e) -> p t e", e=8),
                               AX.X).then_inc(sV.h, 1)
            vr2_ = sV.bump()
            @on("vector")
            def _(eng, vr2_=vr2_):
                eng.wait_ge(sV.h, vr2_)
                eng.tensor_add(RT[:, 32:48], RT[:, 32:48], RT[:, 112:128]).then_inc(sV.h, 1)
            v_dl = sV.bump()
            @on("vector")
            def _(eng, v_dl=v_dl):
                eng.wait_ge(sV.h, v_dl)
                eng.tensor_copy(P_["ri"][:, 0:16], RT[:, 32:48]).then_inc(sV.h, 1)
            v_dli = sV.bump()

            tm_done = sV.n
            @on("gpsimd")
            def _(eng, tm=tm_done, v_dli=v_dli):
                eng.wait_ge(sV.h, max(tm, v_dli))
                for tt in range(NTT):
                    eng.indirect_dma_start(
                        out=FA["g_src"],
                        out_offset=bass.IndirectOffsetOnAxis(P_["ri"][:, tt:tt + 1], 0),
                        in_=lnTM[tt], in_offset=None,
                        bounds_check=BR[TOK - 1], oob_is_err=False,
                    ).then_inc(sSCAT.h, 16)
            scat_done = sSCAT.bump(16 * NTT)
            @on("gpsimd")
            def _(eng, sd=scat_done):
                eng.wait_ge(sSCAT.h, sd)
                eng.collective_compute("AllGather", ALU.bypass,
                                       replica_groups=[list(range(NCORE))],
                                       ins=[FA["g_src"].opt()], outs=[FA["g"].opt()]
                                       ).then_inc(sCC.h, 1)
            ag_g = sCC.bump()

            @on("tensor")
            def _(eng, cl=cnt_ld, vprev=sV.n):
                eng.wait_ge(sCNT.h, cl)
                eng.wait_ge(sV.h, vprev)
                last = eng.matmul(PS[3][0:8, 0:8], P_["stri8"][:], P_["cnt_all"][:],
                           start=True, stop=True)
                last = eng.matmul(PS[3][0:8, 8:16], P_["cnt_all"][:], P_["stri8"][:],
                           start=True, stop=True)
                last = eng.matmul(PS[3][0:8, 16:24], P_["cnt_all"][:], P_["tri8i"][:],
                           start=True, stop=True)
                last = eng.transpose(PS[3][0:8, 32:40], P_["cnt_all"][:], P_["id128_f"][0:8, 0:8])
                last.then_inc(sT.h, 1)
            t_p1 = sT.bump()
            @on("vector")
            def _(eng, t_p1=t_p1):
                eng.wait_ge(sT.h, t_p1)
                eng.tensor_copy(SM[:, 0:8], PS[3][0:8, 0:8])
                eng.tensor_copy(SM[:, 8:16], PS[3][0:8, 8:16])
                eng.tensor_copy(SM[:, 16:24], PS[3][0:8, 16:24])
                last = eng.tensor_copy(SM[:, 24:32], PS[3][0:8, 32:40])
                last.then_inc(sV.h, 1)
            v_p1 = sV.bump()
            @on("tensor")
            def _(eng, v_p1=v_p1):
                eng.wait_ge(sV.h, v_p1)
                last = eng.matmul(PS[3][0:1, 0:8], P_["rank1h"][:], SM[:, 0:8],
                           start=True, stop=True)
                last = eng.matmul(PS[3][0:8, 8:9], SM[:, 8:16], P_["rank1h"][:],
                           start=True, stop=True)
                last = eng.matmul(PS[3][0:8, 9:10], SM[:, 16:24], P_["rank1h"][:],
                           start=True, stop=True)
                last = eng.matmul(PS[3][0:8, 16:24], P_["stri8"][:], SM[:, 24:32],
                           start=True, stop=True)
                last.then_inc(sT.h, 1)
            t_p2 = sT.bump()
            @on("vector")
            def _(eng, t_p2=t_p2):
                eng.wait_ge(sT.h, t_p2)
                eng.tensor_copy(P_["row8"][0:1, 136:144], PS[3][0:1, 0:8])
                eng.tensor_copy(SM[:, 32:33], PS[3][0:8, 8:9])
                eng.tensor_copy(SM[:, 33:34], PS[3][0:8, 9:10])
                last = eng.tensor_copy(SM[:, 16:24], PS[3][0:8, 16:24])
                last.then_inc(sV.h, 1)
            v_p2 = sV.bump()
            @on("tensor")
            def _(eng, v_p2=v_p2):
                eng.wait_ge(sV.h, v_p2)
                last = eng.matmul(PS[3][0:8, 25:26], SM[:, 16:24], P_["rank1h"][:],
                           start=True, stop=True)
                last = eng.matmul(PS[3][:, 40:48], P_["ones_r_f"][0:1, :],
                           P_["row8"][0:1, 136:144], start=True, stop=True)
                last.then_inc(sT.h, 1)
            t_p3 = sT.bump()
            @on("vector")
            def _(eng, t_p3=t_p3):
                eng.wait_ge(sT.h, t_p3)
                eng.tensor_copy(SM[:, 35:36], PS[3][0:8, 25:26])
                last = eng.tensor_copy(RT[:, 144:152], PS[3][:, 40:48])
                last.then_inc(sV.h, 1)
            v_p3 = sV.bump()
            @on("vector")
            def _(eng, v_p3=v_p3):
                eng.wait_ge(sV.h, v_p3)
                eng.tensor_add(SM[:, 36:37], P_["rank2048"][:], SM[:, 35:36]).then_inc(sV.h, 1)
            vb1 = sV.bump()
            @on("vector")
            def _(eng, vb1=vb1):
                eng.wait_ge(sV.h, vb1)
                eng.tensor_sub(SM[:, 36:37], SM[:, 36:37], SM[:, 32:33]).then_inc(sV.h, 1)
            vb2 = sV.bump()
            @on("tensor")
            def _(eng, vb2=vb2):
                eng.wait_ge(sV.h, vb2)
                last = eng.matmul(PS[3][0:1, 48:56], SM[:, 32:33], P_["id128_f"][0:8, 0:8],
                           start=True, stop=True)
                last = eng.matmul(PS[3][0:1, 56:64], SM[:, 33:34], P_["id128_f"][0:8, 0:8],
                           start=True, stop=True)
                last = eng.matmul(PS[3][0:1, 64:72], SM[:, 36:37], P_["id128_f"][0:8, 0:8],
                           start=True, stop=True)
                last.then_inc(sT.h, 1)
            t_p4 = sT.bump()
            @on("vector")
            def _(eng, t_p4=t_p4):
                eng.wait_ge(sT.h, t_p4)
                eng.tensor_copy(P_["row8"][0:1, 152:160], PS[3][0:1, 64:72])
                eng.tensor_copy(SR[0:1, 2:10], PS[3][0:1, 48:56])
                last = eng.tensor_copy(SR[0:1, 10:11], PS[3][0:1, 56:57])
                last.then_inc(sV.h, 1)
            v_p4 = sV.bump()
            @on("tensor")
            def _(eng, v_p4=v_p4):
                eng.wait_ge(sV.h, v_p4)
                last = eng.matmul(PS[2][:, 640:648], P_["ones_r_f"][0:1, :], SR[0:1, 2:10],
                           start=True, stop=True)
                last.then_inc(sT.h, 1)
            t_x1 = sT.bump()
            @on("vector")
            def _(eng, t_p4=t_p4, t_x1=t_x1):
                eng.wait_ge(sT.h, t_p4)
                eng.wait_ge(sT.h, t_x1)
                eng.tensor_copy(SR[0:1, 2:10], PS[3][0:1, 56:64]).then_inc(sV.h, 1)
            v_x1 = sV.bump()
            @on("tensor")
            def _(eng, v_x1=v_x1, t_x1=t_x1):
                eng.wait_ge(sV.h, v_x1)
                last = eng.matmul(PS[2][:, 648:656], P_["ones_r_f"][0:1, :], SR[0:1, 2:10],
                           start=True, stop=True)
                last = eng.matmul(PS[2][:, 656:664], P_["ones_r_f"][0:1, :],
                           P_["row8"][0:1, 152:160], start=True, stop=True)
                last.then_inc(sT.h, 1)
            t_x2 = sT.bump()
            @on("vector")
            def _(eng, t_x1=t_x1, t_x2=t_x2):
                eng.wait_ge(sT.h, max(t_x1, t_x2))
                eng.tensor_copy(RT[:, 152:160], PS[2][:, 640:648])
                eng.tensor_copy(RT[:, 160:168], PS[2][:, 648:656])
                last = eng.tensor_copy(RT[:, 168:176], PS[2][:, 656:664])
                last.then_inc(sV.h, 1)
            v_bc = sV.bump()

            @on("vector")
            def _(eng, v_posmy=v_posmy, vprev=sV.n):
                eng.wait_ge(sV.h, vprev)
                for tt in range(NTT):
                    last = eng.tensor_mul(selscr[:, 8 * tt:8 * tt + 8],
                                          ohall[:, 8 * tt:8 * tt + 8], RT[:, 144:152])
                last.then_inc(sV.h, 1)
            vq1 = sV.bump()
            @on("vector")
            def _(eng, vq1=vq1):
                eng.wait_ge(sV.h, vq1)
                eng.reduce_sum(RT[:, 0:16], selscr.rearrange("p (t e) -> p t e", e=8),
                               AX.X).then_inc(sV.h, 1)
            vq2 = sV.bump()
            @on("vector")
            def _(eng, vq2=vq2):
                eng.wait_ge(sV.h, vq2)
                eng.tensor_add(RT[:, 0:16], RT[:, 0:16], RT[:, 112:128]).then_inc(sV.h, 1)
            v_posg = sV.bump()
            @on("vector")
            def _(eng, v_posg=v_posg):
                eng.wait_ge(sV.h, v_posg)
                eng.tensor_single_scalar(RT[:, 16:32], RT[:, 0:16], float(CAP),
                                         ALU.is_lt).then_inc(sV.h, 1)
            v_keep = sV.bump()
            @on("vector")
            def _(eng, v_keep=v_keep, v_g1=v_g1):
                eng.wait_ge(sV.h, v_keep)
                eng.tensor_mul(RT[:, 64:80], RT[:, 64:80], RT[:, 16:32]).then_inc(sV.h, 1)
            v_g1e = sV.bump()
            @on("vector")
            def _(eng, v_e=v_e, v_posg=v_posg):
                eng.wait_ge(sV.h, v_e)
                eng.tensor_scalar(RT[:, 48:64], RT[:, 128:144], float(CAP), None,
                                  ALU.mult).then_inc(sV.h, 1)
            vc1 = sV.bump()
            @on("vector")
            def _(eng, vc1=vc1):
                eng.wait_ge(sV.h, vc1)
                eng.tensor_add(RT[:, 48:64], RT[:, 48:64], RT[:, 0:16]).then_inc(sV.h, 1)
            vc2 = sV.bump()
            @on("vector")
            def _(eng, vc2=vc2):
                eng.wait_ge(sV.h, vc2)
                eng.tensor_mul(RT[:, 48:64], RT[:, 48:64], RT[:, 16:32]).then_inc(sV.h, 1)
            vc3 = sV.bump()
            @on("vector")
            def _(eng, vc3=vc3):
                eng.wait_ge(sV.h, vc3)
                eng.scalar_tensor_tensor(RT[:, 48:64], RT[:, 16:32],
                                         -float(NCORE * CAP), RT[:, 48:64],
                                         ALU.mult, ALU.add).then_inc(sV.h, 1)
            vc4 = sV.bump()
            @on("vector")
            def _(eng, vc4=vc4):
                eng.wait_ge(sV.h, vc4)
                eng.tensor_scalar_add(RT[:, 48:64], RT[:, 48:64],
                                      float(NCORE * CAP)).then_inc(sV.h, 1)
            vc5 = sV.bump()
            @on("vector")
            def _(eng, vc5=vc5):
                eng.wait_ge(sV.h, vc5)
                eng.tensor_copy(P_["ri"][:, 16:32], RT[:, 48:64]).then_inc(sV.h, 1)
            v_ci = sV.bump()
            @on("vector")
            def _(eng, v_bc=v_bc):
                eng.wait_ge(sV.h, v_bc)
                eng.memset(RT[:, 96:116], 0.0)
                last = eng.memset(RT[:, 116:136], 0.0)
                last.then_inc(sV.h, 1)
            vi0 = sV.bump()
            prev = vi0

            def vstep(fn, waits=()):
                nonlocal_prev = vchain_state
                @on("vector")
                def _(eng, fn=fn, prev=nonlocal_prev["prev"], waits=tuple(waits)):
                    eng.wait_ge(sV.h, prev)
                    for s_, v_ in waits:
                        eng.wait_ge(s_.h, v_)
                    fn(eng).then_inc(sV.h, 1)
                vchain_state["prev"] = sV.bump()

            vchain_state = {"prev": vi0}
            for r in range(8):
                vstep(lambda e, r=r: e.tensor_scalar(selscr[:, 0:20], P_["c_iota"][:],
                                                     RT[:, 152 + r:153 + r], None, ALU.is_ge))
                vstep(lambda e, r=r: e.tensor_scalar(selscr[:, 20:40], P_["c_iota"][:],
                                                     RT[:, 160 + r:161 + r], None, ALU.is_lt))
                vstep(lambda e: e.tensor_mul(selscr[:, 0:20], selscr[:, 0:20], selscr[:, 20:40]))
                vstep(lambda e, r=r: e.tensor_scalar(selscr[:, 20:40], selscr[:, 0:20],
                                                     RT[:, 168 + r:169 + r], None, ALU.mult))
                vstep(lambda e: e.tensor_add(RT[:, 96:116], RT[:, 96:116], selscr[:, 20:40]))
                vstep(lambda e: e.tensor_mul(selscr[:, 20:40], selscr[:, 0:20], P_["c_iota"][:]))
                vstep(lambda e: e.tensor_add(RT[:, 96:116], RT[:, 96:116], selscr[:, 20:40]))
                vstep(lambda e: e.tensor_add(RT[:, 116:136], RT[:, 116:136], selscr[:, 0:20]))
            prev = vchain_state["prev"]
            @on("vector")
            def _(eng, prev=prev):
                eng.wait_ge(sV.h, prev)
                eng.scalar_tensor_tensor(RT[:, 96:116], RT[:, 116:136], -float(T),
                                         RT[:, 96:116], ALU.mult, ALU.add).then_inc(sV.h, 1)
            vi1 = sV.bump()
            @on("vector")
            def _(eng, vi1=vi1):
                eng.wait_ge(sV.h, vi1)
                eng.tensor_scalar_add(RT[:, 96:116], RT[:, 96:116], float(T)).then_inc(sV.h, 1)
            vi2 = sV.bump()
            @on("vector")
            def _(eng, vi2=vi2):
                eng.wait_ge(sV.h, vi2)
                eng.tensor_copy(P_["ri"][:, 32:52], RT[:, 96:116]).then_inc(sV.h, 1)
            v_ei = sV.bump()


            einTM = [abf(16384 + 1024 * j, 1024) for j in range(4)]
            einFM = [abf(20480 + 512 * kt, 512) for kt in range(KT)]
            hbuf = [abf(24576 + 512 * d, 512) for d in range(32)]
            w1s = [abf(8192 * i, 8192) for i in range(2)]
            w2s = [abf(41984 + 1024 * i, 1024) for i in range(8)]
            eo_sb = [abf(20480 + 1024 * i, 1024) for i in range(2)]
            if app == 0:
                @on("vector")
                def _(eng):
                    for j in range(4):
                        last = eng.memset(einTM[j], 0.0)
                    last.then_inc(sV.h, 1)
                sV.bump()
            ein_ready = sV.n

            w1_lastmm = {}
            w2_lastmm = {}
            for c5 in range(5):
                @on("gpsimd")
                def _(eng, c5=c5, ag=ag_g, v_ei=v_ei, vprev=sV.n):
                    eng.wait_ge(sCC.h, ag)
                    eng.wait_ge(sV.h, vprev)
                    for j in range(4):
                        eng.indirect_dma_start(
                            out=einTM[j], out_offset=None, in_=FA["g"],
                            in_offset=bass.IndirectOffsetOnAxis(
                                P_["ri"][:, 32 + 4 * c5 + j:33 + 4 * c5 + j], 0),
                            bounds_check=BR[T - 1], oob_is_err=False,
                        ).then_inc(sEIN.h, 16)
                g_ein = sEIN.bump(64)
                tpv = 0
                s_prev_eo = sS.n
                eo_prev = (sEO0.n, sEO1.n)
                for j in range(4):
                    for kt in range(KT):
                        par = (j * KT + kt) % 2
                        @on("tensor")
                        def _(eng, j=j, kt=kt, par=par, g_ein=g_ein, tpv=tpv,
                              spe=s_prev_eo, eop=eo_prev):
                            eng.wait_ge(sEIN.h, g_ein)
                            eng.wait_ge(sS.h, spe)
                            if eop[0]:
                                eng.wait_ge(sEO0.h, eop[0])
                            if eop[1]:
                                eng.wait_ge(sEO1.h, eop[1])
                            if tpv:
                                eng.wait_ge(sV.h, tpv)
                            last = eng.transpose(PSB[par][:, 0:128],
                                          einTM[j][:, 128 * kt:128 * kt + 128],
                                          P_["id128_b"][:])
                            last.then_inc(sT.h, 1)
                        t_tp = sT.bump()
                        @on("vector")
                        def _(eng, j=j, kt=kt, par=par, t_tp=t_tp):
                            eng.wait_ge(sT.h, t_tp)
                            eng.tensor_copy(einFM[kt][:, 128 * j:128 * j + 128],
                                            PSB[par][:, 0:128]).then_inc(sV.h, 1)
                        newv = sV.bump()
                        if par == 1:
                            tpv = newv
                ein_fm_v = sV.n
                hv = []
                for q in range(4):
                    wsem = sW0 if q % 2 == 0 else sW1
                    @on("sync")
                    def _(eng, q=q, c5=c5, wsem=wsem, wc=wcache_done,
                          tprev=max(w1_lastmm.get((c5 * 4 + q) % 2, 0), lnbf_free)):
                        eng.wait_ge(sWC.h, wc)
                        if tprev:
                            eng.wait_ge(sT.h, tprev)
                        eng.dma_start(out=s3d(w1s[q % 2], 1024),
                                      in_=w1bf[:, 1024 * q:1024 * q + 1024]
                                      .rearrange("(kt p) c -> p kt c", p=128)
                                      ).then_inc(wsem.h, 16)
                    wsem.bump(16)
                    wtarget = wsem.n
                    for dd in range(8):
                        dt_ = 8 * q + dd
                        ps_h = PS[2] if dt_ % 2 == 0 else PS[3]
                        @on("tensor")
                        def _(eng, q=q, dd=dd, ps_h=ps_h, wsem=wsem, wt=wtarget,
                              efm=ein_fm_v, sprev=sS.n):
                            eng.wait_ge(wsem.h, wt)
                            eng.wait_ge(sV.h, efm)
                            eng.wait_ge(sS.h, sprev)
                            for kt in range(KT):
                                last = eng.matmul(ps_h[:, 0:512],
                                           w1s[q % 2][:, 1024 * kt + 128 * dd:1024 * kt + 128 * dd + 128],
                                           einFM[kt], start=(kt == 0), stop=(kt == KT - 1))
                            last.then_inc(sT.h, 1)
                        t_h = sT.bump()
                        w1_lastmm[(c5 * 4 + q) % 2] = t_h
                        @on("scalar")
                        def _(eng, dt_=dt_, ps_h=ps_h, t_h=t_h):
                            eng.wait_ge(sT.h, t_h)
                            eng.activation(hbuf[dt_], ps_h[:, 0:512], AF.Relu,
                                           bias=P_["b1_col"][:, dt_:dt_ + 1]).then_inc(sS.h, 1)
                        hv.append(sS.bump())
                h_done = sS.n
                @on("tensor")
                def _(eng, hd=h_done):
                    eng.wait_ge(sS.h, hd)
                    for stt in range(4):
                        last = eng.matmul(PS[stt][:, 0:512], P_["ones_r_b"][0:1, :],
                                   P_["b2row_b"][0:1, 0:512], start=True, stop=False)
                        last = eng.matmul(PS[stt][:, 512:1024], P_["ones_r_b"][0:1, :],
                                   P_["b2row_b"][0:1, 512:1024], start=True, stop=False)
                    last.then_inc(sT.h, 1)
                sT.bump()
                for q2 in range(8):
                    wsem = sW2 if q2 % 2 == 0 else sW3
                    @on("sync")
                    def _(eng, q2=q2, wsem=wsem, wc=wcache_done,
                          tprev=w2_lastmm.get((c5 * 8 + q2) % 2, 0)):
                        eng.wait_ge(sWC.h, wc)
                        if tprev:
                            eng.wait_ge(sT.h, tprev)
                        for i4 in range(4):
                            eng.dma_start(out=w2s[4 * (q2 % 2) + i4],
                                          in_=w2bf[512 * q2 + 128 * i4:512 * q2 + 128 * i4 + 128, :]
                                          ).then_inc(wsem.h, 16)
                    wsem.bump(64)
                    wt2 = wsem.n
                    @on("tensor")
                    def _(eng, q2=q2, wsem=wsem, wt2=wt2):
                        eng.wait_ge(wsem.h, wt2)
                        for i4 in range(4):
                            kt2 = 4 * q2 + i4
                            for stt in range(4):
                                last_k = (q2 == 7 and i4 == 3)
                                last = eng.matmul(PS[stt][:, 0:512],
                                           hbuf[kt2][:, 128 * stt:128 * stt + 128],
                                           w2s[4 * (q2 % 2) + i4][:, 0:512], start=False,
                                           stop=last_k)
                                last = eng.matmul(PS[stt][:, 512:1024],
                                           hbuf[kt2][:, 128 * stt:128 * stt + 128],
                                           w2s[4 * (q2 % 2) + i4][:, 512:1024], start=False,
                                           stop=last_k)
                        last.then_inc(sT.h, 1)
                    w2_lastmm[(c5 * 8 + q2) % 2] = sT.bump()
                t_eo = sT.n
                for stt in range(4):
                    par = stt % 2
                    eosem = sEO0 if par == 0 else sEO1
                    @on("scalar")
                    def _(eng, stt=stt, par=par, t_eo=t_eo, eosem=eosem,
                          gprev=(sEO0 if par == 0 else sEO1).n):
                        eng.wait_ge(sT.h, t_eo)
                        eng.wait_ge(eosem.h, gprev)
                        eng.copy(eo_sb[par], PS[stt][:, 0:1024]).then_inc(sS.h, 1)
                    s_eo = sS.bump()
                    @on("gpsimd")
                    def _(eng, c5=c5, stt=stt, par=par, s_eo=s_eo, eosem=eosem):
                        eng.wait_ge(sS.h, s_eo)
                        eng.dma_start(out=FA["eo_src"][512 * c5 + 128 * stt:
                                                       512 * c5 + 128 * stt + 128, :],
                                      in_=eo_sb[par]).then_inc(eosem.h, 16)
                    eosem.bump(16)
            eo_done0, eo_done1 = sEO0.n, sEO1.n
            @on("gpsimd")
            def _(eng, e0=eo_done0, e1=eo_done1):
                eng.wait_ge(sEO0.h, e0)
                eng.wait_ge(sEO1.h, e1)
                eng.collective_compute("AllGather", ALU.bypass,
                                       replica_groups=[list(range(NCORE))],
                                       ins=[FA["eo_src"].opt()], outs=[FA["eo"].opt()]
                                       ).then_inc(sCC.h, 1)
            ag_eo = sCC.bump()

            combTM = [abf(16384 + 1024 * i, 1024) for i in range(2)]
            if app == 0:
                @on("vector")
                def _(eng, vprev=sV.n):
                    eng.wait_ge(sV.h, vprev)
                    eng.memset(combTM[0], 0.0)
                    last = eng.memset(combTM[1], 0.0)
                    last.then_inc(sV.h, 1)
                sV.bump()
            comb_v = {"last": {0: 0, 1: 0}}
            for tt in range(NTT):
                par = tt % 2
                lastv = comb_v["last"][par]
                csem = sCMB0 if par == 0 else sCMB1
                @on("gpsimd")
                def _(eng, tt=tt, par=par, ag=ag_eo, vpre=(sV.n if tt < 2 else 0),
                      lastv=lastv, csem=csem):
                    eng.wait_ge(sCC.h, ag)
                    if vpre:
                        eng.wait_ge(sV.h, vpre)
                    if lastv:
                        eng.wait_ge(sV.h, lastv)
                    eng.indirect_dma_start(
                        out=combTM[par], out_offset=None, in_=FA["eo"],
                        in_offset=bass.IndirectOffsetOnAxis(
                            P_["ri"][:, 16 + tt:17 + tt], 0),
                        bounds_check=BR[NCORE * CAP - 1], oob_is_err=False,
                    ).then_inc(csem.h, 16)
                g_c = csem.bump(16)
                @on("vector")
                def _(eng, tt=tt, par=par, g_c=g_c, csem=csem):
                    eng.wait_ge(csem.h, g_c)
                    eng.tensor_scalar(combTM[par], combTM[par], RT[:, 64 + tt:65 + tt],
                                      None, ALU.mult).then_inc(sV.h, 1)
                v_sc = sV.bump()
                addv = 0
                for kt in range(KT):
                    tpar = kt % 2
                    @on("tensor")
                    def _(eng, tt=tt, par=par, kt=kt, tpar=tpar, v_sc=v_sc, addv=addv):
                        eng.wait_ge(sV.h, v_sc)
                        if addv:
                            eng.wait_ge(sV.h, addv)
                        last = eng.transpose(PSB[tpar][:, 256:384],
                                      combTM[par][:, 128 * kt:128 * kt + 128],
                                      P_["id128_b"][:])
                        last.then_inc(sT.h, 1)
                    t_tp = sT.bump()
                    @on("vector")
                    def _(eng, tt=tt, kt=kt, tpar=tpar, t_tp=t_tp):
                        eng.wait_ge(sT.h, t_tp)
                        eng.tensor_add(xres[kt][:, 128 * tt:128 * tt + 128],
                                       xres[kt][:, 128 * tt:128 * tt + 128],
                                       PSB[tpar][:, 256:384]).then_inc(sV.h, 1)
                    newv = sV.bump()
                    if tpar == 1:
                        addv = newv
                comb_v["last"][par] = sV.n
            barrier()

        emit_ffn(0)
        if DEV_STAGE == 2:
            emit_output()
            prog.install(block)
            return nc

        ln3bf = [abf(2048 * kt, 2048) for kt in range(KT)]
        emit_layernorm(xres, ln3bf, 16384)
        ln3_v = sV.n
        @on("gpsimd")
        def _(eng, lv=ln3_v):
            eng.wait_ge(sV.h, lv)
            for j in range(KT):
                eng.indirect_dma_start(
                    out=cc_ln3p,
                    out_offset=bass.IndirectOffsetOnAxis(P_["sc1_idx"][:, j:j + 1], 0),
                    in_=ln3bf[j], in_offset=None,
                    bounds_check=BR[TOK - 1], oob_is_err=False,
                ).then_inc(sLN3C.h, 16)
        c3 = sLN3C.bump(16 * KT)
        @on("gpsimd")
        def _(eng, c3=c3):
            eng.wait_ge(sLN3C.h, c3)
            eng.collective_compute("AllGather", ALU.bypass,
                                   replica_groups=PAIR_RG,
                                   ins=[bar_src.opt()], outs=[BARS[2].opt()]
                                   ).then_inc(sCC.h, 1)
        ag_ln3 = sCC.bump()
        barrier()

        Wf32 = [af32(8192 * i, 4096) for i in range(5)]
        ln3tile = abf(40960, 4096)
        hout_bf = abf(45056, 4096)

        NSL = 2
        chainH = {i: None for i in range(NSL)}

        def step2(en, fn):
            w = 4096 // NSL
            for hf in range(NSL):
                sem = {"vector": sV, "scalar": sS}[en]
                sl = slice(w * hf, w * hf + w)
                prev = chainH[hf]
                @on(en)
                def _(eng, fn=fn, prev=prev, sl=sl):
                    if prev is not None:
                        ps, pv = prev
                        eng.wait_ge(ps.h, pv)
                    fn(eng, sl).then_inc(sem.h, 1)
                chainH[hf] = (sem, sem.bump())

        for ct in range(4):
            @on("gpsimd")
            def _(eng, ct=ct, ag=ag_ln3, vprev=sV.n):
                eng.wait_ge(sCC.h, ag)
                eng.wait_ge(sV.h, vprev)
                for r in range(2):
                    eng.indirect_dma_start(
                        out=ln3tile[:, 2048 * r:2048 * r + 2048],
                        out_offset=None, in_=cc_ln3p,
                        in_offset=bass.IndirectOffsetOnAxis(
                            P_["ln3_idx"][:, 2 * ct + r:2 * ct + r + 1], 0),
                        bounds_check=BR[TOK - 1], oob_is_err=False,
                    ).then_inc(sL3G.h, 16)
            g_l3 = sL3G.bump(32)
            v_ct0 = sV.n
            cpv = []
            for gi in range(3):
                for fch in range(8):
                    j = gi * 8 + fch
                    slot = j % 4
                    ps_ap = PS[slot // 2][:, 512 * (slot % 2):512 * (slot % 2) + 512]
                    war = cpv[j - 4] if j >= 4 else v_ct0
                    @on("tensor")
                    def _(eng, gi=gi, ct=ct, fch=fch, ps_ap=ps_ap, g_l3=g_l3, war=war):
                        eng.wait_ge(sL3G.h, g_l3)
                        eng.wait_ge(sV.h, war)
                        last = eng.matmul(ps_ap,
                                   P_["wbd"][:, 128 * (4 * gi + ct):128 * (4 * gi + ct) + 128],
                                   ln3tile[:, 512 * fch:512 * fch + 512],
                                   start=True, stop=True)
                        last.then_inc(sT.h, 1)
                    t_g = sT.bump()
                    @on("vector")
                    def _(eng, gi=gi, fch=fch, ps_ap=ps_ap, t_g=t_g):
                        eng.wait_ge(sT.h, t_g)
                        eng.tensor_copy(Wf32[gi][:, 512 * fch:512 * fch + 512],
                                        ps_ap).then_inc(sV.h, 1)
                    cpv.append(sV.bump())
            for _i in range(NSL):
                chainH[_i] = (sV, sV.n)
            F_, I_, H_, T1, T2 = Wf32
            step2("scalar", lambda e, sl: e.activation(T1[:, sl], F_[:, sl], AF.Sigmoid))
            step2("scalar", lambda e, sl: e.activation(T2[:, sl], I_[:, sl], AF.Sigmoid))
            step2("scalar", lambda e, sl: e.activation(F_[:, sl], H_[:, sl], AF.Sigmoid))
            step2("vector", lambda e, sl: e.tensor_add(I_[:, sl], T1[:, sl], T2[:, sl]))
            step2("scalar", lambda e, sl: e.activation(I_[:, sl], I_[:, sl], AF.Ln))
            step2("scalar", lambda e, sl: e.activation(T1[:, sl], T1[:, sl], AF.Ln))
            step2("scalar", lambda e, sl: e.activation(T2[:, sl], T2[:, sl], AF.Ln))
            step2("scalar", lambda e, sl: e.activation(F_[:, sl], F_[:, sl], AF.Ln))
            step2("vector", lambda e, sl: e.tensor_sub(T1[:, sl], T1[:, sl], I_[:, sl]))
            step2("vector", lambda e, sl: e.tensor_sub(T2[:, sl], T2[:, sl], I_[:, sl]))
            step2("scalar", lambda e, sl: e.activation(T1[:, sl], T1[:, sl], AF.Exp, bias=CB[:, 1:2]))
            step2("scalar", lambda e, sl: e.activation(T2[:, sl], T2[:, sl], AF.Exp))
            step2("scalar", lambda e, sl: e.activation(F_[:, sl], F_[:, sl], AF.Exp, bias=CB[:, 1:2]))
            step2("scalar", lambda e, sl: e.activation(I_[:, sl], H_[:, sl], AF.Relu))
            step2("scalar", lambda e, sl: e.add(I_[:, sl], I_[:, sl], CB[:, 2:3]))
            step2("vector", lambda e, sl: e.tensor_single_scalar(H_[:, sl], H_[:, sl], 0.0, ALU.is_ge))
            step2("vector", lambda e, sl: e.tensor_sub(I_[:, sl], I_[:, sl], F_[:, sl]))
            step2("vector", lambda e, sl: e.tensor_mul(I_[:, sl], H_[:, sl], I_[:, sl]))
            step2("vector", lambda e, sl: e.tensor_add(I_[:, sl], I_[:, sl], F_[:, sl]))
            step2("vector", lambda e, sl: e.tensor_mul(F_[:, sl], T2[:, sl], I_[:, sl]))
            w = 4096 // NSL
            prev_scan = None
            for hf in range(NSL):
                prevc = chainH[hf]
                ini = 0.0 if hf == 0 else H_[:, w * hf - 1:w * hf]
                @on("vector")
                def _(eng, prevc=prevc, prev_scan=prev_scan, hf=hf, ini=ini):
                    ps, pv = prevc
                    eng.wait_ge(ps.h, pv)
                    if prev_scan is not None:
                        eng.wait_ge(prev_scan[0].h, prev_scan[1])
                    eng.tensor_tensor_scan(H_[:, w * hf:w * hf + w],
                                           T1[:, w * hf:w * hf + w],
                                           F_[:, w * hf:w * hf + w],
                                           ini, ALU.mult, ALU.add).then_inc(sV.h, 1)
                chainH[hf] = (sV, sV.bump())
                prev_scan = chainH[hf]
            pends = [chainH[i] for i in range(NSL)]
            @on("vector")
            def _(eng, pends=tuple(pends), hcprev=sHC.n):
                for ps_, pv_ in pends:
                    eng.wait_ge(ps_.h, pv_)
                eng.wait_ge(sHC.h, hcprev)
                eng.tensor_copy(hout_bf[:, :], H_[:, :]).then_inc(sV.h, 1)
            for _i in range(NSL):
                chainH[_i] = (sV, sV.bump()) if _i == 0 else chainH[0]
            hv = sV.n
            @on("gpsimd")
            def _(eng, ct=ct, hv=hv):
                eng.wait_ge(sV.h, hv)
                eng.indirect_dma_start(
                    out=cc_hp,
                    out_offset=bass.IndirectOffsetOnAxis(P_["sch_idx"][:, ct:ct + 1], 0),
                    in_=hout_bf[:, :], in_offset=None,
                    bounds_check=BR[D - 1], oob_is_err=False,
                ).then_inc(sHC.h, 16)
            sHC.bump(16)
        h_contrib = sHC.n
        @on("gpsimd")
        def _(eng, hc=h_contrib):
            eng.wait_ge(sHC.h, hc)
            eng.collective_compute("AllGather", ALU.bypass,
                                   replica_groups=PAIR_RG,
                                   ins=[bar_src.opt()], outs=[BARS[3].opt()]
                                   ).then_inc(sCC.h, 1)
        ag_h = sCC.bump()
        ccH_v = cc_hp.rearrange("r (h t) -> (r h) t", h=2)
        hadd = [abf(2048 * kt, 2048) for kt in range(KT)]
        @on("gpsimd")
        def _(eng, ag=ag_h, vprev=sV.n):
            eng.wait_ge(sCC.h, ag)
            eng.wait_ge(sV.h, vprev)
            for j in range(KT):
                eng.indirect_dma_start(
                    out=hadd[j], out_offset=None, in_=ccH_v,
                    in_offset=bass.IndirectOffsetOnAxis(P_["h_idx"][:, j:j + 1], 0),
                    bounds_check=BR[TOK - 1], oob_is_err=False,
                ).then_inc(sHG.h, 16)
        g_h = sHG.bump(16 * KT)
        @on("vector")
        def _(eng, g_h=g_h):
            eng.wait_ge(sHG.h, g_h)
            for kt in range(KT):
                last = eng.tensor_add(xres[kt], xres[kt], hadd[kt])
            last.then_inc(sV.h, 1)
        sV.bump()
        barrier()

        if DEV_STAGE == 3:
            emit_output()
            prog.install(block)
            return nc

        emit_ffn(1)
        emit_output()
        prog.install(block)
    return nc



_NC_CACHE = {}


def _host_inputs(inp, k):
    s, h2 = k // 2, k % 2
    bs, c2 = k // 2, k % 2
    f32 = np.float32
    d = {}
    x0 = np.asarray(inp["L0_attn_a_x"], f32)
    d["x0_my"] = np.ascontiguousarray(x0[s, h2 * TOK:(h2 + 1) * TOK, :].T)
    wqkv = np.asarray(inp["L0_attn_a_w_qkv"], f32)
    cols = np.r_[512 * h2:512 * h2 + 512,
                 1024 + 512 * h2:1024 + 512 * h2 + 512,
                 2048 + 512 * h2:2048 + 512 * h2 + 512]
    d["wqkv_my"] = np.ascontiguousarray(wqkv[:, cols])
    bq = np.asarray(inp["L0_attn_a_b_qkv"], f32)
    bq_my = bq[cols]
    d["bqkv_col"] = np.ascontiguousarray(bq_my[:1024].reshape(8, 128).T)
    d["bqkv_vrow"] = np.ascontiguousarray(bq_my[1024:1536].reshape(1, 512))
    d["w_out"] = np.ascontiguousarray(np.asarray(inp["L0_attn_a_w_out"], f32))
    d["bout_col"] = np.ascontiguousarray(
        np.asarray(inp["L0_attn_a_b_out"], f32).reshape(8, 128).T)
    d["wg_t"] = np.ascontiguousarray(
        np.asarray(inp["L0_ffn_wg"], f32).reshape(8, 128, 8).transpose(1, 0, 2)
        .reshape(128, 64))
    d["w1_my"] = np.ascontiguousarray(np.asarray(inp["L0_ffn_w1"], f32)[k])
    d["w2_my"] = np.ascontiguousarray(np.asarray(inp["L0_ffn_w2"], f32)[k])
    d["b1_col"] = np.ascontiguousarray(
        np.asarray(inp["L0_ffn_b1"], f32)[k].reshape(32, 128).T)
    d["b2_row"] = np.ascontiguousarray(
        np.asarray(inp["L0_ffn_b2"], f32)[k].reshape(1, D))
    wbd = np.zeros((12, 128, 128), f32)
    gates = [np.asarray(inp["L0_attn_b_Wf"], f32),
             np.asarray(inp["L0_attn_b_Wi"], f32),
             np.asarray(inp["L0_attn_b_Wh"], f32)]
    for gi in range(3):
        for t in range(4):
            for j in range(16):
                g = 64 * c2 + 16 * t + j
                wbd[4 * gi + t, 8 * j:8 * j + 8, 8 * j:8 * j + 8] = gates[gi][g].T
    d["wbd_my"] = np.ascontiguousarray(wbd.reshape(12 * 128, 128))
    d["consts"] = np.broadcast_to(
        np.array([1e-5, 1e-8, 0.50000001, 1.0, 0, 0, 0, 0], f32), (128, 8)).copy()
    ii = np.arange(128)
    d["tri128i_f"] = (ii[:, None] <= ii[None, :]).astype(f32)
    d["tri128i_b"] = d["tri128i_f"].astype(ml_dtypes.bfloat16)
    d["id128_f"] = np.eye(128, dtype=f32)
    d["id128_b"] = d["id128_f"].astype(ml_dtypes.bfloat16)
    r8 = np.arange(8)
    d["stri8"] = (r8[:, None] < r8[None, :]).astype(f32)
    d["tri8i"] = (r8[:, None] <= r8[None, :]).astype(f32)
    d["ones_c_f"] = np.ones((128, 1), f32)
    d["ones_c_b"] = np.ones((128, 1), ml_dtypes.bfloat16)
    d["ones_r_f"] = np.ones((1, 128), f32)
    d["ones_r_b"] = np.ones((1, 128), ml_dtypes.bfloat16)
    d["colidx8"] = np.broadcast_to(r8.astype(f32), (128, 8)).copy()
    oh = np.zeros((8, 1), f32); oh[k] = 1.0
    d["rank1h"] = oh
    d["rank2048"] = (TOK * np.arange(8, dtype=f32)).reshape(8, 1)
    d["c_iota"] = (np.arange(20)[None, :] * 128 + ii[:, None]).astype(f32)
    tt_ = np.arange(128) // 8
    ee_ = np.arange(128) % 8
    d["ones_sq"] = np.ones((128, 128), f32)
    d["tileoff_m"] = ((tt_[:, None] < tt_[None, :]) &
                      (ee_[:, None] == ee_[None, :])).astype(f32)
    d["esum_m"] = (ee_[:, None] == r8[None, :]).astype(f32)
    ch = np.arange(1024)
    ln1i = np.arange(2 * D).astype(np.int32)
    d["ln1_idx"] = np.ascontiguousarray(ln1i.reshape(16, 128).T)
    d["sc1_idx"] = np.ascontiguousarray(
        (D * h2 + np.arange(D)).astype(np.int32).reshape(8, 128).T)
    sco = np.zeros((128, 8), np.int32)
    sco[0:64, :] = 512 * h2 + 64 * np.arange(8)[None, :] + np.arange(64)[:, None]
    d["scO_idx"] = sco
    d["sch_idx"] = np.ascontiguousarray(
        (512 * c2 + np.arange(512)).astype(np.int32).reshape(4, 128).T)
    oi = (2 * ch + h2).astype(np.int32)
    d["o_idx"] = np.ascontiguousarray(oi.reshape(8, 128).T)
    l3 = np.zeros((128, 8), np.int32)
    for ct in range(4):
        for r in range(2):
            l3[:, 2 * ct + r] = 1024 * r + 512 * c2 + 128 * ct + ii
    d["ln3_idx"] = l3
    hi = (2 * ch + h2).astype(np.int32)
    d["h_idx"] = np.ascontiguousarray(hi.reshape(8, 128).T)
    return d


def kernel(**inputs):
    global _NC_CACHE
    key = DEV_STAGE
    if key not in _NC_CACHE:
        _NC_CACHE[key] = build_nc()
    nc = _NC_CACHE[key]
    in_maps = [_host_inputs(inputs, k) for k in range(NCORE)]
    res = run_bass_kernel_spmd(nc, in_maps, core_ids=list(range(NCORE)))
    out = np.empty((B, S, D), np.float32)
    for k in range(NCORE):
        s, h2 = k // 2, k % 2
        out[s, h2 * TOK:(h2 + 1) * TOK, :] = res.results[k]["out_x"].T
    return out


# version 31
# speedup vs baseline: 1.0012x; 1.0012x over previous
import numpy as np
import ml_dtypes
import contextlib
import concourse.bass as bass
import concourse.mybir as mybir
from concourse.bass_utils import run_bass_kernel_spmd

F32 = mybir.dt.float32
BF16 = mybir.dt.bfloat16
I32 = mybir.dt.int32
AF = mybir.ActivationFunctionType
ALU = mybir.AluOpType
AX = mybir.AxisListType

B, S, D = 4, 4096, 1024
T = B * S
TOK = 2048
E, DFF = 8, 4096
CAP = 2560
NCORE = 8
KT = 8
NTT = 16

DEV_STAGE = 4

ENGS = ("sync", "tensor", "vector", "scalar", "gpsimd")


class Prog:
    def __init__(self):
        self.ops = {e: [] for e in ENGS}

    def install(self, block):
        for name in ENGS:
            def body(eng, _name=name):
                for f in self.ops[_name]:
                    f(eng)
            getattr(block, name)(body)


class Sem:
    def __init__(self, handle):
        self.h = handle
        self.n = 0

    def bump(self, k=1):
        self.n += k
        return self.n


def build_nc():
    nc = bass.Bass("TRN2", target_bir_lowering=False, debug=False)

    def ein(name, shape, dt=F32):
        return nc.dram_tensor(name, shape, dt, kind="ExternalInput").ap()

    x0_my = ein("x0_my", [D, TOK])
    wqkv_my = ein("wqkv_my", [D, 1536])
    w_out_in = ein("w_out", [D, D])
    w1_my = ein("w1_my", [D, DFF])
    w2_my = ein("w2_my", [DFF, D])
    wbd_my = ein("wbd_my", [12 * 128, 128])
    SMALL = {
        "bqkv_col": ([128, 8], F32), "bqkv_vrow": ([1, 512], F32),
        "bout_col": ([128, 8], F32), "wg_t": ([128, 64], F32),
        "b1_col": ([128, 32], F32), "b2_row": ([1, D], F32),
        "consts": ([128, 8], F32), "tri128i_f": ([128, 128], F32),
        "tri128i_b": ([128, 128], BF16), "id128_b": ([128, 128], BF16),
        "id128_f": ([128, 128], F32), "stri8": ([8, 8], F32),
        "tri8i": ([8, 8], F32), "ones_c_f": ([128, 1], F32),
        "ones_c_b": ([128, 1], BF16), "ones_r_f": ([1, 128], F32),
        "ones_r_b": ([1, 128], BF16), "colidx8": ([128, 8], F32),
        "rank1h": ([8, 1], F32), "rank2048": ([8, 1], F32),
        "c_iota": ([128, 20], F32), "tileoff_m": ([128, 128], F32),
        "ones_sq": ([128, 128], F32),
        "esum_m": ([128, 8], F32),
        "ln1_idx": ([128, 16], I32), "o_idx": ([128, 8], I32),
        "ln3_idx": ([128, 8], I32), "h_idx": ([128, 8], I32),
        "sc1_idx": ([128, 8], I32), "scO_idx": ([128, 8], I32),
        "sch_idx": ([128, 4], I32),
    }
    ext_small = {nm: ein(nm, shp, dt) for nm, (shp, dt) in SMALL.items()}

    out_x = nc.dram_tensor("out_x", [D, TOK], F32, kind="ExternalOutput").ap()

    def dr(name, shape, dt=BF16, shared=False):
        if shared:
            return nc.dram_tensor(name, shape, dt, addr_space="Shared").ap()
        return nc.dram_tensor(name, shape, dt).ap()

    cc_ln1p = dr("cc_ln1p", [2 * D, TOK], shared=True)
    bar_src = nc.dram_tensor("bar_src", [1, 8], F32).ap()
    BARS = [nc.dram_tensor(f"bar{i}", [2, 8], F32).ap() for i in range(6)]
    PAIR_RG = [[0, 1], [2, 3], [4, 5], [6, 7]]
    cc_Op = dr("cc_Op", [1024, S], shared=True)
    cc_ln3p = dr("cc_ln3p", [2 * D, TOK], shared=True)
    cc_hp = dr("cc_hp", [1024, S], shared=True)
    bar_full = nc.dram_tensor("bar_full", [8, 8], F32, addr_space="Shared").ap()
    w1bf = dr("w1bf", [D, DFF])
    w2bf = dr("w2bf", [DFF, D])
    FBD = []
    for a in range(2):
        FBD.append(dict(
            cnt_src=dr(f"cc_cnt_src{a}", [1, 8], F32),
            cnt=dr(f"cc_cnt{a}", [8, 8], F32, shared=True),
            g_src=dr(f"cc_g_src{a}", [TOK, D]),
            g=dr(f"cc_g{a}", [T, D], shared=True),
            eo_src=dr(f"cc_eo_src{a}", [CAP, D]),
            eo=dr(f"cc_eo{a}", [NCORE * CAP, D], shared=True),
        ))

    prog = Prog()
    st = contextlib.ExitStack()
    with st:
        AR_COLS = 82944
        AR = st.enter_context(nc.sbuf_tensor("AR", [128, AR_COLS], BF16))

        def abf(c0, n):
            assert 0 <= c0 and c0 + n <= AR_COLS, (c0, n)
            return AR[:, c0:c0 + n]

        def af32(c0, n):
            assert c0 % 2 == 0 and c0 + 2 * n <= AR_COLS, (c0, n)
            return AR[:, c0:c0 + 2 * n].bitcast(F32)

        def t3d(ap2d):
            return ap2d.rearrange("(kt p) t -> p kt t", p=128)

        def s3d(apslice, inner):
            return apslice.rearrange("p (kt t) -> p kt t", t=inner)

        P_ = {nm: st.enter_context(nc.sbuf_tensor("sb_" + nm, shp, dt))
              for nm, (shp, dt) in SMALL.items()}
        for nm, shp, dt in [
            ("bvrow_b", [1, 512], BF16), ("b2row_b", [1, D], BF16),
            ("stats", [128, 16], F32), ("lnscr", [128, 1536], F32),
            ("mrow", [1, TOK], F32), ("rrow", [1, TOK], F32),
            ("route", [128, 176], F32), ("cnt_all", [8, 8], F32),
            ("sm8", [8, 48], F32), ("row8", [1, 160], F32),
            ("wbd", [128, 12 * 128], BF16), ("ri", [128, 64], I32),
        ]:
            P_[nm] = st.enter_context(nc.sbuf_tensor(nm, shp, dt))

        PS = [st.enter_context(nc.psum_tensor(f"ps{i}", [128, 1024], F32)) for i in range(4)]
        PSB = [p.ap().bitcast(BF16) for p in PS]

        semnames = ["sQ", "sG", "sW0", "sW1", "sCC", "sT", "sV", "sS", "sG2",
                    "sCONST", "sXA", "sXB", "sWC", "sLN1C", "sLN1G", "sOC",
                    "sP3", "sCNT", "sSCAT", "sEIN", "sEO0", "sEO1", "sCMB0",
                    "sCMB1", "sLN3C", "sL3G", "sHC", "sHG", "sOUT", "sOC1", "sOC2", "sOC3", "sW2", "sW3", "sP"]
        sems = {nm: Sem(st.enter_context(nc.semaphore(nm))) for nm in semnames}
        (sQ, sG, sW0, sW1, sCC, sT, sV, sS, sG2, sCONST, sXA, sXB, sWC, sLN1C,
         sLN1G, sOC, sP3, sCNT, sSCAT, sEIN, sEO0, sEO1, sCMB0, sCMB1, sLN3C,
         sL3G, sHC, sHG, sOUT, sOC1, sOC2, sOC3, sW2, sW3, sP) = (sems[n] for n in semnames)

        block = st.enter_context(nc.Block())

        def on(eng_name):
            def deco(fn):
                prog.ops[eng_name].append(fn)
            return deco

        def barrier():
            vals = [(s, s.n) for s in sems.values()]
            for en in ENGS:
                @on(en)
                def _(eng, vals=tuple(vals)):
                    for s, v in vals:
                        if v:
                            eng.wait_ge(s.h, v)

        CB = P_["consts"]
        RT = P_["route"]
        SR = P_["stats"]

        @on("sync")
        def _(eng):
            for nm in SMALL:
                eng.dma_start(out=P_[nm][:], in_=ext_small[nm]).then_inc(sCONST.h, 16)
            eng.wait_ge(sCONST.h, 16 * len(SMALL))
            eng.dma_start(out=bar_src, in_=P_["consts"][0:1, 0:8]).then_inc(sCONST.h, 16)
        consts_done = sCONST.bump(16 * len(SMALL))
        barsrc_done = sCONST.bump(16)

        @on("vector")
        def _(eng):
            eng.wait_ge(sCONST.h, consts_done)
            eng.tensor_copy(P_["bvrow_b"][:], P_["bqkv_vrow"][:])
            last = eng.tensor_copy(P_["b2row_b"][:], P_["b2_row"][:])
            last.then_inc(sV.h, 1)
        rowcast_done = sV.bump()

        BR = {}
        @on("gpsimd")
        def _(eng):
            for v in (TOK - 1, D - 1, NCORE * D - 1, T - 1, NCORE * CAP - 1):
                BR[v] = eng.to_reg(v)
            eng.dma_start(out=w1bf, in_=w1_my).then_inc(sWC.h, 16)
            eng.dma_start(out=w2bf, in_=w2_my).then_inc(sWC.h, 16)
            eng.dma_start(out=s3d(P_["wbd"][:], 128),
                          in_=wbd_my.rearrange("(t p) c -> p t c", p=128)).then_inc(sWC.h, 16)
        wcache_done = sWC.bump(48)
        @on("gpsimd")
        def _(eng, bd=barsrc_done):
            eng.wait_ge(sCONST.h, bd)
            eng.collective_compute("AllGather", ALU.bypass,
                                   replica_groups=[list(range(NCORE))],
                                   ins=[bar_src.opt()], outs=[bar_full.opt()]
                                   ).then_inc(sCC.h, 1)
        sync0 = sCC.bump()

        XRES0 = AR_COLS - 32768
        xres = [af32(XRES0 + 4096 * kt, TOK) for kt in range(KT)]

        def emit_layernorm(xsrc, lnout, scr0, need_rows=False):
            xbf = [abf(scr0 + 512 * kt, 512) for kt in range(KT)]
            xsq = [abf(scr0 + 4096 + 512 * kt, 512) for kt in range(KT)]
            sx_slot = [PS[2][0:1, 0:512], PS[2][0:1, 512:1024],
                       PS[3][0:1, 0:512], PS[3][0:1, 512:1024]]
            sq_slot = [PS[0][0:1, 0:512], PS[0][0:1, 512:1024],
                       PS[1][0:1, 0:512], PS[1][0:1, 512:1024]]
            MR, RR = P_["mrow"], P_["rrow"]
            mm_done = {}
            cp_done = {}
            sq_done = {}
            for c in range(4):
                c0 = 512 * c
                wmm = mm_done.get(c - 1, 0)
                @on("vector")
                def _(eng, c0=c0, wmm=wmm):
                    if wmm:
                        eng.wait_ge(sT.h, wmm)
                    for kt in range(KT):
                        last = eng.tensor_copy(xbf[kt], xsrc[kt][:, c0:c0 + 512])
                    last.then_inc(sV.h, 1)
                cp_done[c] = sV.bump()
                @on("scalar")
                def _(eng, c0=c0, wmm=wmm):
                    if wmm:
                        eng.wait_ge(sT.h, wmm)
                    for kt in range(KT):
                        last = eng.activation(xsq[kt], xsrc[kt][:, c0:c0 + 512], AF.Square)
                    last.then_inc(sS.h, 1)
                sq_done[c] = sS.bump()
                @on("tensor")
                def _(eng, c=c, vdone=cp_done[c], sdone=sq_done[c]):
                    eng.wait_ge(sV.h, vdone)
                    eng.wait_ge(sS.h, sdone)
                    for kt in range(KT):
                        eng.matmul(sx_slot[c], P_["ones_c_b"][:], xbf[kt],
                                   start=(kt == 0), stop=(kt == KT - 1))
                    for kt in range(KT):
                        last = eng.matmul(sq_slot[c], P_["ones_c_b"][:], xsq[kt],
                                   start=(kt == 0), stop=(kt == KT - 1))
                    last.then_inc(sT.h, 1)
                mm_done[c] = sT.bump()
            all_mm = mm_done[3]
            @on("vector")
            def _(eng, am=all_mm):
                eng.wait_ge(sT.h, am)
                eng.tensor_scalar_mul(MR[0:1, 0:1024], PS[2][0:1, 0:1024], 1.0 / 1024.0)
                last = eng.tensor_scalar_mul(MR[0:1, 1024:2048], PS[3][0:1, 0:1024],
                                             1.0 / 1024.0)
                last.then_inc(sV.h, 1)
            v_m = sV.bump()
            @on("vector")
            def _(eng, v_m=v_m):
                eng.wait_ge(sV.h, v_m)
                eng.tensor_mul(RR[0:1, 0:2048], MR[0:1, 0:2048], MR[0:1, 0:2048]).then_inc(sV.h, 1)
            v_m2 = sV.bump()
            @on("vector")
            def _(eng, v_m2=v_m2):
                eng.wait_ge(sV.h, v_m2)
                eng.scalar_tensor_tensor(RR[0:1, 0:1024], PS[0][0:1, 0:1024],
                                         1.0 / 1024.0, RR[0:1, 0:1024],
                                         ALU.mult, ALU.subtract)
                last = eng.scalar_tensor_tensor(RR[0:1, 1024:2048], PS[1][0:1, 0:1024],
                                         1.0 / 1024.0, RR[0:1, 1024:2048],
                                         ALU.mult, ALU.subtract)
                last.then_inc(sV.h, 1)
            v_var = sV.bump()
            @on("scalar")
            def _(eng, v_var=v_var):
                eng.wait_ge(sV.h, v_var)
                eng.activation(RR[0:1, 0:2048], RR[0:1, 0:2048], AF.Ln,
                               bias=CB[0:1, 0:1]).then_inc(sS.h, 1)
            s_ln = sS.bump()
            @on("scalar")
            def _(eng, s_ln=s_ln):
                eng.wait_ge(sS.h, s_ln)
                eng.activation(RR[0:1, 0:2048], RR[0:1, 0:2048], AF.Exp,
                               scale=-0.5).then_inc(sS.h, 1)
            s_r = sS.bump()
            norm_done = {}
            for c in range(4):
                c0 = 512 * c
                bps = PS[2] if c % 2 == 0 else PS[3]
                wn = norm_done.get(c - 2, 0)
                @on("tensor")
                def _(eng, c0=c0, bps=bps, s_r=s_r, wn=wn):
                    eng.wait_ge(sS.h, s_r)
                    if wn:
                        eng.wait_ge(sV.h, wn)
                    eng.matmul(bps[:, 0:512], P_["ones_r_f"][0:1, :],
                               MR[0:1, c0:c0 + 512], start=True, stop=True)
                    last = eng.matmul(bps[:, 512:1024], P_["ones_r_f"][0:1, :],
                               RR[0:1, c0:c0 + 512], start=True, stop=True)
                    last.then_inc(sT.h, 1)
                t_bc = sT.bump()
                for kt in range(KT):
                    @on("vector")
                    def _(eng, kt=kt, c0=c0, bps=bps, t_bc=t_bc, vprev=sV.n):
                        if kt == 0:
                            eng.wait_ge(sT.h, t_bc)
                        eng.wait_ge(sV.h, vprev)
                        eng.tensor_sub(P_["lnscr"][:, 0:512], xsrc[kt][:, c0:c0 + 512],
                                       bps[:, 0:512]).then_inc(sV.h, 1)
                    va = sV.bump()
                    @on("vector")
                    def _(eng, kt=kt, c0=c0, bps=bps, va=va):
                        eng.wait_ge(sV.h, va)
                        eng.tensor_mul(lnout[kt][:, c0:c0 + 512], P_["lnscr"][:, 0:512],
                                       bps[:, 512:1024]).then_inc(sV.h, 1)
                    sV.bump()
                norm_done[c] = sV.n
            vX = sV.n
            @on("tensor")
            def _(eng, vX=vX):
                eng.wait_ge(sV.h, vX)

        @on("sync")
        def _(eng):
            eng.dma_start(out=AR[:, XRES0:XRES0 + 32768].bitcast(F32)
                          .rearrange("p (kt t) -> p kt t", t=2048),
                          in_=x0_my.rearrange("(kt p) t -> p kt t", p=128)
                          ).then_inc(sXA.h, 16)
        x0_done = sXA.bump(16)

        ln1out = [abf(TOK * kt, TOK) for kt in range(KT)]
        for en in ("vector", "scalar", "tensor"):
            @on(en)
            def _(eng):
                eng.wait_ge(sXA.h, x0_done)
                eng.wait_ge(sCONST.h, consts_done)
        emit_layernorm(xres, ln1out, 16384)
        ln1_v = sV.n

        @on("gpsimd")
        def _(eng, lv=ln1_v):
            eng.wait_ge(sV.h, lv)
            for j in range(KT):
                eng.indirect_dma_start(
                    out=cc_ln1p,
                    out_offset=bass.IndirectOffsetOnAxis(P_["sc1_idx"][:, j:j + 1], 0),
                    in_=ln1out[j], in_offset=None,
                    bounds_check=BR[TOK - 1], oob_is_err=False,
                ).then_inc(sLN1C.h, 16)
        c1 = sLN1C.bump(16 * KT)
        @on("gpsimd")
        def _(eng, c1=c1, bd=barsrc_done):
            eng.wait_ge(sLN1C.h, c1)
            eng.wait_ge(sCONST.h, bd)
            eng.collective_compute("AllGather", ALU.bypass,
                                   replica_groups=PAIR_RG,
                                   ins=[bar_src.opt()], outs=[BARS[0].opt()]
                                   ).then_inc(sCC.h, 1)
        ag_ln1 = sCC.bump()

        ln1h = [abf(2048 * kt, 2048) for kt in range(KT)]
        wqkv_sb = abf(16384, 12288)
        qk_t = [abf(28672 + 4096 * i, 4096) for i in range(8)]
        v_sb = [abf(61440 + 520 * tt, 520) for tt in range(32)]

        @on("gpsimd")
        def _(eng):
            eng.wait_ge(sCONST.h, consts_done)
            eng.dma_start(out=s3d(wqkv_sb, 1536),
                          in_=wqkv_my.rearrange("(kt p) c -> p kt c", p=128)
                          ).then_inc(sG2.h, 16)
        wqkv_done = sG2.bump(16)

        @on("vector")
        def _(eng):
            for tt in range(32):
                vv = v_sb[tt].rearrange("p (h c) -> p h c", c=65)
                last = eng.memset(vv[:, :, 64:65], 1.0)
            last.then_inc(sV.h, 1)
        vones_done = sV.bump()

        last_evac = {0: 0, 1: 0, 2: 0}

        def emit_qkv_half(half, war_v):
            h0 = half * 2048
            @on("gpsimd")
            def _(eng, half=half, war_v=war_v, ag=ag_ln1):
                eng.wait_ge(sCC.h, ag)
                if war_v is not None:
                    eng.wait_ge(sV.h, war_v)
                for j in range(8):
                    eng.indirect_dma_start(
                        out=ln1h[j], out_offset=None, in_=cc_ln1p,
                        in_offset=bass.IndirectOffsetOnAxis(
                            P_["ln1_idx"][:, 8 * half + j:8 * half + j + 1], 0),
                        bounds_check=BR[TOK - 1], oob_is_err=False,
                    ).then_inc(sLN1G.h, 16)
            ld = sLN1G.bump(16 * 8)
            for ct in range(8):
                for fc in range(4):
                    pidx = (ct * 4 + fc) % 2
                    ps = PS[pidx]
                    lev = last_evac[pidx]
                    @on("tensor")
                    def _(eng, ct=ct, fc=fc, ps=ps, ld=ld, lev=lev, wq=wqkv_done):
                        eng.wait_ge(sLN1G.h, ld)
                        eng.wait_ge(sG2.h, wq)
                        if lev:
                            eng.wait_ge(sV.h, lev)
                        for kt in range(KT):
                            last = eng.matmul(ps[:, 0:512],
                                       wqkv_sb[:, 1536 * kt + 128 * ct:1536 * kt + 128 * ct + 128],
                                       ln1h[kt][:, 512 * fc:512 * fc + 512],
                                       start=(kt == 0), stop=(kt == KT - 1))
                        last.then_inc(sT.h, 1)
                    td = sT.bump()
                    @on("vector")
                    def _(eng, ct=ct, fc=fc, ps=ps, td=td, h0=h0):
                        eng.wait_ge(sT.h, td)
                        eng.tensor_scalar(qk_t[ct][:, h0 + 512 * fc:h0 + 512 * fc + 512],
                                          ps[:, 0:512], P_["bqkv_col"][:, ct:ct + 1],
                                          None, ALU.add).then_inc(sV.h, 1)
                    last_evac[pidx] = sV.bump()
            for tt_loc in range(16):
                tt = half * 16 + tt_loc
                lev = last_evac[2]
                @on("tensor")
                def _(eng, tt_loc=tt_loc, ld=ld, lev=lev, von=vones_done,
                      rc=rowcast_done):
                    eng.wait_ge(sV.h, max(von, rc))
                    eng.wait_ge(sLN1G.h, ld)
                    if lev:
                        eng.wait_ge(sV.h, lev)
                    last = eng.matmul(PS[2][:, 0:512], P_["ones_r_b"][0:1, :],
                               P_["bvrow_b"][0:1, :], start=True, stop=False)
                    for kt in range(KT):
                        last = eng.matmul(PS[2][:, 0:512],
                                   ln1h[kt][:, 128 * tt_loc:128 * tt_loc + 128],
                                   wqkv_sb[:, 1536 * kt + 1024:1536 * kt + 1536],
                                   start=False, stop=(kt == KT - 1))
                    last.then_inc(sT.h, 1)
                td = sT.bump()
                @on("vector")
                def _(eng, tt=tt, td=td):
                    eng.wait_ge(sT.h, td)
                    vv = v_sb[tt].rearrange("p (h c) -> p h c", c=65)
                    for hu in range(8):
                        last = eng.tensor_copy(vv[:, hu, 0:64],
                                               PS[2][:, 64 * hu:64 * hu + 64])
                    last.then_inc(sV.h, 1)
                last_evac[2] = sV.bump()
            return sV.n

        v_half0 = emit_qkv_half(0, None)
        emit_qkv_half(1, v_half0)
        barrier()

        p_sb = [abf(16384 + 1024 * i, 1024) for i in range(3)]
        o_sb = af32(20480, 1024)
        rrec = af32(24576, 1024)
        oacc = [[abf(8192 * p + 4096 * i, 4096) for i in range(2)]
                for p in range(2)]
        odma_done = {0: 0, 1: 0}

        exp_par = {0: 0, 1: 0, 2: 0}
        pv_par = {0: 0, 1: 0, 2: 0}
        PSS = [PS[0], PS[1], PS[3]]
        v_oc_last = 0

        def emit_st(hp, g, kk, qtile, ktile):
            par = kk % 3
            ps_s = PSS[par]
            q0 = 512 * g
            w_exp = exp_par[par]
            @on("tensor")
            def _(eng, kk=kk, ps_s=ps_s, qtile=qtile, ktile=ktile, q0=q0, w_exp=w_exp):
                if w_exp:
                    eng.wait_ge(sS.h, w_exp)
                eng.matmul(ps_s[:, 0:512],
                           ktile[0:64, 128 * kk:128 * kk + 128],
                           qtile[0:64, q0:q0 + 512], start=True, stop=True)
                last = eng.matmul(ps_s[:, 512:1024],
                           ktile[64:128, 128 * kk:128 * kk + 128],
                           qtile[64:128, q0:q0 + 512], start=True, stop=True)
                last.then_inc(sT.h, 1)
            return sT.bump()

        def emit_exp_mask(hp, g, kk, t_st):
            par = kk % 3
            ps_s = PSS[par]
            w_pv = pv_par[par]
            @on("scalar")
            def _(eng, par=par, ps_s=ps_s, t_st=t_st, w_pv=w_pv):
                eng.wait_ge(sT.h, t_st)
                if w_pv:
                    eng.wait_ge(sT.h, w_pv)
                eng.activation(p_sb[par], ps_s[:, 0:1024], AF.Exp,
                               scale=0.125).then_inc(sS.h, 1)
            exp_par[par] = s_exp = sS.bump()
            v_mask = None
            if kk >= 4 * g:
                j = kk - 4 * g
                @on("vector")
                def _(eng, par=par, j=j, s_exp=s_exp):
                    eng.wait_ge(sS.h, s_exp)
                    if j > 0:
                        eng.memset(p_sb[par][:, 0:128 * j], 0.0)
                        last = eng.memset(p_sb[par][:, 512:512 + 128 * j], 0.0)
                        last.then_inc(sV.h, 1)
                    else:
                        eng.engine_nop().then_inc(sV.h, 1)
                vm1 = sV.bump()
                @on("vector")
                def _(eng, par=par, j=j, vm1=vm1):
                    eng.wait_ge(sV.h, vm1)
                    eng.tensor_mul(p_sb[par][:, 128 * j:128 * j + 128],
                                   p_sb[par][:, 128 * j:128 * j + 128],
                                   P_["tri128i_b"][:])
                    last = eng.tensor_mul(p_sb[par][:, 512 + 128 * j:512 + 128 * j + 128],
                                   p_sb[par][:, 512 + 128 * j:512 + 128 * j + 128],
                                   P_["tri128i_b"][:])
                    last.then_inc(sV.h, 1)
                v_mask = sV.bump()
            return s_exp, v_mask

        def emit_pv(hp, g, kk, nk, s_exp, v_mask, first_wait=0):
            par = kk % 3
            @on("tensor")
            def _(eng, hp=hp, kk=kk, nk=nk, par=par, s_exp=s_exp, v_mask=v_mask,
                  fw=first_wait):
                eng.wait_ge(sS.h, s_exp)
                if v_mask:
                    eng.wait_ge(sV.h, v_mask)
                if fw:
                    eng.wait_ge(sV.h, fw)
                vv = v_sb[kk].rearrange("p (h c) -> p h c", c=65)
                eng.matmul(PS[2][0:65, 0:512], vv[:, 2 * hp, :], p_sb[par][:, 0:512],
                           start=(kk == 0), stop=(kk == nk - 1))
                last = eng.matmul(PS[2][0:65, 512:1024], vv[:, 2 * hp + 1, :],
                           p_sb[par][:, 512:1024],
                           start=(kk == 0), stop=(kk == nk - 1))
                last.then_inc(sT.h, 1)
            pv_par[par] = v = sT.bump()
            return v

        for hp in range(4):
            qtile = qk_t[hp]
            ktile = qk_t[4 + hp]
            for g in range(8):
                nk = 4 * g + 4
                steps = {}
                for kk in range(min(3, nk)):
                    t_st = emit_st(hp, g, kk, qtile, ktile)
                    steps[kk] = emit_exp_mask(hp, g, kk, t_st)
                t_pv = None
                pend_st = {}
                for kk in range(nk):
                    if kk + 3 < nk:
                        pend_st[kk + 3] = emit_st(hp, g, kk + 3, qtile, ktile)
                    s_exp, v_mask = steps.pop(kk)
                    t_pv = emit_pv(hp, g, kk, nk, s_exp, v_mask,
                                   first_wait=(v_oc_last if kk == 0 else 0))
                    if kk + 3 < nk:
                        steps[kk + 3] = emit_exp_mask(hp, g, kk + 3, pend_st.pop(kk + 3))
                @on("vector")
                def _(eng, t_pv=t_pv):
                    eng.wait_ge(sT.h, t_pv)
                    eng.reciprocal(rrec[64:65, 0:1024], PS[2][64:65, 0:1024]).then_inc(sV.h, 1)
                vr1 = sV.bump()
                @on("scalar")
                def _(eng, t_pv=t_pv):
                    eng.wait_ge(sT.h, t_pv)
                    eng.copy(o_sb[0:64, 0:1024], PS[2][0:64, 0:1024]).then_inc(sS.h, 1)
                s_oc = sS.bump()
                @on("tensor")
                def _(eng, vr1=vr1, s_oc=s_oc):
                    eng.wait_ge(sV.h, vr1)
                    eng.wait_ge(sS.h, s_oc)
                    eng.matmul(PS[2][0:64, 0:512], P_["ones_sq"][64:65, 0:64],
                               rrec[64:65, 0:512], start=True, stop=True)
                    last = eng.matmul(PS[2][0:64, 512:1024], P_["ones_sq"][64:65, 0:64],
                               rrec[64:65, 512:1024], start=True, stop=True)
                    last.then_inc(sT.h, 1)
                t_bc = sT.bump()
                par2 = hp % 2
                osem = sOC if par2 == 0 else sOC1
                dmaprev = odma_done[par2]
                @on("vector")
                def _(eng, t_bc=t_bc, par2=par2, g=g, dmaprev=dmaprev, osem=osem):
                    eng.wait_ge(sT.h, t_bc)
                    if g == 0 and dmaprev:
                        eng.wait_ge(osem.h, dmaprev)
                    eng.tensor_mul(oacc[par2][0][0:64, 512 * g:512 * g + 512],
                                   o_sb[0:64, 0:512],
                                   PS[2][0:64, 0:512])
                    eng.tensor_mul(oacc[par2][1][0:64, 512 * g:512 * g + 512],
                                   o_sb[0:64, 512:1024],
                                   PS[2][0:64, 512:1024]).then_inc(sV.h, 1)
                v_oc = sV.bump()
                if g == 7:
                    @on("gpsimd")
                    def _(eng, hp=hp, par2=par2, v_oc=v_oc, osem=osem):
                        eng.wait_ge(sV.h, v_oc)
                        he, ho = 2 * hp, 2 * hp + 1
                        eng.indirect_dma_start(
                            out=cc_Op,
                            out_offset=bass.IndirectOffsetOnAxis(
                                P_["scO_idx"][0:64, he:he + 1], 0),
                            in_=oacc[par2][0][0:64, :], in_offset=None,
                            bounds_check=BR[D - 1], oob_is_err=False,
                        ).then_inc(osem.h, 16)
                        eng.indirect_dma_start(
                            out=cc_Op,
                            out_offset=bass.IndirectOffsetOnAxis(
                                P_["scO_idx"][0:64, ho:ho + 1], 0),
                            in_=oacc[par2][1][0:64, :], in_offset=None,
                            bounds_check=BR[D - 1], oob_is_err=False,
                        ).then_inc(osem.h, 16)
                    odma_done[par2] = osem.bump(32)
                v_oc_last = v_oc
        oc_dones = [(sm, sm.n) for sm in (sOC, sOC1)]

        @on("gpsimd")
        def _(eng, ods=tuple(oc_dones)):
            for sm, v in ods:
                eng.wait_ge(sm.h, v)
            eng.collective_compute("AllGather", ALU.bypass,
                                   replica_groups=PAIR_RG,
                                   ins=[bar_src.opt()], outs=[BARS[1].opt()]
                                   ).then_inc(sCC.h, 1)
        ag_O = sCC.bump()
        barrier()

        attn_fm = [abf(2048 * kt, 2048) for kt in range(KT)]
        wout_sb = abf(16384, 8192)

        @on("sync")
        def _(eng):
            eng.dma_start(out=AR[:, XRES0:XRES0 + 32768].bitcast(F32)
                          .rearrange("p (kt t) -> p kt t", t=2048),
                          in_=x0_my.rearrange("(kt p) t -> p kt t", p=128)
                          ).then_inc(sXB.h, 16)
        x0_re = sXB.bump(16)
        @on("gpsimd")
        def _(eng):
            eng.dma_start(out=s3d(wout_sb, 1024),
                          in_=w_out_in.rearrange("(kt p) c -> p kt c", p=128)
                          ).then_inc(sP3.h, 16)
        wout_done = sP3.bump(16)
        ccO_v = cc_Op.rearrange("r (h t) -> (r h) t", h=2)
        @on("gpsimd")
        def _(eng, wd=wout_done):
            eng.wait_ge(sP3.h, wd)
            for j in range(8):
                eng.indirect_dma_start(
                    out=attn_fm[j], out_offset=None, in_=ccO_v,
                    in_offset=bass.IndirectOffsetOnAxis(P_["o_idx"][:, j:j + 1], 0),
                    bounds_check=BR[TOK - 1], oob_is_err=False,
                ).then_inc(sP3.h, 16)
        attn_g = sP3.bump(16 * 8)

        for oc in range(8):
            for fc in range(4):
                pidx = (oc * 4 + fc) % 2
                ps = PS[pidx]
                lev = last_evac[pidx]
                @on("tensor")
                def _(eng, oc=oc, fc=fc, ps=ps, ag=attn_g, x0r=x0_re, lev=lev):
                    eng.wait_ge(sP3.h, ag)
                    eng.wait_ge(sXB.h, x0r)
                    if lev:
                        eng.wait_ge(sV.h, lev)
                    for kt in range(KT):
                        last = eng.matmul(ps[:, 0:512],
                                   wout_sb[:, 1024 * kt + 128 * oc:1024 * kt + 128 * oc + 128],
                                   attn_fm[kt][:, 512 * fc:512 * fc + 512],
                                   start=(kt == 0), stop=(kt == KT - 1))
                    last.then_inc(sT.h, 1)
                td = sT.bump()
                @on("vector")
                def _(eng, oc=oc, fc=fc, ps=ps, td=td):
                    eng.wait_ge(sT.h, td)
                    eng.scalar_tensor_tensor(xres[oc][:, 512 * fc:512 * fc + 512],
                                             ps[:, 0:512], P_["bout_col"][:, oc:oc + 1],
                                             xres[oc][:, 512 * fc:512 * fc + 512],
                                             ALU.add, ALU.add).then_inc(sV.h, 1)
                last_evac[pidx] = sV.bump()
        barrier()

        def emit_output():
            barrier()
            @on("sync")
            def _(eng):
                eng.dma_start(out=out_x.rearrange("(kt p) t -> p kt t", p=128),
                              in_=AR[:, XRES0:XRES0 + 32768].bitcast(F32)
                              .rearrange("p (kt t) -> p kt t", t=2048)
                              ).then_inc(sOUT.h, 16)
            od = sOUT.bump(16)
            @on("sync")
            def _(eng, od=od):
                eng.wait_ge(sOUT.h, od)

        if DEV_STAGE == 1:
            emit_output()
            prog.install(block)
            return nc

        def emit_ffn(app):
            FA = FBD[app]
            lnbf = [abf(2048 * kt, 2048) for kt in range(KT)]
            lnTM = [abf(24576 + 1024 * tt, 1024) for tt in range(NTT)]
            lnf32 = [af32(40960 + 1024 * kt, 512) for kt in range(KT)]
            logitsTM = af32(16384, 128)
            ohall = af32(16640, 128)
            cums = af32(16896, 128)
            selscr = af32(17152, 128)
            emit_layernorm(xres, lnbf, 18432, need_rows=True)
            barrier()
            tp_evac = 0
            for kt in range(KT):
                for tt in range(NTT):
                    par = (kt * NTT + tt) % 2
                    @on("tensor")
                    def _(eng, kt=kt, tt=tt, par=par, tp_evac=tp_evac):
                        if tp_evac:
                            eng.wait_ge(sV.h, tp_evac)
                        last = eng.transpose(PSB[par][:, 0:128],
                                      lnbf[kt][:, 128 * tt:128 * tt + 128],
                                      P_["id128_b"][:])
                        last.then_inc(sT.h, 1)
                    t_tp = sT.bump()
                    @on("vector")
                    def _(eng, kt=kt, tt=tt, par=par, t_tp=t_tp):
                        eng.wait_ge(sT.h, t_tp)
                        eng.tensor_copy(lnTM[tt][:, 128 * kt:128 * kt + 128],
                                        PSB[par][:, 0:128]).then_inc(sV.h, 1)
                    newv = sV.bump()
                    if par == 1:
                        tp_evac = newv
            lnbf_free = sT.n

            for c in range(4):
                c0 = 512 * c
                @on("tensor")
                def _(eng, c0=c0):
                    last = eng.matmul(PS[3][:, 0:512], P_["ones_r_f"][0:1, :],
                               P_["mrow"][0:1, c0:c0 + 512], start=True, stop=True)
                    last = eng.matmul(PS[3][:, 512:1024], P_["ones_r_f"][0:1, :],
                               P_["rrow"][0:1, c0:c0 + 512], start=True, stop=True)
                    last.then_inc(sT.h, 1)
                tb = sT.bump()
                for kt in range(KT):
                    @on("vector")
                    def _(eng, kt=kt, c0=c0, tb=tb, vprev=sV.n):
                        if kt == 0:
                            eng.wait_ge(sT.h, tb)
                        eng.wait_ge(sV.h, vprev)
                        eng.tensor_sub(P_["lnscr"][:, 0:512], xres[kt][:, c0:c0 + 512],
                                       PS[3][:, 0:512]).then_inc(sV.h, 1)
                    va = sV.bump()
                    @on("vector")
                    def _(eng, kt=kt, va=va):
                        eng.wait_ge(sV.h, va)
                        eng.tensor_mul(lnf32[kt], P_["lnscr"][:, 0:512],
                                       PS[3][:, 512:1024]).then_inc(sV.h, 1)
                    sV.bump()
                vls = sV.n
                @on("tensor")
                def _(eng, vls=vls):
                    eng.wait_ge(sV.h, vls)
                    for kt in range(KT):
                        last = eng.matmul(PS[2][0:8, 0:512], P_["wg_t"][:, 8 * kt:8 * kt + 8],
                                   lnf32[kt], start=(kt == 0), stop=(kt == KT - 1))
                    last.then_inc(sT.h, 1)
                t_log = sT.bump()
                @on("vector")
                def _(eng, t_log=t_log):
                    eng.wait_ge(sT.h, t_log)
                    eng.tensor_copy(P_["lnscr"][0:8, 0:512], PS[2][0:8, 0:512]).then_inc(sV.h, 1)
                v_log = sV.bump()
                for tl in range(4):
                    tt = 4 * c + tl
                    @on("tensor")
                    def _(eng, tl=tl, v_log=v_log, vprev=sV.n):
                        eng.wait_ge(sV.h, v_log)
                        eng.wait_ge(sV.h, vprev)
                        last = eng.transpose(PS[2][:, 512:520],
                                      P_["lnscr"][0:8, 128 * tl:128 * tl + 128],
                                      P_["id128_f"][0:8, 0:8])
                        last.then_inc(sT.h, 1)
                    t_tp = sT.bump()
                    @on("vector")
                    def _(eng, tt=tt, t_tp=t_tp):
                        eng.wait_ge(sT.h, t_tp)
                        eng.tensor_copy(logitsTM[:, 8 * tt:8 * tt + 8],
                                        PS[2][:, 512:520]).then_inc(sV.h, 1)
                    sV.bump()
            logits_v = sV.n

            lv3 = logitsTM.rearrange("p (t e) -> p t e", e=8)
            @on("vector")
            def _(eng, lv=logits_v):
                eng.wait_ge(sV.h, lv)
                eng.reduce_max(RT[:, 0:16], lv3, AX.X).then_inc(sV.h, 1)
            v_m = sV.bump()
            @on("vector")
            def _(eng, v_m=v_m):
                eng.wait_ge(sV.h, v_m)
                for tt in range(NTT):
                    last = eng.tensor_scalar(ohall[:, 8 * tt:8 * tt + 8],
                                             logitsTM[:, 8 * tt:8 * tt + 8],
                                             RT[:, tt:tt + 1], None, ALU.is_ge)
                last.then_inc(sV.h, 1)
            v_oh = sV.bump()
            @on("scalar")
            def _(eng, lv=logits_v, v_m=v_m):
                eng.wait_ge(sV.h, v_m)
                eng.activation(selscr[:, 0:128], logitsTM[:, 0:128], AF.Exp).then_inc(sS.h, 1)
            s_el = sS.bump()
            @on("scalar")
            def _(eng, v_m=v_m):
                eng.wait_ge(sV.h, v_m)
                eng.activation(RT[:, 16:32], RT[:, 0:16], AF.Exp).then_inc(sS.h, 1)
            s_em = sS.bump()
            @on("vector")
            def _(eng, s_el=s_el):
                eng.wait_ge(sS.h, s_el)
                eng.reduce_sum(RT[:, 32:48], selscr.rearrange("p (t e) -> p t e", e=8),
                               AX.X).then_inc(sV.h, 1)
            v_se = sV.bump()
            @on("vector")
            def _(eng, v_se=v_se):
                eng.wait_ge(sV.h, v_se)
                eng.reciprocal(RT[:, 48:64], RT[:, 32:48]).then_inc(sV.h, 1)
            v_rs = sV.bump()
            @on("vector")
            def _(eng, v_rs=v_rs, s_em=s_em):
                eng.wait_ge(sV.h, v_rs)
                eng.wait_ge(sS.h, s_em)
                eng.tensor_mul(RT[:, 64:80], RT[:, 16:32], RT[:, 48:64]).then_inc(sV.h, 1)
            v_g1 = sV.bump()

            @on("tensor")
            def _(eng, v_oh=v_oh):
                eng.wait_ge(sV.h, v_oh)
                for tt in range(NTT):
                    last = eng.matmul(PS[2][:, 512 + 8 * tt:512 + 8 * tt + 8],
                               P_["tri128i_f"][:], ohall[:, 8 * tt:8 * tt + 8],
                               start=True, stop=True)
                last = eng.matmul(PS[3][0:1, 0:128], P_["ones_c_f"][:], ohall[:, 0:128],
                           start=True, stop=True)
                last.then_inc(sT.h, 1)
            t_cs = sT.bump()
            @on("vector")
            def _(eng, t_cs=t_cs):
                eng.wait_ge(sT.h, t_cs)
                eng.tensor_copy(cums[:, 0:128], PS[2][:, 512:640]).then_inc(sV.h, 1)
            v_cums = sV.bump()
            @on("vector")
            def _(eng, t_cs=t_cs):
                eng.wait_ge(sT.h, t_cs)
                eng.tensor_copy(P_["row8"][0:1, 0:128], PS[3][0:1, 0:128]).then_inc(sV.h, 1)
            v_ts = sV.bump()
            @on("tensor")
            def _(eng, v_ts=v_ts):
                eng.wait_ge(sV.h, v_ts)
                last = eng.matmul(PS[3][:, 0:1], P_["row8"][0:1, 0:128], CB[0:1, 3:4],
                           start=True, stop=True)
                last.then_inc(sT.h, 1)
            t_tc = sT.bump()
            @on("vector")
            def _(eng, t_tc=t_tc):
                eng.wait_ge(sT.h, t_tc)
                eng.tensor_copy(SR[:, 0:1], PS[3][:, 0:1]).then_inc(sV.h, 1)
            v_tc = sV.bump()
            @on("tensor")
            def _(eng, v_tc=v_tc):
                eng.wait_ge(sV.h, v_tc)
                last = eng.matmul(PS[3][:, 4:5], P_["tileoff_m"][:], SR[:, 0:1],
                           start=True, stop=True)
                last = eng.matmul(PS[3][0:1, 8:16], SR[:, 0:1], P_["esum_m"][:],
                           start=True, stop=True)
                last.then_inc(sT.h, 1)
            t_off = sT.bump()
            @on("vector")
            def _(eng, t_off=t_off):
                eng.wait_ge(sT.h, t_off)
                eng.tensor_copy(SR[:, 1:2], PS[3][:, 4:5])
                last = eng.tensor_copy(P_["row8"][0:1, 128:136], PS[3][0:1, 8:16])
                last.then_inc(sV.h, 1)
            v_off = sV.bump()
            @on("tensor")
            def _(eng, v_off=v_off):
                eng.wait_ge(sV.h, v_off)
                last = eng.matmul(PS[3][0:1, 16:144], SR[:, 1:2], P_["id128_f"][:],
                           start=True, stop=True)
                last.then_inc(sT.h, 1)
            t_or = sT.bump()
            @on("vector")
            def _(eng, t_or=t_or):
                eng.wait_ge(sT.h, t_or)
                eng.tensor_copy(P_["row8"][0:1, 0:128], PS[3][0:1, 16:144]).then_inc(sV.h, 1)
            v_or = sV.bump()
            @on("tensor")
            def _(eng, v_or=v_or):
                eng.wait_ge(sV.h, v_or)
                last = eng.matmul(PS[3][:, 512:640], P_["ones_r_f"][0:1, :],
                           P_["row8"][0:1, 0:128], start=True, stop=True)
                last.then_inc(sT.h, 1)
            t_ob = sT.bump()
            @on("vector")
            def _(eng, v_cums=v_cums, v_oh=v_oh):
                eng.wait_ge(sV.h, v_cums)
                eng.tensor_mul(selscr[:, 0:128], ohall[:, 0:128], cums[:, 0:128]).then_inc(sV.h, 1)
            v1 = sV.bump()
            @on("vector")
            def _(eng, v1=v1):
                eng.wait_ge(sV.h, v1)
                eng.reduce_sum(RT[:, 80:96], selscr.rearrange("p (t e) -> p t e", e=8),
                               AX.X).then_inc(sV.h, 1)
            v2 = sV.bump()
            @on("vector")
            def _(eng, t_ob=t_ob, vprev=sV.n):
                eng.wait_ge(sT.h, t_ob)
                eng.wait_ge(sV.h, vprev)
                eng.tensor_mul(selscr[:, 0:128], ohall[:, 0:128],
                               PS[3][:, 512:640]).then_inc(sV.h, 1)
            v3 = sV.bump()
            @on("vector")
            def _(eng, v3=v3):
                eng.wait_ge(sV.h, v3)
                eng.reduce_sum(RT[:, 96:112], selscr.rearrange("p (t e) -> p t e", e=8),
                               AX.X).then_inc(sV.h, 1)
            v4 = sV.bump()
            @on("vector")
            def _(eng, v2=v2, v4=v4):
                eng.wait_ge(sV.h, max(v2, v4))
                eng.tensor_add(RT[:, 112:128], RT[:, 80:96], RT[:, 96:112]).then_inc(sV.h, 1)
            v5 = sV.bump()
            @on("vector")
            def _(eng, v5=v5):
                eng.wait_ge(sV.h, v5)
                eng.tensor_scalar_add(RT[:, 112:128], RT[:, 112:128], -1.0).then_inc(sV.h, 1)
            v_posmy = sV.bump()
            @on("vector")
            def _(eng, v_oh=v_oh, vprev=sV.n):
                eng.wait_ge(sV.h, vprev)
                for tt in range(NTT):
                    last = eng.tensor_mul(selscr[:, 8 * tt:8 * tt + 8],
                                          ohall[:, 8 * tt:8 * tt + 8], P_["colidx8"][:])
                last.then_inc(sV.h, 1)
            v6 = sV.bump()
            @on("vector")
            def _(eng, v6=v6):
                eng.wait_ge(sV.h, v6)
                eng.reduce_sum(RT[:, 128:144], selscr.rearrange("p (t e) -> p t e", e=8),
                               AX.X).then_inc(sV.h, 1)
            v_e = sV.bump()

            @on("gpsimd")
            def _(eng, v_off=v_off):
                eng.wait_ge(sV.h, v_off)
                eng.dma_start(out=FA["cnt_src"], in_=P_["row8"][0:1, 128:136]).then_inc(sCNT.h, 16)
            g1_ = sCNT.bump(16)
            @on("gpsimd")
            def _(eng, g1_=g1_):
                eng.wait_ge(sCNT.h, g1_)
                eng.collective_compute("AllGather", ALU.bypass,
                                       replica_groups=[list(range(NCORE))],
                                       ins=[FA["cnt_src"].opt()], outs=[FA["cnt"].opt()]
                                       ).then_inc(sCC.h, 1)
            ag_cnt = sCC.bump()
            @on("gpsimd")
            def _(eng, ag=ag_cnt):
                eng.wait_ge(sCC.h, ag)
                eng.dma_start(out=P_["cnt_all"][:], in_=FA["cnt"]).then_inc(sCNT.h, 16)
            cnt_ld = sCNT.bump(16)

            SM = P_["sm8"]
            @on("tensor")
            def _(eng, v_off=v_off, vprev=sV.n):
                eng.wait_ge(sV.h, vprev)
                last = eng.matmul(PS[3][0:8, 24:25], P_["row8"][0:1, 128:136],
                                  CB[0:1, 3:4], start=True, stop=True)
                last.then_inc(sT.h, 1)
            t_l1 = sT.bump()
            @on("vector")
            def _(eng, t_l1=t_l1):
                eng.wait_ge(sT.h, t_l1)
                eng.tensor_copy(SM[:, 34:35], PS[3][0:8, 24:25]).then_inc(sV.h, 1)
            v_l1 = sV.bump()
            @on("tensor")
            def _(eng, v_l1=v_l1):
                eng.wait_ge(sV.h, v_l1)
                last = eng.matmul(PS[3][0:1, 26:34], SM[:, 34:35], P_["stri8"][:],
                                  start=True, stop=True)
                last.then_inc(sT.h, 1)
            t_l2 = sT.bump()
            @on("vector")
            def _(eng, t_l2=t_l2):
                eng.wait_ge(sT.h, t_l2)
                eng.tensor_copy(P_["row8"][0:1, 144:152], PS[3][0:1, 26:34]).then_inc(sV.h, 1)
            v_l2 = sV.bump()
            @on("tensor")
            def _(eng, v_l2=v_l2, vprev=sV.n):
                eng.wait_ge(sV.h, v_l2)
                eng.wait_ge(sV.h, vprev)
                last = eng.matmul(PS[2][:, 664:672], P_["ones_r_f"][0:1, :],
                                  P_["row8"][0:1, 144:152], start=True, stop=True)
                last.then_inc(sT.h, 1)
            t_rb = sT.bump()
            @on("vector")
            def _(eng, t_rb=t_rb):
                eng.wait_ge(sT.h, t_rb)
                eng.tensor_copy(RT[:, 144:152], PS[2][:, 664:672]).then_inc(sV.h, 1)
            v_rb = sV.bump()
            @on("vector")
            def _(eng, v_rb=v_rb, vprev=sV.n):
                eng.wait_ge(sV.h, max(v_rb, vprev))
                for tt in range(NTT):
                    last = eng.tensor_mul(selscr[:, 8 * tt:8 * tt + 8],
                                          ohall[:, 8 * tt:8 * tt + 8], RT[:, 144:152])
                last.then_inc(sV.h, 1)
            vr1_ = sV.bump()
            @on("vector")
            def _(eng, vr1_=vr1_):
                eng.wait_ge(sV.h, vr1_)
                eng.reduce_sum(RT[:, 32:48], selscr.rearrange("p (t e) -> p t e", e=8),
                               AX.X).then_inc(sV.h, 1)
            vr2_ = sV.bump()
            @on("vector")
            def _(eng, vr2_=vr2_):
                eng.wait_ge(sV.h, vr2_)
                eng.tensor_add(RT[:, 32:48], RT[:, 32:48], RT[:, 112:128]).then_inc(sV.h, 1)
            v_dl = sV.bump()
            @on("vector")
            def _(eng, v_dl=v_dl):
                eng.wait_ge(sV.h, v_dl)
                eng.tensor_copy(P_["ri"][:, 0:16], RT[:, 32:48]).then_inc(sV.h, 1)
            v_dli = sV.bump()

            tm_done = sV.n
            @on("gpsimd")
            def _(eng, tm=tm_done, v_dli=v_dli):
                eng.wait_ge(sV.h, max(tm, v_dli))
                for tt in range(NTT):
                    eng.indirect_dma_start(
                        out=FA["g_src"],
                        out_offset=bass.IndirectOffsetOnAxis(P_["ri"][:, tt:tt + 1], 0),
                        in_=lnTM[tt], in_offset=None,
                        bounds_check=BR[TOK - 1], oob_is_err=False,
                    ).then_inc(sSCAT.h, 16)
            scat_done = sSCAT.bump(16 * NTT)
            @on("gpsimd")
            def _(eng, sd=scat_done):
                eng.wait_ge(sSCAT.h, sd)
                eng.collective_compute("AllGather", ALU.bypass,
                                       replica_groups=[list(range(NCORE))],
                                       ins=[FA["g_src"].opt()], outs=[FA["g"].opt()]
                                       ).then_inc(sCC.h, 1)
            ag_g = sCC.bump()

            @on("tensor")
            def _(eng, cl=cnt_ld, vprev=sV.n):
                eng.wait_ge(sCNT.h, cl)
                eng.wait_ge(sV.h, vprev)
                last = eng.matmul(PS[3][0:8, 0:8], P_["stri8"][:], P_["cnt_all"][:],
                           start=True, stop=True)
                last = eng.matmul(PS[3][0:8, 8:16], P_["cnt_all"][:], P_["stri8"][:],
                           start=True, stop=True)
                last = eng.matmul(PS[3][0:8, 16:24], P_["cnt_all"][:], P_["tri8i"][:],
                           start=True, stop=True)
                last = eng.transpose(PS[3][0:8, 32:40], P_["cnt_all"][:], P_["id128_f"][0:8, 0:8])
                last.then_inc(sT.h, 1)
            t_p1 = sT.bump()
            @on("vector")
            def _(eng, t_p1=t_p1):
                eng.wait_ge(sT.h, t_p1)
                eng.tensor_copy(SM[:, 0:8], PS[3][0:8, 0:8])
                eng.tensor_copy(SM[:, 8:16], PS[3][0:8, 8:16])
                eng.tensor_copy(SM[:, 16:24], PS[3][0:8, 16:24])
                last = eng.tensor_copy(SM[:, 24:32], PS[3][0:8, 32:40])
                last.then_inc(sV.h, 1)
            v_p1 = sV.bump()
            @on("tensor")
            def _(eng, v_p1=v_p1):
                eng.wait_ge(sV.h, v_p1)
                last = eng.matmul(PS[3][0:1, 0:8], P_["rank1h"][:], SM[:, 0:8],
                           start=True, stop=True)
                last = eng.matmul(PS[3][0:8, 8:9], SM[:, 8:16], P_["rank1h"][:],
                           start=True, stop=True)
                last = eng.matmul(PS[3][0:8, 9:10], SM[:, 16:24], P_["rank1h"][:],
                           start=True, stop=True)
                last = eng.matmul(PS[3][0:8, 16:24], P_["stri8"][:], SM[:, 24:32],
                           start=True, stop=True)
                last.then_inc(sT.h, 1)
            t_p2 = sT.bump()
            @on("vector")
            def _(eng, t_p2=t_p2):
                eng.wait_ge(sT.h, t_p2)
                eng.tensor_copy(P_["row8"][0:1, 136:144], PS[3][0:1, 0:8])
                eng.tensor_copy(SM[:, 32:33], PS[3][0:8, 8:9])
                eng.tensor_copy(SM[:, 33:34], PS[3][0:8, 9:10])
                last = eng.tensor_copy(SM[:, 16:24], PS[3][0:8, 16:24])
                last.then_inc(sV.h, 1)
            v_p2 = sV.bump()
            @on("tensor")
            def _(eng, v_p2=v_p2):
                eng.wait_ge(sV.h, v_p2)
                last = eng.matmul(PS[3][0:8, 25:26], SM[:, 16:24], P_["rank1h"][:],
                           start=True, stop=True)
                last = eng.matmul(PS[3][:, 40:48], P_["ones_r_f"][0:1, :],
                           P_["row8"][0:1, 136:144], start=True, stop=True)
                last.then_inc(sT.h, 1)
            t_p3 = sT.bump()
            @on("vector")
            def _(eng, t_p3=t_p3):
                eng.wait_ge(sT.h, t_p3)
                eng.tensor_copy(SM[:, 35:36], PS[3][0:8, 25:26])
                last = eng.tensor_copy(RT[:, 144:152], PS[3][:, 40:48])
                last.then_inc(sV.h, 1)
            v_p3 = sV.bump()
            @on("vector")
            def _(eng, v_p3=v_p3):
                eng.wait_ge(sV.h, v_p3)
                eng.tensor_add(SM[:, 36:37], P_["rank2048"][:], SM[:, 35:36]).then_inc(sV.h, 1)
            vb1 = sV.bump()
            @on("vector")
            def _(eng, vb1=vb1):
                eng.wait_ge(sV.h, vb1)
                eng.tensor_sub(SM[:, 36:37], SM[:, 36:37], SM[:, 32:33]).then_inc(sV.h, 1)
            vb2 = sV.bump()
            @on("tensor")
            def _(eng, vb2=vb2):
                eng.wait_ge(sV.h, vb2)
                last = eng.matmul(PS[3][0:1, 48:56], SM[:, 32:33], P_["id128_f"][0:8, 0:8],
                           start=True, stop=True)
                last = eng.matmul(PS[3][0:1, 56:64], SM[:, 33:34], P_["id128_f"][0:8, 0:8],
                           start=True, stop=True)
                last = eng.matmul(PS[3][0:1, 64:72], SM[:, 36:37], P_["id128_f"][0:8, 0:8],
                           start=True, stop=True)
                last.then_inc(sT.h, 1)
            t_p4 = sT.bump()
            @on("vector")
            def _(eng, t_p4=t_p4):
                eng.wait_ge(sT.h, t_p4)
                eng.tensor_copy(P_["row8"][0:1, 152:160], PS[3][0:1, 64:72])
                eng.tensor_copy(SR[0:1, 2:10], PS[3][0:1, 48:56])
                last = eng.tensor_copy(SR[0:1, 10:11], PS[3][0:1, 56:57])
                last.then_inc(sV.h, 1)
            v_p4 = sV.bump()
            @on("tensor")
            def _(eng, v_p4=v_p4):
                eng.wait_ge(sV.h, v_p4)
                last = eng.matmul(PS[2][:, 640:648], P_["ones_r_f"][0:1, :], SR[0:1, 2:10],
                           start=True, stop=True)
                last.then_inc(sT.h, 1)
            t_x1 = sT.bump()
            @on("vector")
            def _(eng, t_p4=t_p4, t_x1=t_x1):
                eng.wait_ge(sT.h, t_p4)
                eng.wait_ge(sT.h, t_x1)
                eng.tensor_copy(SR[0:1, 2:10], PS[3][0:1, 56:64]).then_inc(sV.h, 1)
            v_x1 = sV.bump()
            @on("tensor")
            def _(eng, v_x1=v_x1, t_x1=t_x1):
                eng.wait_ge(sV.h, v_x1)
                last = eng.matmul(PS[2][:, 648:656], P_["ones_r_f"][0:1, :], SR[0:1, 2:10],
                           start=True, stop=True)
                last = eng.matmul(PS[2][:, 656:664], P_["ones_r_f"][0:1, :],
                           P_["row8"][0:1, 152:160], start=True, stop=True)
                last.then_inc(sT.h, 1)
            t_x2 = sT.bump()
            @on("vector")
            def _(eng, t_x1=t_x1, t_x2=t_x2):
                eng.wait_ge(sT.h, max(t_x1, t_x2))
                eng.tensor_copy(RT[:, 152:160], PS[2][:, 640:648])
                eng.tensor_copy(RT[:, 160:168], PS[2][:, 648:656])
                last = eng.tensor_copy(RT[:, 168:176], PS[2][:, 656:664])
                last.then_inc(sV.h, 1)
            v_bc = sV.bump()

            @on("vector")
            def _(eng, v_posmy=v_posmy, vprev=sV.n):
                eng.wait_ge(sV.h, vprev)
                for tt in range(NTT):
                    last = eng.tensor_mul(selscr[:, 8 * tt:8 * tt + 8],
                                          ohall[:, 8 * tt:8 * tt + 8], RT[:, 144:152])
                last.then_inc(sV.h, 1)
            vq1 = sV.bump()
            @on("vector")
            def _(eng, vq1=vq1):
                eng.wait_ge(sV.h, vq1)
                eng.reduce_sum(RT[:, 0:16], selscr.rearrange("p (t e) -> p t e", e=8),
                               AX.X).then_inc(sV.h, 1)
            vq2 = sV.bump()
            @on("vector")
            def _(eng, vq2=vq2):
                eng.wait_ge(sV.h, vq2)
                eng.tensor_add(RT[:, 0:16], RT[:, 0:16], RT[:, 112:128]).then_inc(sV.h, 1)
            v_posg = sV.bump()
            @on("vector")
            def _(eng, v_posg=v_posg):
                eng.wait_ge(sV.h, v_posg)
                eng.tensor_single_scalar(RT[:, 16:32], RT[:, 0:16], float(CAP),
                                         ALU.is_lt).then_inc(sV.h, 1)
            v_keep = sV.bump()
            @on("vector")
            def _(eng, v_keep=v_keep, v_g1=v_g1):
                eng.wait_ge(sV.h, v_keep)
                eng.tensor_mul(RT[:, 64:80], RT[:, 64:80], RT[:, 16:32]).then_inc(sV.h, 1)
            v_g1e = sV.bump()
            @on("vector")
            def _(eng, v_e=v_e, v_posg=v_posg):
                eng.wait_ge(sV.h, v_e)
                eng.tensor_scalar(RT[:, 48:64], RT[:, 128:144], float(CAP), None,
                                  ALU.mult).then_inc(sV.h, 1)
            vc1 = sV.bump()
            @on("vector")
            def _(eng, vc1=vc1):
                eng.wait_ge(sV.h, vc1)
                eng.tensor_add(RT[:, 48:64], RT[:, 48:64], RT[:, 0:16]).then_inc(sV.h, 1)
            vc2 = sV.bump()
            @on("vector")
            def _(eng, vc2=vc2):
                eng.wait_ge(sV.h, vc2)
                eng.tensor_mul(RT[:, 48:64], RT[:, 48:64], RT[:, 16:32]).then_inc(sV.h, 1)
            vc3 = sV.bump()
            @on("vector")
            def _(eng, vc3=vc3):
                eng.wait_ge(sV.h, vc3)
                eng.scalar_tensor_tensor(RT[:, 48:64], RT[:, 16:32],
                                         -float(NCORE * CAP), RT[:, 48:64],
                                         ALU.mult, ALU.add).then_inc(sV.h, 1)
            vc4 = sV.bump()
            @on("vector")
            def _(eng, vc4=vc4):
                eng.wait_ge(sV.h, vc4)
                eng.tensor_scalar_add(RT[:, 48:64], RT[:, 48:64],
                                      float(NCORE * CAP)).then_inc(sV.h, 1)
            vc5 = sV.bump()
            @on("vector")
            def _(eng, vc5=vc5):
                eng.wait_ge(sV.h, vc5)
                eng.tensor_copy(P_["ri"][:, 16:32], RT[:, 48:64]).then_inc(sV.h, 1)
            v_ci = sV.bump()
            @on("vector")
            def _(eng, v_bc=v_bc):
                eng.wait_ge(sV.h, v_bc)
                eng.memset(RT[:, 96:116], 0.0)
                last = eng.memset(RT[:, 116:136], 0.0)
                last.then_inc(sV.h, 1)
            vi0 = sV.bump()
            prev = vi0

            def vstep(fn, waits=()):
                nonlocal_prev = vchain_state
                @on("vector")
                def _(eng, fn=fn, prev=nonlocal_prev["prev"], waits=tuple(waits)):
                    eng.wait_ge(sV.h, prev)
                    for s_, v_ in waits:
                        eng.wait_ge(s_.h, v_)
                    fn(eng).then_inc(sV.h, 1)
                vchain_state["prev"] = sV.bump()

            vchain_state = {"prev": vi0}
            for r in range(8):
                vstep(lambda e, r=r: e.tensor_scalar(selscr[:, 0:20], P_["c_iota"][:],
                                                     RT[:, 152 + r:153 + r], None, ALU.is_ge))
                vstep(lambda e, r=r: e.tensor_scalar(selscr[:, 20:40], P_["c_iota"][:],
                                                     RT[:, 160 + r:161 + r], None, ALU.is_lt))
                vstep(lambda e: e.tensor_mul(selscr[:, 0:20], selscr[:, 0:20], selscr[:, 20:40]))
                vstep(lambda e, r=r: e.tensor_scalar(selscr[:, 20:40], selscr[:, 0:20],
                                                     RT[:, 168 + r:169 + r], None, ALU.mult))
                vstep(lambda e: e.tensor_add(RT[:, 96:116], RT[:, 96:116], selscr[:, 20:40]))
                vstep(lambda e: e.tensor_mul(selscr[:, 20:40], selscr[:, 0:20], P_["c_iota"][:]))
                vstep(lambda e: e.tensor_add(RT[:, 96:116], RT[:, 96:116], selscr[:, 20:40]))
                vstep(lambda e: e.tensor_add(RT[:, 116:136], RT[:, 116:136], selscr[:, 0:20]))
            prev = vchain_state["prev"]
            @on("vector")
            def _(eng, prev=prev):
                eng.wait_ge(sV.h, prev)
                eng.scalar_tensor_tensor(RT[:, 96:116], RT[:, 116:136], -float(T),
                                         RT[:, 96:116], ALU.mult, ALU.add).then_inc(sV.h, 1)
            vi1 = sV.bump()
            @on("vector")
            def _(eng, vi1=vi1):
                eng.wait_ge(sV.h, vi1)
                eng.tensor_scalar_add(RT[:, 96:116], RT[:, 96:116], float(T)).then_inc(sV.h, 1)
            vi2 = sV.bump()
            @on("vector")
            def _(eng, vi2=vi2):
                eng.wait_ge(sV.h, vi2)
                eng.tensor_copy(P_["ri"][:, 32:52], RT[:, 96:116]).then_inc(sV.h, 1)
            v_ei = sV.bump()


            einTM = [abf(16384 + 1024 * j, 1024) for j in range(4)]
            einFM = [abf(20480 + 512 * kt, 512) for kt in range(KT)]
            hbuf = [abf(24576 + 512 * d, 512) for d in range(32)]
            w1s = [abf(8192 * i, 8192) for i in range(2)]
            w2s = [abf(41984 + 1024 * i, 1024) for i in range(8)]
            eo_sb = [abf(20480 + 1024 * i, 1024) for i in range(2)]
            if app == 0:
                @on("vector")
                def _(eng):
                    for j in range(4):
                        last = eng.memset(einTM[j], 0.0)
                    last.then_inc(sV.h, 1)
                sV.bump()
            ein_ready = sV.n

            w1_lastmm = {}
            w2_lastmm = {}
            for c5 in range(5):
                @on("gpsimd")
                def _(eng, c5=c5, ag=ag_g, v_ei=v_ei, vprev=sV.n):
                    eng.wait_ge(sCC.h, ag)
                    eng.wait_ge(sV.h, vprev)
                    for j in range(4):
                        eng.indirect_dma_start(
                            out=einTM[j], out_offset=None, in_=FA["g"],
                            in_offset=bass.IndirectOffsetOnAxis(
                                P_["ri"][:, 32 + 4 * c5 + j:33 + 4 * c5 + j], 0),
                            bounds_check=BR[T - 1], oob_is_err=False,
                        ).then_inc(sEIN.h, 16)
                g_ein = sEIN.bump(64)
                tpv = 0
                s_prev_eo = sS.n
                eo_prev = (sEO0.n, sEO1.n)
                for j in range(4):
                    for kt in range(KT):
                        par = (j * KT + kt) % 2
                        @on("tensor")
                        def _(eng, j=j, kt=kt, par=par, g_ein=g_ein, tpv=tpv,
                              spe=s_prev_eo, eop=eo_prev):
                            eng.wait_ge(sEIN.h, g_ein)
                            eng.wait_ge(sS.h, spe)
                            if eop[0]:
                                eng.wait_ge(sEO0.h, eop[0])
                            if eop[1]:
                                eng.wait_ge(sEO1.h, eop[1])
                            if tpv:
                                eng.wait_ge(sV.h, tpv)
                            last = eng.transpose(PSB[par][:, 0:128],
                                          einTM[j][:, 128 * kt:128 * kt + 128],
                                          P_["id128_b"][:])
                            last.then_inc(sT.h, 1)
                        t_tp = sT.bump()
                        @on("vector")
                        def _(eng, j=j, kt=kt, par=par, t_tp=t_tp):
                            eng.wait_ge(sT.h, t_tp)
                            eng.tensor_copy(einFM[kt][:, 128 * j:128 * j + 128],
                                            PSB[par][:, 0:128]).then_inc(sV.h, 1)
                        newv = sV.bump()
                        if par == 1:
                            tpv = newv
                ein_fm_v = sV.n
                hv = []
                for q in range(4):
                    wsem = sW0 if q % 2 == 0 else sW1
                    @on("sync")
                    def _(eng, q=q, c5=c5, wsem=wsem, wc=wcache_done,
                          tprev=max(w1_lastmm.get((c5 * 4 + q) % 2, 0), lnbf_free)):
                        eng.wait_ge(sWC.h, wc)
                        if tprev:
                            eng.wait_ge(sT.h, tprev)
                        eng.dma_start(out=s3d(w1s[q % 2], 1024),
                                      in_=w1bf[:, 1024 * q:1024 * q + 1024]
                                      .rearrange("(kt p) c -> p kt c", p=128)
                                      ).then_inc(wsem.h, 16)
                    wsem.bump(16)
                    wtarget = wsem.n
                    for dd in range(8):
                        dt_ = 8 * q + dd
                        ps_h = PS[2] if dt_ % 2 == 0 else PS[3]
                        @on("tensor")
                        def _(eng, q=q, dd=dd, ps_h=ps_h, wsem=wsem, wt=wtarget,
                              efm=ein_fm_v, sprev=sS.n):
                            eng.wait_ge(wsem.h, wt)
                            eng.wait_ge(sV.h, efm)
                            eng.wait_ge(sS.h, sprev)
                            for kt in range(KT):
                                last = eng.matmul(ps_h[:, 0:512],
                                           w1s[q % 2][:, 1024 * kt + 128 * dd:1024 * kt + 128 * dd + 128],
                                           einFM[kt], start=(kt == 0), stop=(kt == KT - 1))
                            last.then_inc(sT.h, 1)
                        t_h = sT.bump()
                        w1_lastmm[(c5 * 4 + q) % 2] = t_h
                        @on("scalar")
                        def _(eng, dt_=dt_, ps_h=ps_h, t_h=t_h):
                            eng.wait_ge(sT.h, t_h)
                            eng.activation(hbuf[dt_], ps_h[:, 0:512], AF.Relu,
                                           bias=P_["b1_col"][:, dt_:dt_ + 1]).then_inc(sS.h, 1)
                        hv.append(sS.bump())
                h_done = sS.n
                @on("tensor")
                def _(eng, hd=h_done):
                    eng.wait_ge(sS.h, hd)
                    for stt in range(4):
                        last = eng.matmul(PS[stt][:, 0:512], P_["ones_r_b"][0:1, :],
                                   P_["b2row_b"][0:1, 0:512], start=True, stop=False)
                        last = eng.matmul(PS[stt][:, 512:1024], P_["ones_r_b"][0:1, :],
                                   P_["b2row_b"][0:1, 512:1024], start=True, stop=False)
                    last.then_inc(sT.h, 1)
                sT.bump()
                for q2 in range(8):
                    wsem = sW2 if q2 % 2 == 0 else sW3
                    @on("sync")
                    def _(eng, q2=q2, wsem=wsem, wc=wcache_done,
                          tprev=w2_lastmm.get((c5 * 8 + q2) % 2, 0)):
                        eng.wait_ge(sWC.h, wc)
                        if tprev:
                            eng.wait_ge(sT.h, tprev)
                        for i4 in range(4):
                            eng.dma_start(out=w2s[4 * (q2 % 2) + i4],
                                          in_=w2bf[512 * q2 + 128 * i4:512 * q2 + 128 * i4 + 128, :]
                                          ).then_inc(wsem.h, 16)
                    wsem.bump(64)
                    wt2 = wsem.n
                    @on("tensor")
                    def _(eng, q2=q2, wsem=wsem, wt2=wt2):
                        eng.wait_ge(wsem.h, wt2)
                        for i4 in range(4):
                            kt2 = 4 * q2 + i4
                            for stt in range(4):
                                last_k = (q2 == 7 and i4 == 3)
                                last = eng.matmul(PS[stt][:, 0:512],
                                           hbuf[kt2][:, 128 * stt:128 * stt + 128],
                                           w2s[4 * (q2 % 2) + i4][:, 0:512], start=False,
                                           stop=last_k)
                                last = eng.matmul(PS[stt][:, 512:1024],
                                           hbuf[kt2][:, 128 * stt:128 * stt + 128],
                                           w2s[4 * (q2 % 2) + i4][:, 512:1024], start=False,
                                           stop=last_k)
                        last.then_inc(sT.h, 1)
                    w2_lastmm[(c5 * 8 + q2) % 2] = sT.bump()
                t_eo = sT.n
                for stt in range(4):
                    par = stt % 2
                    eosem = sEO0 if par == 0 else sEO1
                    @on("scalar")
                    def _(eng, stt=stt, par=par, t_eo=t_eo, eosem=eosem,
                          gprev=(sEO0 if par == 0 else sEO1).n):
                        eng.wait_ge(sT.h, t_eo)
                        eng.wait_ge(eosem.h, gprev)
                        eng.copy(eo_sb[par], PS[stt][:, 0:1024]).then_inc(sS.h, 1)
                    s_eo = sS.bump()
                    @on("gpsimd")
                    def _(eng, c5=c5, stt=stt, par=par, s_eo=s_eo, eosem=eosem):
                        eng.wait_ge(sS.h, s_eo)
                        eng.dma_start(out=FA["eo_src"][512 * c5 + 128 * stt:
                                                       512 * c5 + 128 * stt + 128, :],
                                      in_=eo_sb[par]).then_inc(eosem.h, 16)
                    eosem.bump(16)
            eo_done0, eo_done1 = sEO0.n, sEO1.n
            @on("gpsimd")
            def _(eng, e0=eo_done0, e1=eo_done1):
                eng.wait_ge(sEO0.h, e0)
                eng.wait_ge(sEO1.h, e1)
                eng.collective_compute("AllGather", ALU.bypass,
                                       replica_groups=[list(range(NCORE))],
                                       ins=[FA["eo_src"].opt()], outs=[FA["eo"].opt()]
                                       ).then_inc(sCC.h, 1)
            ag_eo = sCC.bump()

            combTM = [abf(16384 + 1024 * i, 1024) for i in range(2)]
            if app == 0:
                @on("vector")
                def _(eng, vprev=sV.n):
                    eng.wait_ge(sV.h, vprev)
                    eng.memset(combTM[0], 0.0)
                    last = eng.memset(combTM[1], 0.0)
                    last.then_inc(sV.h, 1)
                sV.bump()
            comb_v = {"last": {0: 0, 1: 0}}
            for tt in range(NTT):
                par = tt % 2
                lastv = comb_v["last"][par]
                csem = sCMB0 if par == 0 else sCMB1
                @on("gpsimd")
                def _(eng, tt=tt, par=par, ag=ag_eo, vpre=(sV.n if tt < 2 else 0),
                      lastv=lastv, csem=csem):
                    eng.wait_ge(sCC.h, ag)
                    if vpre:
                        eng.wait_ge(sV.h, vpre)
                    if lastv:
                        eng.wait_ge(sV.h, lastv)
                    eng.indirect_dma_start(
                        out=combTM[par], out_offset=None, in_=FA["eo"],
                        in_offset=bass.IndirectOffsetOnAxis(
                            P_["ri"][:, 16 + tt:17 + tt], 0),
                        bounds_check=BR[NCORE * CAP - 1], oob_is_err=False,
                    ).then_inc(csem.h, 16)
                g_c = csem.bump(16)
                @on("vector")
                def _(eng, tt=tt, par=par, g_c=g_c, csem=csem):
                    eng.wait_ge(csem.h, g_c)
                    eng.tensor_scalar(combTM[par], combTM[par], RT[:, 64 + tt:65 + tt],
                                      None, ALU.mult).then_inc(sV.h, 1)
                v_sc = sV.bump()
                addv = 0
                for kt in range(KT):
                    tpar = kt % 2
                    @on("tensor")
                    def _(eng, tt=tt, par=par, kt=kt, tpar=tpar, v_sc=v_sc, addv=addv):
                        eng.wait_ge(sV.h, v_sc)
                        if addv:
                            eng.wait_ge(sV.h, addv)
                        last = eng.transpose(PSB[tpar][:, 256:384],
                                      combTM[par][:, 128 * kt:128 * kt + 128],
                                      P_["id128_b"][:])
                        last.then_inc(sT.h, 1)
                    t_tp = sT.bump()
                    @on("vector")
                    def _(eng, tt=tt, kt=kt, tpar=tpar, t_tp=t_tp):
                        eng.wait_ge(sT.h, t_tp)
                        eng.tensor_add(xres[kt][:, 128 * tt:128 * tt + 128],
                                       xres[kt][:, 128 * tt:128 * tt + 128],
                                       PSB[tpar][:, 256:384]).then_inc(sV.h, 1)
                    newv = sV.bump()
                    if tpar == 1:
                        addv = newv
                comb_v["last"][par] = sV.n
            barrier()

        emit_ffn(0)
        if DEV_STAGE == 2:
            emit_output()
            prog.install(block)
            return nc

        ln3bf = [abf(2048 * kt, 2048) for kt in range(KT)]
        emit_layernorm(xres, ln3bf, 16384)
        ln3_v = sV.n
        @on("gpsimd")
        def _(eng, lv=ln3_v):
            eng.wait_ge(sV.h, lv)
            for j in range(KT):
                eng.indirect_dma_start(
                    out=cc_ln3p,
                    out_offset=bass.IndirectOffsetOnAxis(P_["sc1_idx"][:, j:j + 1], 0),
                    in_=ln3bf[j], in_offset=None,
                    bounds_check=BR[TOK - 1], oob_is_err=False,
                ).then_inc(sLN3C.h, 16)
        c3 = sLN3C.bump(16 * KT)
        @on("gpsimd")
        def _(eng, c3=c3):
            eng.wait_ge(sLN3C.h, c3)
            eng.collective_compute("AllGather", ALU.bypass,
                                   replica_groups=PAIR_RG,
                                   ins=[bar_src.opt()], outs=[BARS[2].opt()]
                                   ).then_inc(sCC.h, 1)
        ag_ln3 = sCC.bump()
        barrier()

        Wf32 = [af32(8192 * i, 4096) for i in range(5)]
        ln3tile = abf(40960, 4096)
        hout_bf = abf(45056, 4096)

        NSL = 4
        chainH = {i: None for i in range(NSL)}

        def step2(en, fn):
            w = 4096 // NSL
            for hf in range(NSL):
                sem = {"vector": sV, "scalar": sS}[en]
                sl = slice(w * hf, w * hf + w)
                prev = chainH[hf]
                @on(en)
                def _(eng, fn=fn, prev=prev, sl=sl):
                    if prev is not None:
                        ps, pv = prev
                        eng.wait_ge(ps.h, pv)
                    fn(eng, sl).then_inc(sem.h, 1)
                chainH[hf] = (sem, sem.bump())

        for ct in range(4):
            @on("gpsimd")
            def _(eng, ct=ct, ag=ag_ln3, vprev=sV.n):
                eng.wait_ge(sCC.h, ag)
                eng.wait_ge(sV.h, vprev)
                for r in range(2):
                    eng.indirect_dma_start(
                        out=ln3tile[:, 2048 * r:2048 * r + 2048],
                        out_offset=None, in_=cc_ln3p,
                        in_offset=bass.IndirectOffsetOnAxis(
                            P_["ln3_idx"][:, 2 * ct + r:2 * ct + r + 1], 0),
                        bounds_check=BR[TOK - 1], oob_is_err=False,
                    ).then_inc(sL3G.h, 16)
            g_l3 = sL3G.bump(32)
            v_ct0 = sV.n
            cpv = []
            for gi in range(3):
                for fch in range(8):
                    j = gi * 8 + fch
                    slot = j % 8
                    ps_ap = PS[slot // 2][:, 512 * (slot % 2):512 * (slot % 2) + 512]
                    war = cpv[j - 8] if j >= 8 else v_ct0
                    @on("tensor")
                    def _(eng, gi=gi, ct=ct, fch=fch, ps_ap=ps_ap, g_l3=g_l3, war=war):
                        eng.wait_ge(sL3G.h, g_l3)
                        eng.wait_ge(sV.h, war)
                        last = eng.matmul(ps_ap,
                                   P_["wbd"][:, 128 * (4 * gi + ct):128 * (4 * gi + ct) + 128],
                                   ln3tile[:, 512 * fch:512 * fch + 512],
                                   start=True, stop=True)
                        last.then_inc(sT.h, 1)
                    t_g = sT.bump()
                    @on("vector")
                    def _(eng, gi=gi, fch=fch, ps_ap=ps_ap, t_g=t_g):
                        eng.wait_ge(sT.h, t_g)
                        eng.tensor_copy(Wf32[gi][:, 512 * fch:512 * fch + 512],
                                        ps_ap).then_inc(sV.h, 1)
                    cpv.append(sV.bump())
            for _i in range(NSL):
                chainH[_i] = (sV, sV.n)
            F_, I_, H_, T1, T2 = Wf32
            step2("scalar", lambda e, sl: e.activation(T1[:, sl], F_[:, sl], AF.Sigmoid))
            step2("scalar", lambda e, sl: e.activation(T2[:, sl], I_[:, sl], AF.Sigmoid))
            step2("scalar", lambda e, sl: e.activation(F_[:, sl], H_[:, sl], AF.Sigmoid))
            step2("vector", lambda e, sl: e.tensor_add(I_[:, sl], T1[:, sl], T2[:, sl]))
            step2("scalar", lambda e, sl: e.activation(I_[:, sl], I_[:, sl], AF.Ln))
            step2("scalar", lambda e, sl: e.activation(T1[:, sl], T1[:, sl], AF.Ln))
            step2("scalar", lambda e, sl: e.activation(T2[:, sl], T2[:, sl], AF.Ln))
            step2("scalar", lambda e, sl: e.activation(F_[:, sl], F_[:, sl], AF.Ln))
            step2("vector", lambda e, sl: e.tensor_sub(T1[:, sl], T1[:, sl], I_[:, sl]))
            step2("vector", lambda e, sl: e.tensor_sub(T2[:, sl], T2[:, sl], I_[:, sl]))
            step2("scalar", lambda e, sl: e.activation(T1[:, sl], T1[:, sl], AF.Exp, bias=CB[:, 1:2]))
            step2("scalar", lambda e, sl: e.activation(T2[:, sl], T2[:, sl], AF.Exp))
            step2("scalar", lambda e, sl: e.activation(F_[:, sl], F_[:, sl], AF.Exp, bias=CB[:, 1:2]))
            step2("scalar", lambda e, sl: e.activation(I_[:, sl], H_[:, sl], AF.Relu))
            step2("scalar", lambda e, sl: e.add(I_[:, sl], I_[:, sl], CB[:, 2:3]))
            step2("vector", lambda e, sl: e.tensor_single_scalar(H_[:, sl], H_[:, sl], 0.0, ALU.is_ge))
            step2("vector", lambda e, sl: e.tensor_sub(I_[:, sl], I_[:, sl], F_[:, sl]))
            step2("vector", lambda e, sl: e.tensor_mul(I_[:, sl], H_[:, sl], I_[:, sl]))
            step2("vector", lambda e, sl: e.tensor_add(I_[:, sl], I_[:, sl], F_[:, sl]))
            step2("vector", lambda e, sl: e.tensor_mul(F_[:, sl], T2[:, sl], I_[:, sl]))
            w = 4096 // NSL
            prev_scan = None
            for hf in range(NSL):
                prevc = chainH[hf]
                ini = 0.0 if hf == 0 else H_[:, w * hf - 1:w * hf]
                @on("vector")
                def _(eng, prevc=prevc, prev_scan=prev_scan, hf=hf, ini=ini):
                    ps, pv = prevc
                    eng.wait_ge(ps.h, pv)
                    if prev_scan is not None:
                        eng.wait_ge(prev_scan[0].h, prev_scan[1])
                    eng.tensor_tensor_scan(H_[:, w * hf:w * hf + w],
                                           T1[:, w * hf:w * hf + w],
                                           F_[:, w * hf:w * hf + w],
                                           ini, ALU.mult, ALU.add).then_inc(sV.h, 1)
                chainH[hf] = (sV, sV.bump())
                prev_scan = chainH[hf]
            pends = [chainH[i] for i in range(NSL)]
            @on("vector")
            def _(eng, pends=tuple(pends), hcprev=sHC.n):
                for ps_, pv_ in pends:
                    eng.wait_ge(ps_.h, pv_)
                eng.wait_ge(sHC.h, hcprev)
                eng.tensor_copy(hout_bf[:, :], H_[:, :]).then_inc(sV.h, 1)
            for _i in range(NSL):
                chainH[_i] = (sV, sV.bump()) if _i == 0 else chainH[0]
            hv = sV.n
            @on("gpsimd")
            def _(eng, ct=ct, hv=hv):
                eng.wait_ge(sV.h, hv)
                eng.indirect_dma_start(
                    out=cc_hp,
                    out_offset=bass.IndirectOffsetOnAxis(P_["sch_idx"][:, ct:ct + 1], 0),
                    in_=hout_bf[:, :], in_offset=None,
                    bounds_check=BR[D - 1], oob_is_err=False,
                ).then_inc(sHC.h, 16)
            sHC.bump(16)
        h_contrib = sHC.n
        @on("gpsimd")
        def _(eng, hc=h_contrib):
            eng.wait_ge(sHC.h, hc)
            eng.collective_compute("AllGather", ALU.bypass,
                                   replica_groups=PAIR_RG,
                                   ins=[bar_src.opt()], outs=[BARS[3].opt()]
                                   ).then_inc(sCC.h, 1)
        ag_h = sCC.bump()
        ccH_v = cc_hp.rearrange("r (h t) -> (r h) t", h=2)
        hadd = [abf(2048 * kt, 2048) for kt in range(KT)]
        @on("gpsimd")
        def _(eng, ag=ag_h, vprev=sV.n):
            eng.wait_ge(sCC.h, ag)
            eng.wait_ge(sV.h, vprev)
            for j in range(KT):
                eng.indirect_dma_start(
                    out=hadd[j], out_offset=None, in_=ccH_v,
                    in_offset=bass.IndirectOffsetOnAxis(P_["h_idx"][:, j:j + 1], 0),
                    bounds_check=BR[TOK - 1], oob_is_err=False,
                ).then_inc(sHG.h, 16)
        g_h = sHG.bump(16 * KT)
        @on("vector")
        def _(eng, g_h=g_h):
            eng.wait_ge(sHG.h, g_h)
            for kt in range(KT):
                last = eng.tensor_add(xres[kt], xres[kt], hadd[kt])
            last.then_inc(sV.h, 1)
        sV.bump()
        barrier()

        if DEV_STAGE == 3:
            emit_output()
            prog.install(block)
            return nc

        emit_ffn(1)
        emit_output()
        prog.install(block)
    return nc



_NC_CACHE = {}


def _host_inputs(inp, k):
    s, h2 = k // 2, k % 2
    bs, c2 = k // 2, k % 2
    f32 = np.float32
    d = {}
    x0 = np.asarray(inp["L0_attn_a_x"], f32)
    d["x0_my"] = np.ascontiguousarray(x0[s, h2 * TOK:(h2 + 1) * TOK, :].T)
    wqkv = np.asarray(inp["L0_attn_a_w_qkv"], f32)
    cols = np.r_[512 * h2:512 * h2 + 512,
                 1024 + 512 * h2:1024 + 512 * h2 + 512,
                 2048 + 512 * h2:2048 + 512 * h2 + 512]
    d["wqkv_my"] = np.ascontiguousarray(wqkv[:, cols])
    bq = np.asarray(inp["L0_attn_a_b_qkv"], f32)
    bq_my = bq[cols]
    d["bqkv_col"] = np.ascontiguousarray(bq_my[:1024].reshape(8, 128).T)
    d["bqkv_vrow"] = np.ascontiguousarray(bq_my[1024:1536].reshape(1, 512))
    d["w_out"] = np.ascontiguousarray(np.asarray(inp["L0_attn_a_w_out"], f32))
    d["bout_col"] = np.ascontiguousarray(
        np.asarray(inp["L0_attn_a_b_out"], f32).reshape(8, 128).T)
    d["wg_t"] = np.ascontiguousarray(
        np.asarray(inp["L0_ffn_wg"], f32).reshape(8, 128, 8).transpose(1, 0, 2)
        .reshape(128, 64))
    d["w1_my"] = np.ascontiguousarray(np.asarray(inp["L0_ffn_w1"], f32)[k])
    d["w2_my"] = np.ascontiguousarray(np.asarray(inp["L0_ffn_w2"], f32)[k])
    d["b1_col"] = np.ascontiguousarray(
        np.asarray(inp["L0_ffn_b1"], f32)[k].reshape(32, 128).T)
    d["b2_row"] = np.ascontiguousarray(
        np.asarray(inp["L0_ffn_b2"], f32)[k].reshape(1, D))
    wbd = np.zeros((12, 128, 128), f32)
    gates = [np.asarray(inp["L0_attn_b_Wf"], f32),
             np.asarray(inp["L0_attn_b_Wi"], f32),
             np.asarray(inp["L0_attn_b_Wh"], f32)]
    for gi in range(3):
        for t in range(4):
            for j in range(16):
                g = 64 * c2 + 16 * t + j
                wbd[4 * gi + t, 8 * j:8 * j + 8, 8 * j:8 * j + 8] = gates[gi][g].T
    d["wbd_my"] = np.ascontiguousarray(wbd.reshape(12 * 128, 128))
    d["consts"] = np.broadcast_to(
        np.array([1e-5, 1e-8, 0.50000001, 1.0, 0, 0, 0, 0], f32), (128, 8)).copy()
    ii = np.arange(128)
    d["tri128i_f"] = (ii[:, None] <= ii[None, :]).astype(f32)
    d["tri128i_b"] = d["tri128i_f"].astype(ml_dtypes.bfloat16)
    d["id128_f"] = np.eye(128, dtype=f32)
    d["id128_b"] = d["id128_f"].astype(ml_dtypes.bfloat16)
    r8 = np.arange(8)
    d["stri8"] = (r8[:, None] < r8[None, :]).astype(f32)
    d["tri8i"] = (r8[:, None] <= r8[None, :]).astype(f32)
    d["ones_c_f"] = np.ones((128, 1), f32)
    d["ones_c_b"] = np.ones((128, 1), ml_dtypes.bfloat16)
    d["ones_r_f"] = np.ones((1, 128), f32)
    d["ones_r_b"] = np.ones((1, 128), ml_dtypes.bfloat16)
    d["colidx8"] = np.broadcast_to(r8.astype(f32), (128, 8)).copy()
    oh = np.zeros((8, 1), f32); oh[k] = 1.0
    d["rank1h"] = oh
    d["rank2048"] = (TOK * np.arange(8, dtype=f32)).reshape(8, 1)
    d["c_iota"] = (np.arange(20)[None, :] * 128 + ii[:, None]).astype(f32)
    tt_ = np.arange(128) // 8
    ee_ = np.arange(128) % 8
    d["ones_sq"] = np.ones((128, 128), f32)
    d["tileoff_m"] = ((tt_[:, None] < tt_[None, :]) &
                      (ee_[:, None] == ee_[None, :])).astype(f32)
    d["esum_m"] = (ee_[:, None] == r8[None, :]).astype(f32)
    ch = np.arange(1024)
    ln1i = np.arange(2 * D).astype(np.int32)
    d["ln1_idx"] = np.ascontiguousarray(ln1i.reshape(16, 128).T)
    d["sc1_idx"] = np.ascontiguousarray(
        (D * h2 + np.arange(D)).astype(np.int32).reshape(8, 128).T)
    sco = np.zeros((128, 8), np.int32)
    sco[0:64, :] = 512 * h2 + 64 * np.arange(8)[None, :] + np.arange(64)[:, None]
    d["scO_idx"] = sco
    d["sch_idx"] = np.ascontiguousarray(
        (512 * c2 + np.arange(512)).astype(np.int32).reshape(4, 128).T)
    oi = (2 * ch + h2).astype(np.int32)
    d["o_idx"] = np.ascontiguousarray(oi.reshape(8, 128).T)
    l3 = np.zeros((128, 8), np.int32)
    for ct in range(4):
        for r in range(2):
            l3[:, 2 * ct + r] = 1024 * r + 512 * c2 + 128 * ct + ii
    d["ln3_idx"] = l3
    hi = (2 * ch + h2).astype(np.int32)
    d["h_idx"] = np.ascontiguousarray(hi.reshape(8, 128).T)
    return d


def kernel(**inputs):
    global _NC_CACHE
    key = DEV_STAGE
    if key not in _NC_CACHE:
        _NC_CACHE[key] = build_nc()
    nc = _NC_CACHE[key]
    in_maps = [_host_inputs(inputs, k) for k in range(NCORE)]
    res = run_bass_kernel_spmd(nc, in_maps, core_ids=list(range(NCORE)))
    out = np.empty((B, S, D), np.float32)
    for k in range(NCORE):
        s, h2 = k // 2, k % 2
        out[s, h2 * TOK:(h2 + 1) * TOK, :] = res.results[k]["out_x"].T
    return out


# version 32 (final)
# speedup vs baseline: 1.0474x; 1.0461x over previous
import numpy as np
import ml_dtypes
import contextlib
import concourse.bass as bass
import concourse.mybir as mybir
from concourse.bass_utils import run_bass_kernel_spmd

F32 = mybir.dt.float32
BF16 = mybir.dt.bfloat16
I32 = mybir.dt.int32
AF = mybir.ActivationFunctionType
ALU = mybir.AluOpType
AX = mybir.AxisListType

B, S, D = 4, 4096, 1024
T = B * S
TOK = 2048
E, DFF = 8, 4096
CAP = 2560
NCORE = 8
KT = 8
NTT = 16

DEV_STAGE = 4

ENGS = ("sync", "tensor", "vector", "scalar", "gpsimd")


class Prog:
    def __init__(self):
        self.ops = {e: [] for e in ENGS}

    def install(self, block):
        for name in ENGS:
            def body(eng, _name=name):
                for f in self.ops[_name]:
                    f(eng)
            getattr(block, name)(body)


class Sem:
    def __init__(self, handle):
        self.h = handle
        self.n = 0

    def bump(self, k=1):
        self.n += k
        return self.n


def build_nc():
    nc = bass.Bass("TRN2", target_bir_lowering=False, debug=False)

    def ein(name, shape, dt=F32):
        return nc.dram_tensor(name, shape, dt, kind="ExternalInput").ap()

    x0_my = ein("x0_my", [D, TOK])
    wqkv_my = ein("wqkv_my", [D, 1536])
    w_out_in = ein("w_out", [D, D])
    w1_my = ein("w1_my", [D, DFF])
    w2_my = ein("w2_my", [DFF, D])
    wbd_my = ein("wbd_my", [12 * 128, 128])
    SMALL = {
        "bqkv_col": ([128, 8], F32), "bqkv_vrow": ([1, 512], F32),
        "bout_col": ([128, 8], F32), "wg_t": ([128, 64], F32),
        "b1_col": ([128, 32], F32), "b2_row": ([1, D], F32),
        "consts": ([128, 8], F32), "tri128i_f": ([128, 128], F32),
        "tri128i_b": ([128, 128], BF16), "id128_b": ([128, 128], BF16),
        "id128_f": ([128, 128], F32), "stri8": ([8, 8], F32),
        "tri8i": ([8, 8], F32), "ones_c_f": ([128, 1], F32),
        "ones_c_b": ([128, 1], BF16), "ones_r_f": ([1, 128], F32),
        "ones_r_b": ([1, 128], BF16), "colidx8": ([128, 8], F32),
        "rank1h": ([8, 1], F32), "rank2048": ([8, 1], F32),
        "c_iota": ([128, 20], F32), "tileoff_m": ([128, 128], F32),
        "ones_sq": ([128, 128], F32),
        "esum_m": ([128, 8], F32),
        "ln1_idx": ([128, 16], I32), "o_idx": ([128, 8], I32),
        "ln3_idx": ([128, 8], I32), "h_idx": ([128, 8], I32),
        "sc1_idx": ([128, 8], I32), "scO_idx": ([128, 8], I32),
        "sch_idx": ([128, 4], I32),
    }
    ext_small = {nm: ein(nm, shp, dt) for nm, (shp, dt) in SMALL.items()}

    out_x = nc.dram_tensor("out_x", [D, TOK], F32, kind="ExternalOutput").ap()

    def dr(name, shape, dt=BF16, shared=False):
        if shared:
            return nc.dram_tensor(name, shape, dt, addr_space="Shared").ap()
        return nc.dram_tensor(name, shape, dt).ap()

    cc_ln1p = dr("cc_ln1p", [2 * D, TOK], shared=True)
    bar_src = nc.dram_tensor("bar_src", [1, 8], F32).ap()
    BARS = [nc.dram_tensor(f"bar{i}", [2, 8], F32).ap() for i in range(6)]
    PAIR_RG = [[0, 1], [2, 3], [4, 5], [6, 7]]
    cc_Op = dr("cc_Op", [1024, S], shared=True)
    cc_ln3p = dr("cc_ln3p", [2 * D, TOK], shared=True)
    cc_hp = dr("cc_hp", [1024, S], shared=True)
    bar_full = nc.dram_tensor("bar_full", [8, 8], F32, addr_space="Shared").ap()
    w1bf = dr("w1bf", [D, DFF])
    w2bf = dr("w2bf", [DFF, D])
    FBD = []
    for a in range(2):
        FBD.append(dict(
            cnt_src=dr(f"cc_cnt_src{a}", [1, 8], F32),
            cnt=dr(f"cc_cnt{a}", [8, 8], F32, shared=True),
            g_src=dr(f"cc_g_src{a}", [TOK, D]),
            g=dr(f"cc_g{a}", [T, D], shared=True),
            eo_src=dr(f"cc_eo_src{a}", [CAP, D]),
            eo=dr(f"cc_eo{a}", [NCORE * CAP, D], shared=True),
        ))

    prog = Prog()
    st = contextlib.ExitStack()
    with st:
        AR_COLS = 82944
        AR = st.enter_context(nc.sbuf_tensor("AR", [128, AR_COLS], BF16))

        def abf(c0, n):
            assert 0 <= c0 and c0 + n <= AR_COLS, (c0, n)
            return AR[:, c0:c0 + n]

        def af32(c0, n):
            assert c0 % 2 == 0 and c0 + 2 * n <= AR_COLS, (c0, n)
            return AR[:, c0:c0 + 2 * n].bitcast(F32)

        def t3d(ap2d):
            return ap2d.rearrange("(kt p) t -> p kt t", p=128)

        def s3d(apslice, inner):
            return apslice.rearrange("p (kt t) -> p kt t", t=inner)

        P_ = {nm: st.enter_context(nc.sbuf_tensor("sb_" + nm, shp, dt))
              for nm, (shp, dt) in SMALL.items()}
        for nm, shp, dt in [
            ("bvrow_b", [1, 512], BF16), ("b2row_b", [1, D], BF16),
            ("stats", [128, 16], F32), ("lnscr", [128, 1536], F32),
            ("mrow", [1, TOK], F32), ("rrow", [1, TOK], F32),
            ("route", [128, 176], F32), ("cnt_all", [8, 8], F32),
            ("sm8", [8, 48], F32), ("row8", [1, 160], F32),
            ("wbd", [128, 12 * 128], BF16), ("ri", [128, 64], I32),
        ]:
            P_[nm] = st.enter_context(nc.sbuf_tensor(nm, shp, dt))

        PS = [st.enter_context(nc.psum_tensor(f"ps{i}", [128, 1024], F32)) for i in range(4)]
        PSB = [p.ap().bitcast(BF16) for p in PS]

        semnames = ["sQ", "sG", "sW0", "sW1", "sCC", "sT", "sV", "sS", "sG2",
                    "sCONST", "sXA", "sXB", "sWC", "sLN1C", "sLN1G", "sOC",
                    "sP3", "sCNT", "sSCAT", "sEIN", "sEO0", "sEO1", "sCMB0",
                    "sCMB1", "sLN3C", "sL3G", "sHC", "sHG", "sOUT", "sOC1", "sOC2", "sOC3", "sW2", "sW3", "sP"]
        sems = {nm: Sem(st.enter_context(nc.semaphore(nm))) for nm in semnames}
        (sQ, sG, sW0, sW1, sCC, sT, sV, sS, sG2, sCONST, sXA, sXB, sWC, sLN1C,
         sLN1G, sOC, sP3, sCNT, sSCAT, sEIN, sEO0, sEO1, sCMB0, sCMB1, sLN3C,
         sL3G, sHC, sHG, sOUT, sOC1, sOC2, sOC3, sW2, sW3, sP) = (sems[n] for n in semnames)

        block = st.enter_context(nc.Block())

        def on(eng_name):
            def deco(fn):
                prog.ops[eng_name].append(fn)
            return deco

        def barrier():
            vals = [(s, s.n) for s in sems.values()]
            for en in ENGS:
                @on(en)
                def _(eng, vals=tuple(vals)):
                    for s, v in vals:
                        if v:
                            eng.wait_ge(s.h, v)

        CB = P_["consts"]
        RT = P_["route"]
        SR = P_["stats"]

        @on("sync")
        def _(eng):
            for nm in SMALL:
                eng.dma_start(out=P_[nm][:], in_=ext_small[nm]).then_inc(sCONST.h, 16)
            eng.wait_ge(sCONST.h, 16 * len(SMALL))
            eng.dma_start(out=bar_src, in_=P_["consts"][0:1, 0:8]).then_inc(sCONST.h, 16)
        consts_done = sCONST.bump(16 * len(SMALL))
        barsrc_done = sCONST.bump(16)

        @on("vector")
        def _(eng):
            eng.wait_ge(sCONST.h, consts_done)
            eng.tensor_copy(P_["bvrow_b"][:], P_["bqkv_vrow"][:])
            last = eng.tensor_copy(P_["b2row_b"][:], P_["b2_row"][:])
            last.then_inc(sV.h, 1)
        rowcast_done = sV.bump()

        BR = {}
        @on("gpsimd")
        def _(eng):
            for v in (TOK - 1, D - 1, NCORE * D - 1, T - 1, NCORE * CAP - 1):
                BR[v] = eng.to_reg(v)
            eng.dma_start(out=w1bf, in_=w1_my).then_inc(sWC.h, 16)
            eng.dma_start(out=w2bf, in_=w2_my).then_inc(sWC.h, 16)
            eng.dma_start(out=s3d(P_["wbd"][:], 128),
                          in_=wbd_my.rearrange("(t p) c -> p t c", p=128)).then_inc(sWC.h, 16)
        wcache_done = sWC.bump(48)
        @on("gpsimd")
        def _(eng, bd=barsrc_done):
            eng.wait_ge(sCONST.h, bd)
            eng.collective_compute("AllGather", ALU.bypass,
                                   replica_groups=[list(range(NCORE))],
                                   ins=[bar_src.opt()], outs=[bar_full.opt()]
                                   ).then_inc(sCC.h, 1)
        sync0 = sCC.bump()

        XRES0 = AR_COLS - 32768
        xres = [af32(XRES0 + 4096 * kt, TOK) for kt in range(KT)]

        def emit_layernorm(xsrc, lnout, scr0, need_rows=False):
            xbf = [abf(scr0 + 512 * kt, 512) for kt in range(KT)]
            xsq = [abf(scr0 + 4096 + 512 * kt, 512) for kt in range(KT)]
            sx_slot = [PS[2][0:1, 0:512], PS[2][0:1, 512:1024],
                       PS[3][0:1, 0:512], PS[3][0:1, 512:1024]]
            sq_slot = [PS[0][0:1, 0:512], PS[0][0:1, 512:1024],
                       PS[1][0:1, 0:512], PS[1][0:1, 512:1024]]
            MR, RR = P_["mrow"], P_["rrow"]
            mm_done = {}
            cp_done = {}
            sq_done = {}
            for c in range(4):
                c0 = 512 * c
                wmm = mm_done.get(c - 1, 0)
                @on("vector")
                def _(eng, c0=c0, wmm=wmm):
                    if wmm:
                        eng.wait_ge(sT.h, wmm)
                    for kt in range(KT):
                        last = eng.tensor_copy(xbf[kt], xsrc[kt][:, c0:c0 + 512])
                    last.then_inc(sV.h, 1)
                cp_done[c] = sV.bump()
                @on("scalar")
                def _(eng, c0=c0, wmm=wmm):
                    if wmm:
                        eng.wait_ge(sT.h, wmm)
                    for kt in range(KT):
                        last = eng.activation(xsq[kt], xsrc[kt][:, c0:c0 + 512], AF.Square)
                    last.then_inc(sS.h, 1)
                sq_done[c] = sS.bump()
                @on("tensor")
                def _(eng, c=c, vdone=cp_done[c], sdone=sq_done[c]):
                    eng.wait_ge(sV.h, vdone)
                    eng.wait_ge(sS.h, sdone)
                    for kt in range(KT):
                        eng.matmul(sx_slot[c], P_["ones_c_b"][:], xbf[kt],
                                   start=(kt == 0), stop=(kt == KT - 1))
                    for kt in range(KT):
                        last = eng.matmul(sq_slot[c], P_["ones_c_b"][:], xsq[kt],
                                   start=(kt == 0), stop=(kt == KT - 1))
                    last.then_inc(sT.h, 1)
                mm_done[c] = sT.bump()
            all_mm = mm_done[3]
            @on("vector")
            def _(eng, am=all_mm):
                eng.wait_ge(sT.h, am)
                eng.tensor_scalar_mul(MR[0:1, 0:1024], PS[2][0:1, 0:1024], 1.0 / 1024.0)
                last = eng.tensor_scalar_mul(MR[0:1, 1024:2048], PS[3][0:1, 0:1024],
                                             1.0 / 1024.0)
                last.then_inc(sV.h, 1)
            v_m = sV.bump()
            @on("vector")
            def _(eng, v_m=v_m):
                eng.wait_ge(sV.h, v_m)
                eng.tensor_mul(RR[0:1, 0:2048], MR[0:1, 0:2048], MR[0:1, 0:2048]).then_inc(sV.h, 1)
            v_m2 = sV.bump()
            @on("vector")
            def _(eng, v_m2=v_m2):
                eng.wait_ge(sV.h, v_m2)
                eng.scalar_tensor_tensor(RR[0:1, 0:1024], PS[0][0:1, 0:1024],
                                         1.0 / 1024.0, RR[0:1, 0:1024],
                                         ALU.mult, ALU.subtract)
                last = eng.scalar_tensor_tensor(RR[0:1, 1024:2048], PS[1][0:1, 0:1024],
                                         1.0 / 1024.0, RR[0:1, 1024:2048],
                                         ALU.mult, ALU.subtract)
                last.then_inc(sV.h, 1)
            v_var = sV.bump()
            @on("scalar")
            def _(eng, v_var=v_var):
                eng.wait_ge(sV.h, v_var)
                eng.activation(RR[0:1, 0:2048], RR[0:1, 0:2048], AF.Ln,
                               bias=CB[0:1, 0:1]).then_inc(sS.h, 1)
            s_ln = sS.bump()
            @on("scalar")
            def _(eng, s_ln=s_ln):
                eng.wait_ge(sS.h, s_ln)
                eng.activation(RR[0:1, 0:2048], RR[0:1, 0:2048], AF.Exp,
                               scale=-0.5).then_inc(sS.h, 1)
            s_r = sS.bump()
            norm_done = {}
            for c in range(4):
                c0 = 512 * c
                bps = PS[2] if c % 2 == 0 else PS[3]
                wn = norm_done.get(c - 2, 0)
                @on("tensor")
                def _(eng, c0=c0, bps=bps, s_r=s_r, wn=wn):
                    eng.wait_ge(sS.h, s_r)
                    if wn:
                        eng.wait_ge(sV.h, wn)
                    eng.matmul(bps[:, 0:512], P_["ones_r_f"][0:1, :],
                               MR[0:1, c0:c0 + 512], start=True, stop=True)
                    last = eng.matmul(bps[:, 512:1024], P_["ones_r_f"][0:1, :],
                               RR[0:1, c0:c0 + 512], start=True, stop=True)
                    last.then_inc(sT.h, 1)
                t_bc = sT.bump()
                for kt in range(KT):
                    @on("vector")
                    def _(eng, kt=kt, c0=c0, bps=bps, t_bc=t_bc, vprev=sV.n):
                        if kt == 0:
                            eng.wait_ge(sT.h, t_bc)
                        eng.wait_ge(sV.h, vprev)
                        eng.tensor_sub(P_["lnscr"][:, 0:512], xsrc[kt][:, c0:c0 + 512],
                                       bps[:, 0:512]).then_inc(sV.h, 1)
                    va = sV.bump()
                    @on("vector")
                    def _(eng, kt=kt, c0=c0, bps=bps, va=va):
                        eng.wait_ge(sV.h, va)
                        eng.tensor_mul(lnout[kt][:, c0:c0 + 512], P_["lnscr"][:, 0:512],
                                       bps[:, 512:1024]).then_inc(sV.h, 1)
                    sV.bump()
                norm_done[c] = sV.n
            vX = sV.n
            @on("tensor")
            def _(eng, vX=vX):
                eng.wait_ge(sV.h, vX)

        @on("sync")
        def _(eng):
            eng.dma_start(out=AR[:, XRES0:XRES0 + 32768].bitcast(F32)
                          .rearrange("p (kt t) -> p kt t", t=2048),
                          in_=x0_my.rearrange("(kt p) t -> p kt t", p=128)
                          ).then_inc(sXA.h, 16)
        x0_done = sXA.bump(16)

        ln1out = [abf(TOK * kt, TOK) for kt in range(KT)]
        for en in ("vector", "scalar", "tensor"):
            @on(en)
            def _(eng):
                eng.wait_ge(sXA.h, x0_done)
                eng.wait_ge(sCONST.h, consts_done)
        emit_layernorm(xres, ln1out, 16384)
        ln1_v = sV.n

        @on("gpsimd")
        def _(eng, lv=ln1_v):
            eng.wait_ge(sV.h, lv)
            for j in range(KT):
                eng.indirect_dma_start(
                    out=cc_ln1p,
                    out_offset=bass.IndirectOffsetOnAxis(P_["sc1_idx"][:, j:j + 1], 0),
                    in_=ln1out[j], in_offset=None,
                    bounds_check=BR[TOK - 1], oob_is_err=False,
                ).then_inc(sLN1C.h, 16)
        c1 = sLN1C.bump(16 * KT)
        @on("gpsimd")
        def _(eng, c1=c1, bd=barsrc_done):
            eng.wait_ge(sLN1C.h, c1)
            eng.wait_ge(sCONST.h, bd)
            eng.collective_compute("AllGather", ALU.bypass,
                                   replica_groups=PAIR_RG,
                                   ins=[bar_src.opt()], outs=[BARS[0].opt()]
                                   ).then_inc(sCC.h, 1)
        ag_ln1 = sCC.bump()

        ln1h = [abf(2048 * kt, 2048) for kt in range(KT)]
        wqkv_sb = abf(16384, 12288)
        qk_t = [abf(28672 + 4096 * i, 4096) for i in range(8)]
        v_sb = [abf(61440 + 520 * tt, 520) for tt in range(32)]

        @on("gpsimd")
        def _(eng):
            eng.wait_ge(sCONST.h, consts_done)
            eng.dma_start(out=s3d(wqkv_sb, 1536),
                          in_=wqkv_my.rearrange("(kt p) c -> p kt c", p=128)
                          ).then_inc(sG2.h, 16)
        wqkv_done = sG2.bump(16)

        @on("vector")
        def _(eng):
            for tt in range(32):
                vv = v_sb[tt].rearrange("p (h c) -> p h c", c=65)
                last = eng.memset(vv[:, :, 64:65], 1.0)
            last.then_inc(sV.h, 1)
        vones_done = sV.bump()

        last_evac = {0: 0, 1: 0, 2: 0}

        def emit_qkv_half(half, war_v):
            h0 = half * 2048
            @on("gpsimd")
            def _(eng, half=half, war_v=war_v, ag=ag_ln1):
                eng.wait_ge(sCC.h, ag)
                if war_v is not None:
                    eng.wait_ge(sV.h, war_v)
                for j in range(8):
                    eng.indirect_dma_start(
                        out=ln1h[j], out_offset=None, in_=cc_ln1p,
                        in_offset=bass.IndirectOffsetOnAxis(
                            P_["ln1_idx"][:, 8 * half + j:8 * half + j + 1], 0),
                        bounds_check=BR[TOK - 1], oob_is_err=False,
                    ).then_inc(sLN1G.h, 16)
            ld = sLN1G.bump(16 * 8)
            for ct in range(8):
                for fc in range(4):
                    pidx = (ct * 4 + fc) % 2
                    ps = PS[pidx]
                    lev = last_evac[pidx]
                    @on("tensor")
                    def _(eng, ct=ct, fc=fc, ps=ps, ld=ld, lev=lev, wq=wqkv_done):
                        eng.wait_ge(sLN1G.h, ld)
                        eng.wait_ge(sG2.h, wq)
                        if lev:
                            eng.wait_ge(sV.h, lev)
                        for kt in range(KT):
                            last = eng.matmul(ps[:, 0:512],
                                       wqkv_sb[:, 1536 * kt + 128 * ct:1536 * kt + 128 * ct + 128],
                                       ln1h[kt][:, 512 * fc:512 * fc + 512],
                                       start=(kt == 0), stop=(kt == KT - 1))
                        last.then_inc(sT.h, 1)
                    td = sT.bump()
                    @on("vector")
                    def _(eng, ct=ct, fc=fc, ps=ps, td=td, h0=h0):
                        eng.wait_ge(sT.h, td)
                        eng.tensor_scalar(qk_t[ct][:, h0 + 512 * fc:h0 + 512 * fc + 512],
                                          ps[:, 0:512], P_["bqkv_col"][:, ct:ct + 1],
                                          None, ALU.add).then_inc(sV.h, 1)
                    last_evac[pidx] = sV.bump()
            for tt_loc in range(16):
                tt = half * 16 + tt_loc
                lev = last_evac[2]
                @on("tensor")
                def _(eng, tt_loc=tt_loc, ld=ld, lev=lev, von=vones_done,
                      rc=rowcast_done):
                    eng.wait_ge(sV.h, max(von, rc))
                    eng.wait_ge(sLN1G.h, ld)
                    if lev:
                        eng.wait_ge(sV.h, lev)
                    last = eng.matmul(PS[2][:, 0:512], P_["ones_r_b"][0:1, :],
                               P_["bvrow_b"][0:1, :], start=True, stop=False)
                    for kt in range(KT):
                        last = eng.matmul(PS[2][:, 0:512],
                                   ln1h[kt][:, 128 * tt_loc:128 * tt_loc + 128],
                                   wqkv_sb[:, 1536 * kt + 1024:1536 * kt + 1536],
                                   start=False, stop=(kt == KT - 1))
                    last.then_inc(sT.h, 1)
                td = sT.bump()
                @on("vector")
                def _(eng, tt=tt, td=td):
                    eng.wait_ge(sT.h, td)
                    vv = v_sb[tt].rearrange("p (h c) -> p h c", c=65)
                    for hu in range(8):
                        last = eng.tensor_copy(vv[:, hu, 0:64],
                                               PS[2][:, 64 * hu:64 * hu + 64])
                    last.then_inc(sV.h, 1)
                last_evac[2] = sV.bump()
            return sV.n

        v_half0 = emit_qkv_half(0, None)
        emit_qkv_half(1, v_half0)
        barrier()

        p_sb = [abf(16384 + 1024 * i, 1024) for i in range(3)]
        o_sb = af32(20480, 1024)
        rrec = af32(24576, 1024)
        oacc = [[abf(8192 * p + 4096 * i, 4096) for i in range(2)]
                for p in range(2)]
        odma_done = {0: 0, 1: 0}

        exp_par = {0: 0, 1: 0, 2: 0}
        pv_par = {0: 0, 1: 0, 2: 0}
        PSS = [PS[0], PS[1], PS[3]]
        v_oc_last = 0

        def emit_st(hp, g, kk, qtile, ktile):
            par = kk % 3
            ps_s = PSS[par]
            q0 = 512 * g
            w_exp = exp_par[par]
            @on("tensor")
            def _(eng, kk=kk, ps_s=ps_s, qtile=qtile, ktile=ktile, q0=q0, w_exp=w_exp):
                if w_exp:
                    eng.wait_ge(sS.h, w_exp)
                eng.matmul(ps_s[:, 0:512],
                           ktile[0:64, 128 * kk:128 * kk + 128],
                           qtile[0:64, q0:q0 + 512], start=True, stop=True)
                last = eng.matmul(ps_s[:, 512:1024],
                           ktile[64:128, 128 * kk:128 * kk + 128],
                           qtile[64:128, q0:q0 + 512], start=True, stop=True)
                last.then_inc(sT.h, 1)
            return sT.bump()

        def emit_exp_mask(hp, g, kk, t_st):
            par = kk % 3
            ps_s = PSS[par]
            w_pv = pv_par[par]
            @on("scalar")
            def _(eng, par=par, ps_s=ps_s, t_st=t_st, w_pv=w_pv):
                eng.wait_ge(sT.h, t_st)
                if w_pv:
                    eng.wait_ge(sT.h, w_pv)
                eng.activation(p_sb[par], ps_s[:, 0:1024], AF.Exp,
                               scale=0.125).then_inc(sS.h, 1)
            exp_par[par] = s_exp = sS.bump()
            v_mask = None
            if kk >= 4 * g:
                j = kk - 4 * g
                @on("vector")
                def _(eng, par=par, j=j, s_exp=s_exp):
                    eng.wait_ge(sS.h, s_exp)
                    if j > 0:
                        eng.memset(p_sb[par][:, 0:128 * j], 0.0)
                        last = eng.memset(p_sb[par][:, 512:512 + 128 * j], 0.0)
                        last.then_inc(sV.h, 1)
                    else:
                        eng.engine_nop().then_inc(sV.h, 1)
                vm1 = sV.bump()
                @on("vector")
                def _(eng, par=par, j=j, vm1=vm1):
                    eng.wait_ge(sV.h, vm1)
                    eng.tensor_mul(p_sb[par][:, 128 * j:128 * j + 128],
                                   p_sb[par][:, 128 * j:128 * j + 128],
                                   P_["tri128i_b"][:])
                    last = eng.tensor_mul(p_sb[par][:, 512 + 128 * j:512 + 128 * j + 128],
                                   p_sb[par][:, 512 + 128 * j:512 + 128 * j + 128],
                                   P_["tri128i_b"][:])
                    last.then_inc(sV.h, 1)
                v_mask = sV.bump()
            return s_exp, v_mask

        def emit_pv(hp, g, kk, nk, s_exp, v_mask, first_wait=0):
            par = kk % 3
            @on("tensor")
            def _(eng, hp=hp, kk=kk, nk=nk, par=par, s_exp=s_exp, v_mask=v_mask,
                  fw=first_wait):
                eng.wait_ge(sS.h, s_exp)
                if v_mask:
                    eng.wait_ge(sV.h, v_mask)
                if fw:
                    eng.wait_ge(sV.h, fw)
                vv = v_sb[kk].rearrange("p (h c) -> p h c", c=65)
                eng.matmul(PS[2][0:65, 0:512], vv[:, 2 * hp, :], p_sb[par][:, 0:512],
                           start=(kk == 0), stop=(kk == nk - 1))
                last = eng.matmul(PS[2][0:65, 512:1024], vv[:, 2 * hp + 1, :],
                           p_sb[par][:, 512:1024],
                           start=(kk == 0), stop=(kk == nk - 1))
                last.then_inc(sT.h, 1)
            pv_par[par] = v = sT.bump()
            return v

        for hp in range(4):
            qtile = qk_t[hp]
            ktile = qk_t[4 + hp]
            for g in range(8):
                nk = 4 * g + 4
                steps = {}
                for kk in range(min(3, nk)):
                    t_st = emit_st(hp, g, kk, qtile, ktile)
                    steps[kk] = emit_exp_mask(hp, g, kk, t_st)
                t_pv = None
                pend_st = {}
                for kk in range(nk):
                    if kk + 3 < nk:
                        pend_st[kk + 3] = emit_st(hp, g, kk + 3, qtile, ktile)
                    s_exp, v_mask = steps.pop(kk)
                    t_pv = emit_pv(hp, g, kk, nk, s_exp, v_mask,
                                   first_wait=(v_oc_last if kk == 0 else 0))
                    if kk + 3 < nk:
                        steps[kk + 3] = emit_exp_mask(hp, g, kk + 3, pend_st.pop(kk + 3))
                @on("vector")
                def _(eng, t_pv=t_pv):
                    eng.wait_ge(sT.h, t_pv)
                    eng.reciprocal(rrec[64:65, 0:1024], PS[2][64:65, 0:1024]).then_inc(sV.h, 1)
                vr1 = sV.bump()
                @on("scalar")
                def _(eng, t_pv=t_pv):
                    eng.wait_ge(sT.h, t_pv)
                    eng.copy(o_sb[0:64, 0:1024], PS[2][0:64, 0:1024]).then_inc(sS.h, 1)
                s_oc = sS.bump()
                @on("tensor")
                def _(eng, vr1=vr1, s_oc=s_oc):
                    eng.wait_ge(sV.h, vr1)
                    eng.wait_ge(sS.h, s_oc)
                    eng.matmul(PS[2][0:64, 0:512], P_["ones_sq"][64:65, 0:64],
                               rrec[64:65, 0:512], start=True, stop=True)
                    last = eng.matmul(PS[2][0:64, 512:1024], P_["ones_sq"][64:65, 0:64],
                               rrec[64:65, 512:1024], start=True, stop=True)
                    last.then_inc(sT.h, 1)
                t_bc = sT.bump()
                par2 = hp % 2
                osem = sOC if par2 == 0 else sOC1
                dmaprev = odma_done[par2]
                @on("vector")
                def _(eng, t_bc=t_bc, par2=par2, g=g, dmaprev=dmaprev, osem=osem):
                    eng.wait_ge(sT.h, t_bc)
                    if g == 0 and dmaprev:
                        eng.wait_ge(osem.h, dmaprev)
                    eng.tensor_mul(oacc[par2][0][0:64, 512 * g:512 * g + 512],
                                   o_sb[0:64, 0:512],
                                   PS[2][0:64, 0:512])
                    eng.tensor_mul(oacc[par2][1][0:64, 512 * g:512 * g + 512],
                                   o_sb[0:64, 512:1024],
                                   PS[2][0:64, 512:1024]).then_inc(sV.h, 1)
                v_oc = sV.bump()
                if g == 7:
                    @on("gpsimd")
                    def _(eng, hp=hp, par2=par2, v_oc=v_oc, osem=osem):
                        eng.wait_ge(sV.h, v_oc)
                        he, ho = 2 * hp, 2 * hp + 1
                        eng.indirect_dma_start(
                            out=cc_Op,
                            out_offset=bass.IndirectOffsetOnAxis(
                                P_["scO_idx"][0:64, he:he + 1], 0),
                            in_=oacc[par2][0][0:64, :], in_offset=None,
                            bounds_check=BR[D - 1], oob_is_err=False,
                        ).then_inc(osem.h, 16)
                        eng.indirect_dma_start(
                            out=cc_Op,
                            out_offset=bass.IndirectOffsetOnAxis(
                                P_["scO_idx"][0:64, ho:ho + 1], 0),
                            in_=oacc[par2][1][0:64, :], in_offset=None,
                            bounds_check=BR[D - 1], oob_is_err=False,
                        ).then_inc(osem.h, 16)
                    odma_done[par2] = osem.bump(32)
                v_oc_last = v_oc
        oc_dones = [(sm, sm.n) for sm in (sOC, sOC1)]

        @on("gpsimd")
        def _(eng, ods=tuple(oc_dones)):
            for sm, v in ods:
                eng.wait_ge(sm.h, v)
            eng.collective_compute("AllGather", ALU.bypass,
                                   replica_groups=PAIR_RG,
                                   ins=[bar_src.opt()], outs=[BARS[1].opt()]
                                   ).then_inc(sCC.h, 1)
        ag_O = sCC.bump()
        barrier()

        attn_fm = [abf(2048 * kt, 2048) for kt in range(KT)]
        wout_sb = abf(16384, 8192)

        @on("sync")
        def _(eng):
            eng.dma_start(out=AR[:, XRES0:XRES0 + 32768].bitcast(F32)
                          .rearrange("p (kt t) -> p kt t", t=2048),
                          in_=x0_my.rearrange("(kt p) t -> p kt t", p=128)
                          ).then_inc(sXB.h, 16)
        x0_re = sXB.bump(16)
        @on("gpsimd")
        def _(eng):
            eng.dma_start(out=s3d(wout_sb, 1024),
                          in_=w_out_in.rearrange("(kt p) c -> p kt c", p=128)
                          ).then_inc(sP3.h, 16)
        wout_done = sP3.bump(16)
        ccO_v = cc_Op.rearrange("r (h t) -> (r h) t", h=2)
        @on("gpsimd")
        def _(eng, wd=wout_done):
            eng.wait_ge(sP3.h, wd)
            for j in range(8):
                eng.indirect_dma_start(
                    out=attn_fm[j], out_offset=None, in_=ccO_v,
                    in_offset=bass.IndirectOffsetOnAxis(P_["o_idx"][:, j:j + 1], 0),
                    bounds_check=BR[TOK - 1], oob_is_err=False,
                ).then_inc(sP3.h, 16)
        attn_g = sP3.bump(16 * 8)

        for oc in range(8):
            for fc in range(4):
                pidx = (oc * 4 + fc) % 2
                ps = PS[pidx]
                lev = last_evac[pidx]
                @on("tensor")
                def _(eng, oc=oc, fc=fc, ps=ps, ag=attn_g, x0r=x0_re, lev=lev):
                    eng.wait_ge(sP3.h, ag)
                    eng.wait_ge(sXB.h, x0r)
                    if lev:
                        eng.wait_ge(sV.h, lev)
                    for kt in range(KT):
                        last = eng.matmul(ps[:, 0:512],
                                   wout_sb[:, 1024 * kt + 128 * oc:1024 * kt + 128 * oc + 128],
                                   attn_fm[kt][:, 512 * fc:512 * fc + 512],
                                   start=(kt == 0), stop=(kt == KT - 1))
                    last.then_inc(sT.h, 1)
                td = sT.bump()
                @on("vector")
                def _(eng, oc=oc, fc=fc, ps=ps, td=td):
                    eng.wait_ge(sT.h, td)
                    eng.scalar_tensor_tensor(xres[oc][:, 512 * fc:512 * fc + 512],
                                             ps[:, 0:512], P_["bout_col"][:, oc:oc + 1],
                                             xres[oc][:, 512 * fc:512 * fc + 512],
                                             ALU.add, ALU.add).then_inc(sV.h, 1)
                last_evac[pidx] = sV.bump()
        barrier()

        def emit_output():
            barrier()
            @on("sync")
            def _(eng):
                eng.dma_start(out=out_x.rearrange("(kt p) t -> p kt t", p=128),
                              in_=AR[:, XRES0:XRES0 + 32768].bitcast(F32)
                              .rearrange("p (kt t) -> p kt t", t=2048)
                              ).then_inc(sOUT.h, 16)
            od = sOUT.bump(16)
            @on("sync")
            def _(eng, od=od):
                eng.wait_ge(sOUT.h, od)

        if DEV_STAGE == 1:
            emit_output()
            prog.install(block)
            return nc

        def emit_ffn(app):
            FA = FBD[app]
            lnbf = [abf(2048 * kt, 2048) for kt in range(KT)]
            lnTM = [abf(24576 + 1024 * tt, 1024) for tt in range(NTT)]
            lnf32 = [af32(40960 + 1024 * kt, 512) for kt in range(KT)]
            logitsTM = af32(16384, 128)
            ohall = af32(16640, 128)
            cums = af32(16896, 128)
            selscr = af32(17152, 128)
            emit_layernorm(xres, lnbf, 18432, need_rows=True)
            barrier()
            tp_evac = 0
            for kt in range(KT):
                for tt in range(NTT):
                    par = (kt * NTT + tt) % 2
                    @on("tensor")
                    def _(eng, kt=kt, tt=tt, par=par, tp_evac=tp_evac):
                        if tp_evac:
                            eng.wait_ge(sV.h, tp_evac)
                        last = eng.transpose(PSB[par][:, 0:128],
                                      lnbf[kt][:, 128 * tt:128 * tt + 128],
                                      P_["id128_b"][:])
                        last.then_inc(sT.h, 1)
                    t_tp = sT.bump()
                    @on("vector")
                    def _(eng, kt=kt, tt=tt, par=par, t_tp=t_tp):
                        eng.wait_ge(sT.h, t_tp)
                        eng.tensor_copy(lnTM[tt][:, 128 * kt:128 * kt + 128],
                                        PSB[par][:, 0:128]).then_inc(sV.h, 1)
                    newv = sV.bump()
                    if par == 1:
                        tp_evac = newv
            lnbf_free = sT.n

            for c in range(4):
                c0 = 512 * c
                @on("tensor")
                def _(eng, c0=c0):
                    last = eng.matmul(PS[3][:, 0:512], P_["ones_r_f"][0:1, :],
                               P_["mrow"][0:1, c0:c0 + 512], start=True, stop=True)
                    last = eng.matmul(PS[3][:, 512:1024], P_["ones_r_f"][0:1, :],
                               P_["rrow"][0:1, c0:c0 + 512], start=True, stop=True)
                    last.then_inc(sT.h, 1)
                tb = sT.bump()
                for kt in range(KT):
                    @on("vector")
                    def _(eng, kt=kt, c0=c0, tb=tb, vprev=sV.n):
                        if kt == 0:
                            eng.wait_ge(sT.h, tb)
                        eng.wait_ge(sV.h, vprev)
                        eng.tensor_sub(P_["lnscr"][:, 0:512], xres[kt][:, c0:c0 + 512],
                                       PS[3][:, 0:512]).then_inc(sV.h, 1)
                    va = sV.bump()
                    @on("vector")
                    def _(eng, kt=kt, va=va):
                        eng.wait_ge(sV.h, va)
                        eng.tensor_mul(lnf32[kt], P_["lnscr"][:, 0:512],
                                       PS[3][:, 512:1024]).then_inc(sV.h, 1)
                    sV.bump()
                vls = sV.n
                @on("tensor")
                def _(eng, vls=vls):
                    eng.wait_ge(sV.h, vls)
                    for kt in range(KT):
                        last = eng.matmul(PS[2][0:8, 0:512], P_["wg_t"][:, 8 * kt:8 * kt + 8],
                                   lnf32[kt], start=(kt == 0), stop=(kt == KT - 1))
                    last.then_inc(sT.h, 1)
                t_log = sT.bump()
                @on("vector")
                def _(eng, t_log=t_log):
                    eng.wait_ge(sT.h, t_log)
                    eng.tensor_copy(P_["lnscr"][0:8, 0:512], PS[2][0:8, 0:512]).then_inc(sV.h, 1)
                v_log = sV.bump()
                for tl in range(4):
                    tt = 4 * c + tl
                    @on("tensor")
                    def _(eng, tl=tl, v_log=v_log, vprev=sV.n):
                        eng.wait_ge(sV.h, v_log)
                        eng.wait_ge(sV.h, vprev)
                        last = eng.transpose(PS[2][:, 512:520],
                                      P_["lnscr"][0:8, 128 * tl:128 * tl + 128],
                                      P_["id128_f"][0:8, 0:8])
                        last.then_inc(sT.h, 1)
                    t_tp = sT.bump()
                    @on("vector")
                    def _(eng, tt=tt, t_tp=t_tp):
                        eng.wait_ge(sT.h, t_tp)
                        eng.tensor_copy(logitsTM[:, 8 * tt:8 * tt + 8],
                                        PS[2][:, 512:520]).then_inc(sV.h, 1)
                    sV.bump()
            logits_v = sV.n

            lv3 = logitsTM.rearrange("p (t e) -> p t e", e=8)
            @on("vector")
            def _(eng, lv=logits_v):
                eng.wait_ge(sV.h, lv)
                eng.reduce_max(RT[:, 0:16], lv3, AX.X).then_inc(sV.h, 1)
            v_m = sV.bump()
            @on("vector")
            def _(eng, v_m=v_m):
                eng.wait_ge(sV.h, v_m)
                for tt in range(NTT):
                    last = eng.tensor_scalar(ohall[:, 8 * tt:8 * tt + 8],
                                             logitsTM[:, 8 * tt:8 * tt + 8],
                                             RT[:, tt:tt + 1], None, ALU.is_ge)
                last.then_inc(sV.h, 1)
            v_oh = sV.bump()
            @on("scalar")
            def _(eng, lv=logits_v, v_m=v_m):
                eng.wait_ge(sV.h, v_m)
                eng.activation(selscr[:, 0:128], logitsTM[:, 0:128], AF.Exp).then_inc(sS.h, 1)
            s_el = sS.bump()
            @on("scalar")
            def _(eng, v_m=v_m):
                eng.wait_ge(sV.h, v_m)
                eng.activation(RT[:, 16:32], RT[:, 0:16], AF.Exp).then_inc(sS.h, 1)
            s_em = sS.bump()
            @on("vector")
            def _(eng, s_el=s_el):
                eng.wait_ge(sS.h, s_el)
                eng.reduce_sum(RT[:, 32:48], selscr.rearrange("p (t e) -> p t e", e=8),
                               AX.X).then_inc(sV.h, 1)
            v_se = sV.bump()
            @on("vector")
            def _(eng, v_se=v_se):
                eng.wait_ge(sV.h, v_se)
                eng.reciprocal(RT[:, 48:64], RT[:, 32:48]).then_inc(sV.h, 1)
            v_rs = sV.bump()
            @on("vector")
            def _(eng, v_rs=v_rs, s_em=s_em):
                eng.wait_ge(sV.h, v_rs)
                eng.wait_ge(sS.h, s_em)
                eng.tensor_mul(RT[:, 64:80], RT[:, 16:32], RT[:, 48:64]).then_inc(sV.h, 1)
            v_g1 = sV.bump()

            @on("tensor")
            def _(eng, v_oh=v_oh):
                eng.wait_ge(sV.h, v_oh)
                for tt in range(NTT):
                    last = eng.matmul(PS[2][:, 512 + 8 * tt:512 + 8 * tt + 8],
                               P_["tri128i_f"][:], ohall[:, 8 * tt:8 * tt + 8],
                               start=True, stop=True)
                last = eng.matmul(PS[3][0:1, 0:128], P_["ones_c_f"][:], ohall[:, 0:128],
                           start=True, stop=True)
                last.then_inc(sT.h, 1)
            t_cs = sT.bump()
            @on("vector")
            def _(eng, t_cs=t_cs):
                eng.wait_ge(sT.h, t_cs)
                eng.tensor_copy(cums[:, 0:128], PS[2][:, 512:640]).then_inc(sV.h, 1)
            v_cums = sV.bump()
            @on("vector")
            def _(eng, t_cs=t_cs):
                eng.wait_ge(sT.h, t_cs)
                eng.tensor_copy(P_["row8"][0:1, 0:128], PS[3][0:1, 0:128]).then_inc(sV.h, 1)
            v_ts = sV.bump()
            @on("tensor")
            def _(eng, v_ts=v_ts):
                eng.wait_ge(sV.h, v_ts)
                last = eng.matmul(PS[3][:, 0:1], P_["row8"][0:1, 0:128], CB[0:1, 3:4],
                           start=True, stop=True)
                last.then_inc(sT.h, 1)
            t_tc = sT.bump()
            @on("vector")
            def _(eng, t_tc=t_tc):
                eng.wait_ge(sT.h, t_tc)
                eng.tensor_copy(SR[:, 0:1], PS[3][:, 0:1]).then_inc(sV.h, 1)
            v_tc = sV.bump()
            @on("tensor")
            def _(eng, v_tc=v_tc):
                eng.wait_ge(sV.h, v_tc)
                last = eng.matmul(PS[3][:, 4:5], P_["tileoff_m"][:], SR[:, 0:1],
                           start=True, stop=True)
                last = eng.matmul(PS[3][0:1, 8:16], SR[:, 0:1], P_["esum_m"][:],
                           start=True, stop=True)
                last.then_inc(sT.h, 1)
            t_off = sT.bump()
            @on("vector")
            def _(eng, t_off=t_off):
                eng.wait_ge(sT.h, t_off)
                eng.tensor_copy(SR[:, 1:2], PS[3][:, 4:5])
                last = eng.tensor_copy(P_["row8"][0:1, 128:136], PS[3][0:1, 8:16])
                last.then_inc(sV.h, 1)
            v_off = sV.bump()
            @on("tensor")
            def _(eng, v_off=v_off):
                eng.wait_ge(sV.h, v_off)
                last = eng.matmul(PS[3][0:1, 16:144], SR[:, 1:2], P_["id128_f"][:],
                           start=True, stop=True)
                last.then_inc(sT.h, 1)
            t_or = sT.bump()
            @on("vector")
            def _(eng, t_or=t_or):
                eng.wait_ge(sT.h, t_or)
                eng.tensor_copy(P_["row8"][0:1, 0:128], PS[3][0:1, 16:144]).then_inc(sV.h, 1)
            v_or = sV.bump()
            @on("tensor")
            def _(eng, v_or=v_or):
                eng.wait_ge(sV.h, v_or)
                last = eng.matmul(PS[3][:, 512:640], P_["ones_r_f"][0:1, :],
                           P_["row8"][0:1, 0:128], start=True, stop=True)
                last.then_inc(sT.h, 1)
            t_ob = sT.bump()
            @on("vector")
            def _(eng, v_cums=v_cums, v_oh=v_oh):
                eng.wait_ge(sV.h, v_cums)
                eng.tensor_mul(selscr[:, 0:128], ohall[:, 0:128], cums[:, 0:128]).then_inc(sV.h, 1)
            v1 = sV.bump()
            @on("vector")
            def _(eng, v1=v1):
                eng.wait_ge(sV.h, v1)
                eng.reduce_sum(RT[:, 80:96], selscr.rearrange("p (t e) -> p t e", e=8),
                               AX.X).then_inc(sV.h, 1)
            v2 = sV.bump()
            @on("vector")
            def _(eng, t_ob=t_ob, vprev=sV.n):
                eng.wait_ge(sT.h, t_ob)
                eng.wait_ge(sV.h, vprev)
                eng.tensor_mul(selscr[:, 0:128], ohall[:, 0:128],
                               PS[3][:, 512:640]).then_inc(sV.h, 1)
            v3 = sV.bump()
            @on("vector")
            def _(eng, v3=v3):
                eng.wait_ge(sV.h, v3)
                eng.reduce_sum(RT[:, 96:112], selscr.rearrange("p (t e) -> p t e", e=8),
                               AX.X).then_inc(sV.h, 1)
            v4 = sV.bump()
            @on("vector")
            def _(eng, v2=v2, v4=v4):
                eng.wait_ge(sV.h, max(v2, v4))
                eng.tensor_add(RT[:, 112:128], RT[:, 80:96], RT[:, 96:112]).then_inc(sV.h, 1)
            v5 = sV.bump()
            @on("vector")
            def _(eng, v5=v5):
                eng.wait_ge(sV.h, v5)
                eng.tensor_scalar_add(RT[:, 112:128], RT[:, 112:128], -1.0).then_inc(sV.h, 1)
            v_posmy = sV.bump()
            @on("vector")
            def _(eng, v_oh=v_oh, vprev=sV.n):
                eng.wait_ge(sV.h, vprev)
                for tt in range(NTT):
                    last = eng.tensor_mul(selscr[:, 8 * tt:8 * tt + 8],
                                          ohall[:, 8 * tt:8 * tt + 8], P_["colidx8"][:])
                last.then_inc(sV.h, 1)
            v6 = sV.bump()
            @on("vector")
            def _(eng, v6=v6):
                eng.wait_ge(sV.h, v6)
                eng.reduce_sum(RT[:, 128:144], selscr.rearrange("p (t e) -> p t e", e=8),
                               AX.X).then_inc(sV.h, 1)
            v_e = sV.bump()

            @on("gpsimd")
            def _(eng, v_off=v_off):
                eng.wait_ge(sV.h, v_off)
                eng.dma_start(out=FA["cnt_src"], in_=P_["row8"][0:1, 128:136]).then_inc(sCNT.h, 16)
            g1_ = sCNT.bump(16)
            @on("gpsimd")
            def _(eng, g1_=g1_):
                eng.wait_ge(sCNT.h, g1_)
                eng.collective_compute("AllGather", ALU.bypass,
                                       replica_groups=[list(range(NCORE))],
                                       ins=[FA["cnt_src"].opt()], outs=[FA["cnt"].opt()]
                                       ).then_inc(sCC.h, 1)
            ag_cnt = sCC.bump()
            @on("gpsimd")
            def _(eng, ag=ag_cnt):
                eng.wait_ge(sCC.h, ag)
                eng.dma_start(out=P_["cnt_all"][:], in_=FA["cnt"]).then_inc(sCNT.h, 16)
            cnt_ld = sCNT.bump(16)

            SM = P_["sm8"]
            @on("tensor")
            def _(eng, v_off=v_off, vprev=sV.n):
                eng.wait_ge(sV.h, vprev)
                last = eng.matmul(PS[3][0:8, 24:25], P_["row8"][0:1, 128:136],
                                  CB[0:1, 3:4], start=True, stop=True)
                last.then_inc(sT.h, 1)
            t_l1 = sT.bump()
            @on("vector")
            def _(eng, t_l1=t_l1):
                eng.wait_ge(sT.h, t_l1)
                eng.tensor_copy(SM[:, 34:35], PS[3][0:8, 24:25]).then_inc(sV.h, 1)
            v_l1 = sV.bump()
            @on("tensor")
            def _(eng, v_l1=v_l1):
                eng.wait_ge(sV.h, v_l1)
                last = eng.matmul(PS[3][0:1, 26:34], SM[:, 34:35], P_["stri8"][:],
                                  start=True, stop=True)
                last.then_inc(sT.h, 1)
            t_l2 = sT.bump()
            @on("vector")
            def _(eng, t_l2=t_l2):
                eng.wait_ge(sT.h, t_l2)
                eng.tensor_copy(P_["row8"][0:1, 144:152], PS[3][0:1, 26:34]).then_inc(sV.h, 1)
            v_l2 = sV.bump()
            @on("tensor")
            def _(eng, v_l2=v_l2, vprev=sV.n):
                eng.wait_ge(sV.h, v_l2)
                eng.wait_ge(sV.h, vprev)
                last = eng.matmul(PS[2][:, 664:672], P_["ones_r_f"][0:1, :],
                                  P_["row8"][0:1, 144:152], start=True, stop=True)
                last.then_inc(sT.h, 1)
            t_rb = sT.bump()
            @on("vector")
            def _(eng, t_rb=t_rb):
                eng.wait_ge(sT.h, t_rb)
                eng.tensor_copy(RT[:, 144:152], PS[2][:, 664:672]).then_inc(sV.h, 1)
            v_rb = sV.bump()
            @on("vector")
            def _(eng, v_rb=v_rb, vprev=sV.n):
                eng.wait_ge(sV.h, max(v_rb, vprev))
                for tt in range(NTT):
                    last = eng.tensor_mul(selscr[:, 8 * tt:8 * tt + 8],
                                          ohall[:, 8 * tt:8 * tt + 8], RT[:, 144:152])
                last.then_inc(sV.h, 1)
            vr1_ = sV.bump()
            @on("vector")
            def _(eng, vr1_=vr1_):
                eng.wait_ge(sV.h, vr1_)
                eng.reduce_sum(RT[:, 32:48], selscr.rearrange("p (t e) -> p t e", e=8),
                               AX.X).then_inc(sV.h, 1)
            vr2_ = sV.bump()
            @on("vector")
            def _(eng, vr2_=vr2_):
                eng.wait_ge(sV.h, vr2_)
                eng.tensor_add(RT[:, 32:48], RT[:, 32:48], RT[:, 112:128]).then_inc(sV.h, 1)
            v_dl = sV.bump()
            @on("vector")
            def _(eng, v_dl=v_dl):
                eng.wait_ge(sV.h, v_dl)
                eng.tensor_copy(P_["ri"][:, 0:16], RT[:, 32:48]).then_inc(sV.h, 1)
            v_dli = sV.bump()

            tm_done = sV.n
            @on("gpsimd")
            def _(eng, tm=tm_done, v_dli=v_dli):
                eng.wait_ge(sV.h, max(tm, v_dli))
                for tt in range(NTT):
                    eng.indirect_dma_start(
                        out=FA["g_src"],
                        out_offset=bass.IndirectOffsetOnAxis(P_["ri"][:, tt:tt + 1], 0),
                        in_=lnTM[tt], in_offset=None,
                        bounds_check=BR[TOK - 1], oob_is_err=False,
                    ).then_inc(sSCAT.h, 16)
            scat_done = sSCAT.bump(16 * NTT)
            @on("gpsimd")
            def _(eng, sd=scat_done):
                eng.wait_ge(sSCAT.h, sd)
                eng.collective_compute("AllGather", ALU.bypass,
                                       replica_groups=[list(range(NCORE))],
                                       ins=[FA["g_src"].opt()], outs=[FA["g"].opt()]
                                       ).then_inc(sCC.h, 1)
            ag_g = sCC.bump()

            @on("tensor")
            def _(eng, cl=cnt_ld, vprev=sV.n):
                eng.wait_ge(sCNT.h, cl)
                eng.wait_ge(sV.h, vprev)
                last = eng.matmul(PS[3][0:8, 0:8], P_["stri8"][:], P_["cnt_all"][:],
                           start=True, stop=True)
                last = eng.matmul(PS[3][0:8, 8:16], P_["cnt_all"][:], P_["stri8"][:],
                           start=True, stop=True)
                last = eng.matmul(PS[3][0:8, 16:24], P_["cnt_all"][:], P_["tri8i"][:],
                           start=True, stop=True)
                last = eng.transpose(PS[3][0:8, 32:40], P_["cnt_all"][:], P_["id128_f"][0:8, 0:8])
                last.then_inc(sT.h, 1)
            t_p1 = sT.bump()
            @on("vector")
            def _(eng, t_p1=t_p1):
                eng.wait_ge(sT.h, t_p1)
                eng.tensor_copy(SM[:, 0:8], PS[3][0:8, 0:8])
                eng.tensor_copy(SM[:, 8:16], PS[3][0:8, 8:16])
                eng.tensor_copy(SM[:, 16:24], PS[3][0:8, 16:24])
                last = eng.tensor_copy(SM[:, 24:32], PS[3][0:8, 32:40])
                last.then_inc(sV.h, 1)
            v_p1 = sV.bump()
            @on("tensor")
            def _(eng, v_p1=v_p1):
                eng.wait_ge(sV.h, v_p1)
                last = eng.matmul(PS[3][0:1, 0:8], P_["rank1h"][:], SM[:, 0:8],
                           start=True, stop=True)
                last = eng.matmul(PS[3][0:8, 8:9], SM[:, 8:16], P_["rank1h"][:],
                           start=True, stop=True)
                last = eng.matmul(PS[3][0:8, 9:10], SM[:, 16:24], P_["rank1h"][:],
                           start=True, stop=True)
                last = eng.matmul(PS[3][0:8, 16:24], P_["stri8"][:], SM[:, 24:32],
                           start=True, stop=True)
                last.then_inc(sT.h, 1)
            t_p2 = sT.bump()
            @on("vector")
            def _(eng, t_p2=t_p2):
                eng.wait_ge(sT.h, t_p2)
                eng.tensor_copy(P_["row8"][0:1, 136:144], PS[3][0:1, 0:8])
                eng.tensor_copy(SM[:, 32:33], PS[3][0:8, 8:9])
                eng.tensor_copy(SM[:, 33:34], PS[3][0:8, 9:10])
                last = eng.tensor_copy(SM[:, 16:24], PS[3][0:8, 16:24])
                last.then_inc(sV.h, 1)
            v_p2 = sV.bump()
            @on("tensor")
            def _(eng, v_p2=v_p2):
                eng.wait_ge(sV.h, v_p2)
                last = eng.matmul(PS[3][0:8, 25:26], SM[:, 16:24], P_["rank1h"][:],
                           start=True, stop=True)
                last = eng.matmul(PS[3][:, 40:48], P_["ones_r_f"][0:1, :],
                           P_["row8"][0:1, 136:144], start=True, stop=True)
                last.then_inc(sT.h, 1)
            t_p3 = sT.bump()
            @on("vector")
            def _(eng, t_p3=t_p3):
                eng.wait_ge(sT.h, t_p3)
                eng.tensor_copy(SM[:, 35:36], PS[3][0:8, 25:26])
                last = eng.tensor_copy(RT[:, 144:152], PS[3][:, 40:48])
                last.then_inc(sV.h, 1)
            v_p3 = sV.bump()
            @on("vector")
            def _(eng, v_p3=v_p3):
                eng.wait_ge(sV.h, v_p3)
                eng.tensor_add(SM[:, 36:37], P_["rank2048"][:], SM[:, 35:36]).then_inc(sV.h, 1)
            vb1 = sV.bump()
            @on("vector")
            def _(eng, vb1=vb1):
                eng.wait_ge(sV.h, vb1)
                eng.tensor_sub(SM[:, 36:37], SM[:, 36:37], SM[:, 32:33]).then_inc(sV.h, 1)
            vb2 = sV.bump()
            @on("tensor")
            def _(eng, vb2=vb2):
                eng.wait_ge(sV.h, vb2)
                last = eng.matmul(PS[3][0:1, 48:56], SM[:, 32:33], P_["id128_f"][0:8, 0:8],
                           start=True, stop=True)
                last = eng.matmul(PS[3][0:1, 56:64], SM[:, 33:34], P_["id128_f"][0:8, 0:8],
                           start=True, stop=True)
                last = eng.matmul(PS[3][0:1, 64:72], SM[:, 36:37], P_["id128_f"][0:8, 0:8],
                           start=True, stop=True)
                last.then_inc(sT.h, 1)
            t_p4 = sT.bump()
            @on("vector")
            def _(eng, t_p4=t_p4):
                eng.wait_ge(sT.h, t_p4)
                eng.tensor_copy(P_["row8"][0:1, 152:160], PS[3][0:1, 64:72])
                eng.tensor_copy(SR[0:1, 2:10], PS[3][0:1, 48:56])
                last = eng.tensor_copy(SR[0:1, 10:11], PS[3][0:1, 56:57])
                last.then_inc(sV.h, 1)
            v_p4 = sV.bump()
            @on("tensor")
            def _(eng, v_p4=v_p4):
                eng.wait_ge(sV.h, v_p4)
                last = eng.matmul(PS[2][:, 640:648], P_["ones_r_f"][0:1, :], SR[0:1, 2:10],
                           start=True, stop=True)
                last.then_inc(sT.h, 1)
            t_x1 = sT.bump()
            @on("vector")
            def _(eng, t_p4=t_p4, t_x1=t_x1):
                eng.wait_ge(sT.h, t_p4)
                eng.wait_ge(sT.h, t_x1)
                eng.tensor_copy(SR[0:1, 2:10], PS[3][0:1, 56:64]).then_inc(sV.h, 1)
            v_x1 = sV.bump()
            @on("tensor")
            def _(eng, v_x1=v_x1, t_x1=t_x1):
                eng.wait_ge(sV.h, v_x1)
                last = eng.matmul(PS[2][:, 648:656], P_["ones_r_f"][0:1, :], SR[0:1, 2:10],
                           start=True, stop=True)
                last = eng.matmul(PS[2][:, 656:664], P_["ones_r_f"][0:1, :],
                           P_["row8"][0:1, 152:160], start=True, stop=True)
                last.then_inc(sT.h, 1)
            t_x2 = sT.bump()
            @on("vector")
            def _(eng, t_x1=t_x1, t_x2=t_x2):
                eng.wait_ge(sT.h, max(t_x1, t_x2))
                eng.tensor_copy(RT[:, 152:160], PS[2][:, 640:648])
                eng.tensor_copy(RT[:, 160:168], PS[2][:, 648:656])
                last = eng.tensor_copy(RT[:, 168:176], PS[2][:, 656:664])
                last.then_inc(sV.h, 1)
            v_bc = sV.bump()

            @on("vector")
            def _(eng, v_posmy=v_posmy, vprev=sV.n):
                eng.wait_ge(sV.h, vprev)
                for tt in range(NTT):
                    last = eng.tensor_mul(selscr[:, 8 * tt:8 * tt + 8],
                                          ohall[:, 8 * tt:8 * tt + 8], RT[:, 144:152])
                last.then_inc(sV.h, 1)
            vq1 = sV.bump()
            @on("vector")
            def _(eng, vq1=vq1):
                eng.wait_ge(sV.h, vq1)
                eng.reduce_sum(RT[:, 0:16], selscr.rearrange("p (t e) -> p t e", e=8),
                               AX.X).then_inc(sV.h, 1)
            vq2 = sV.bump()
            @on("vector")
            def _(eng, vq2=vq2):
                eng.wait_ge(sV.h, vq2)
                eng.tensor_add(RT[:, 0:16], RT[:, 0:16], RT[:, 112:128]).then_inc(sV.h, 1)
            v_posg = sV.bump()
            @on("vector")
            def _(eng, v_posg=v_posg):
                eng.wait_ge(sV.h, v_posg)
                eng.tensor_single_scalar(RT[:, 16:32], RT[:, 0:16], float(CAP),
                                         ALU.is_lt).then_inc(sV.h, 1)
            v_keep = sV.bump()
            @on("vector")
            def _(eng, v_keep=v_keep, v_g1=v_g1):
                eng.wait_ge(sV.h, v_keep)
                eng.tensor_mul(RT[:, 64:80], RT[:, 64:80], RT[:, 16:32]).then_inc(sV.h, 1)
            v_g1e = sV.bump()
            @on("vector")
            def _(eng, v_e=v_e, v_posg=v_posg):
                eng.wait_ge(sV.h, v_e)
                eng.tensor_scalar(RT[:, 48:64], RT[:, 128:144], float(CAP), None,
                                  ALU.mult).then_inc(sV.h, 1)
            vc1 = sV.bump()
            @on("vector")
            def _(eng, vc1=vc1):
                eng.wait_ge(sV.h, vc1)
                eng.tensor_add(RT[:, 48:64], RT[:, 48:64], RT[:, 0:16]).then_inc(sV.h, 1)
            vc2 = sV.bump()
            @on("vector")
            def _(eng, vc2=vc2):
                eng.wait_ge(sV.h, vc2)
                eng.tensor_mul(RT[:, 48:64], RT[:, 48:64], RT[:, 16:32]).then_inc(sV.h, 1)
            vc3 = sV.bump()
            @on("vector")
            def _(eng, vc3=vc3):
                eng.wait_ge(sV.h, vc3)
                eng.scalar_tensor_tensor(RT[:, 48:64], RT[:, 16:32],
                                         -float(NCORE * CAP), RT[:, 48:64],
                                         ALU.mult, ALU.add).then_inc(sV.h, 1)
            vc4 = sV.bump()
            @on("vector")
            def _(eng, vc4=vc4):
                eng.wait_ge(sV.h, vc4)
                eng.tensor_scalar_add(RT[:, 48:64], RT[:, 48:64],
                                      float(NCORE * CAP)).then_inc(sV.h, 1)
            vc5 = sV.bump()
            @on("vector")
            def _(eng, vc5=vc5):
                eng.wait_ge(sV.h, vc5)
                eng.tensor_copy(P_["ri"][:, 16:32], RT[:, 48:64]).then_inc(sV.h, 1)
            v_ci = sV.bump()
            @on("vector")
            def _(eng, v_bc=v_bc):
                eng.wait_ge(sV.h, v_bc)
                eng.memset(RT[:, 96:116], 0.0)
                last = eng.memset(RT[:, 116:136], 0.0)
                last.then_inc(sV.h, 1)
            vi0 = sV.bump()
            prev = vi0

            def vstep(fn, waits=()):
                nonlocal_prev = vchain_state
                @on("vector")
                def _(eng, fn=fn, prev=nonlocal_prev["prev"], waits=tuple(waits)):
                    eng.wait_ge(sV.h, prev)
                    for s_, v_ in waits:
                        eng.wait_ge(s_.h, v_)
                    fn(eng).then_inc(sV.h, 1)
                vchain_state["prev"] = sV.bump()

            vchain_state = {"prev": vi0}
            for r in range(8):
                vstep(lambda e, r=r: e.tensor_scalar(selscr[:, 0:20], P_["c_iota"][:],
                                                     RT[:, 152 + r:153 + r], None, ALU.is_ge))
                vstep(lambda e, r=r: e.tensor_scalar(selscr[:, 20:40], P_["c_iota"][:],
                                                     RT[:, 160 + r:161 + r], None, ALU.is_lt))
                vstep(lambda e: e.tensor_mul(selscr[:, 0:20], selscr[:, 0:20], selscr[:, 20:40]))
                vstep(lambda e, r=r: e.tensor_scalar(selscr[:, 20:40], selscr[:, 0:20],
                                                     RT[:, 168 + r:169 + r], None, ALU.mult))
                vstep(lambda e: e.tensor_add(RT[:, 96:116], RT[:, 96:116], selscr[:, 20:40]))
                vstep(lambda e: e.tensor_mul(selscr[:, 20:40], selscr[:, 0:20], P_["c_iota"][:]))
                vstep(lambda e: e.tensor_add(RT[:, 96:116], RT[:, 96:116], selscr[:, 20:40]))
                vstep(lambda e: e.tensor_add(RT[:, 116:136], RT[:, 116:136], selscr[:, 0:20]))
            prev = vchain_state["prev"]
            @on("vector")
            def _(eng, prev=prev):
                eng.wait_ge(sV.h, prev)
                eng.scalar_tensor_tensor(RT[:, 96:116], RT[:, 116:136], -float(T),
                                         RT[:, 96:116], ALU.mult, ALU.add).then_inc(sV.h, 1)
            vi1 = sV.bump()
            @on("vector")
            def _(eng, vi1=vi1):
                eng.wait_ge(sV.h, vi1)
                eng.tensor_scalar_add(RT[:, 96:116], RT[:, 96:116], float(T)).then_inc(sV.h, 1)
            vi2 = sV.bump()
            @on("vector")
            def _(eng, vi2=vi2):
                eng.wait_ge(sV.h, vi2)
                eng.tensor_copy(P_["ri"][:, 32:52], RT[:, 96:116]).then_inc(sV.h, 1)
            v_ei = sV.bump()


            einTM = [abf(16384 + 1024 * j, 1024) for j in range(4)]
            einFM = [abf(20480 + 512 * kt, 512) for kt in range(KT)]
            hbuf = [abf(24576 + 512 * d, 512) for d in range(32)]
            w1s = [abf(8192 * i, 8192) for i in range(2)]
            w2s = [abf(41984 + 1024 * i, 1024) for i in range(8)]
            eo_sb = [abf(20480 + 1024 * i, 1024) for i in range(2)]
            if app == 0:
                @on("vector")
                def _(eng):
                    for j in range(4):
                        last = eng.memset(einTM[j], 0.0)
                    last.then_inc(sV.h, 1)
                sV.bump()
            ein_ready = sV.n

            w1_lastmm = {}
            w2_lastmm = {}
            for c5 in range(5):
                @on("gpsimd")
                def _(eng, c5=c5, ag=ag_g, v_ei=v_ei, vprev=sV.n):
                    eng.wait_ge(sCC.h, ag)
                    eng.wait_ge(sV.h, vprev)
                    for j in range(4):
                        eng.indirect_dma_start(
                            out=einTM[j], out_offset=None, in_=FA["g"],
                            in_offset=bass.IndirectOffsetOnAxis(
                                P_["ri"][:, 32 + 4 * c5 + j:33 + 4 * c5 + j], 0),
                            bounds_check=BR[T - 1], oob_is_err=False,
                        ).then_inc(sEIN.h, 16)
                g_ein = sEIN.bump(64)
                tpv = 0
                s_prev_eo = sS.n
                eo_prev = (sEO0.n, sEO1.n)
                for j in range(4):
                    for kt in range(KT):
                        par = (j * KT + kt) % 2
                        @on("tensor")
                        def _(eng, j=j, kt=kt, par=par, g_ein=g_ein, tpv=tpv,
                              spe=s_prev_eo, eop=eo_prev):
                            eng.wait_ge(sEIN.h, g_ein)
                            eng.wait_ge(sS.h, spe)
                            if eop[0]:
                                eng.wait_ge(sEO0.h, eop[0])
                            if eop[1]:
                                eng.wait_ge(sEO1.h, eop[1])
                            if tpv:
                                eng.wait_ge(sV.h, tpv)
                            last = eng.transpose(PSB[par][:, 0:128],
                                          einTM[j][:, 128 * kt:128 * kt + 128],
                                          P_["id128_b"][:])
                            last.then_inc(sT.h, 1)
                        t_tp = sT.bump()
                        @on("vector")
                        def _(eng, j=j, kt=kt, par=par, t_tp=t_tp):
                            eng.wait_ge(sT.h, t_tp)
                            eng.tensor_copy(einFM[kt][:, 128 * j:128 * j + 128],
                                            PSB[par][:, 0:128]).then_inc(sV.h, 1)
                        newv = sV.bump()
                        if par == 1:
                            tpv = newv
                ein_fm_v = sV.n
                hv = []
                for q in range(4):
                    wsem = sW0 if q % 2 == 0 else sW1
                    @on("sync")
                    def _(eng, q=q, c5=c5, wsem=wsem, wc=wcache_done,
                          tprev=max(w1_lastmm.get((c5 * 4 + q) % 2, 0), lnbf_free)):
                        eng.wait_ge(sWC.h, wc)
                        if tprev:
                            eng.wait_ge(sT.h, tprev)
                        eng.dma_start(out=s3d(w1s[q % 2], 1024),
                                      in_=w1bf[:, 1024 * q:1024 * q + 1024]
                                      .rearrange("(kt p) c -> p kt c", p=128)
                                      ).then_inc(wsem.h, 16)
                    wsem.bump(16)
                    wtarget = wsem.n
                    for dd in range(8):
                        dt_ = 8 * q + dd
                        ps_h = PS[2] if dt_ % 2 == 0 else PS[3]
                        @on("tensor")
                        def _(eng, q=q, dd=dd, ps_h=ps_h, wsem=wsem, wt=wtarget,
                              efm=ein_fm_v,
                              sprev=(hv[-2] if len(hv) >= 2 else sS.n)):
                            eng.wait_ge(wsem.h, wt)
                            eng.wait_ge(sV.h, efm)
                            eng.wait_ge(sS.h, sprev)
                            for kt in range(KT):
                                last = eng.matmul(ps_h[:, 0:512],
                                           w1s[q % 2][:, 1024 * kt + 128 * dd:1024 * kt + 128 * dd + 128],
                                           einFM[kt], start=(kt == 0), stop=(kt == KT - 1))
                            last.then_inc(sT.h, 1)
                        t_h = sT.bump()
                        w1_lastmm[(c5 * 4 + q) % 2] = t_h
                        @on("scalar")
                        def _(eng, dt_=dt_, ps_h=ps_h, t_h=t_h):
                            eng.wait_ge(sT.h, t_h)
                            eng.activation(hbuf[dt_], ps_h[:, 0:512], AF.Relu,
                                           bias=P_["b1_col"][:, dt_:dt_ + 1]).then_inc(sS.h, 1)
                        hv.append(sS.bump())
                h_done = sS.n
                @on("tensor")
                def _(eng, hd=h_done):
                    eng.wait_ge(sS.h, hd)
                    for stt in range(4):
                        last = eng.matmul(PS[stt][:, 0:512], P_["ones_r_b"][0:1, :],
                                   P_["b2row_b"][0:1, 0:512], start=True, stop=False)
                        last = eng.matmul(PS[stt][:, 512:1024], P_["ones_r_b"][0:1, :],
                                   P_["b2row_b"][0:1, 512:1024], start=True, stop=False)
                    last.then_inc(sT.h, 1)
                sT.bump()
                for q2 in range(8):
                    wsem = sW2 if q2 % 2 == 0 else sW3
                    @on("sync")
                    def _(eng, q2=q2, wsem=wsem, wc=wcache_done,
                          tprev=w2_lastmm.get((c5 * 8 + q2) % 2, 0)):
                        eng.wait_ge(sWC.h, wc)
                        if tprev:
                            eng.wait_ge(sT.h, tprev)
                        for i4 in range(4):
                            eng.dma_start(out=w2s[4 * (q2 % 2) + i4],
                                          in_=w2bf[512 * q2 + 128 * i4:512 * q2 + 128 * i4 + 128, :]
                                          ).then_inc(wsem.h, 16)
                    wsem.bump(64)
                    wt2 = wsem.n
                    @on("tensor")
                    def _(eng, q2=q2, wsem=wsem, wt2=wt2):
                        eng.wait_ge(wsem.h, wt2)
                        for i4 in range(4):
                            kt2 = 4 * q2 + i4
                            for stt in range(4):
                                last_k = (q2 == 7 and i4 == 3)
                                last = eng.matmul(PS[stt][:, 0:512],
                                           hbuf[kt2][:, 128 * stt:128 * stt + 128],
                                           w2s[4 * (q2 % 2) + i4][:, 0:512], start=False,
                                           stop=last_k)
                                last = eng.matmul(PS[stt][:, 512:1024],
                                           hbuf[kt2][:, 128 * stt:128 * stt + 128],
                                           w2s[4 * (q2 % 2) + i4][:, 512:1024], start=False,
                                           stop=last_k)
                        last.then_inc(sT.h, 1)
                    w2_lastmm[(c5 * 8 + q2) % 2] = sT.bump()
                t_eo = sT.n
                for stt in range(4):
                    par = stt % 2
                    eosem = sEO0 if par == 0 else sEO1
                    @on("scalar")
                    def _(eng, stt=stt, par=par, t_eo=t_eo, eosem=eosem,
                          gprev=(sEO0 if par == 0 else sEO1).n):
                        eng.wait_ge(sT.h, t_eo)
                        eng.wait_ge(eosem.h, gprev)
                        eng.copy(eo_sb[par], PS[stt][:, 0:1024]).then_inc(sS.h, 1)
                    s_eo = sS.bump()
                    @on("gpsimd")
                    def _(eng, c5=c5, stt=stt, par=par, s_eo=s_eo, eosem=eosem):
                        eng.wait_ge(sS.h, s_eo)
                        eng.dma_start(out=FA["eo_src"][512 * c5 + 128 * stt:
                                                       512 * c5 + 128 * stt + 128, :],
                                      in_=eo_sb[par]).then_inc(eosem.h, 16)
                    eosem.bump(16)
            eo_done0, eo_done1 = sEO0.n, sEO1.n
            @on("gpsimd")
            def _(eng, e0=eo_done0, e1=eo_done1):
                eng.wait_ge(sEO0.h, e0)
                eng.wait_ge(sEO1.h, e1)
                eng.collective_compute("AllGather", ALU.bypass,
                                       replica_groups=[list(range(NCORE))],
                                       ins=[FA["eo_src"].opt()], outs=[FA["eo"].opt()]
                                       ).then_inc(sCC.h, 1)
            ag_eo = sCC.bump()

            combTM = [abf(16384 + 1024 * i, 1024) for i in range(2)]
            if app == 0:
                @on("vector")
                def _(eng, vprev=sV.n):
                    eng.wait_ge(sV.h, vprev)
                    eng.memset(combTM[0], 0.0)
                    last = eng.memset(combTM[1], 0.0)
                    last.then_inc(sV.h, 1)
                sV.bump()
            comb_v = {"last": {0: 0, 1: 0}}
            for tt in range(NTT):
                par = tt % 2
                lastv = comb_v["last"][par]
                csem = sCMB0 if par == 0 else sCMB1
                @on("gpsimd")
                def _(eng, tt=tt, par=par, ag=ag_eo, vpre=(sV.n if tt < 2 else 0),
                      lastv=lastv, csem=csem):
                    eng.wait_ge(sCC.h, ag)
                    if vpre:
                        eng.wait_ge(sV.h, vpre)
                    if lastv:
                        eng.wait_ge(sV.h, lastv)
                    eng.indirect_dma_start(
                        out=combTM[par], out_offset=None, in_=FA["eo"],
                        in_offset=bass.IndirectOffsetOnAxis(
                            P_["ri"][:, 16 + tt:17 + tt], 0),
                        bounds_check=BR[NCORE * CAP - 1], oob_is_err=False,
                    ).then_inc(csem.h, 16)
                g_c = csem.bump(16)
                @on("vector")
                def _(eng, tt=tt, par=par, g_c=g_c, csem=csem):
                    eng.wait_ge(csem.h, g_c)
                    eng.tensor_scalar(combTM[par], combTM[par], RT[:, 64 + tt:65 + tt],
                                      None, ALU.mult).then_inc(sV.h, 1)
                v_sc = sV.bump()
                addv = 0
                for kt in range(KT):
                    tpar = kt % 2
                    @on("tensor")
                    def _(eng, tt=tt, par=par, kt=kt, tpar=tpar, v_sc=v_sc, addv=addv):
                        eng.wait_ge(sV.h, v_sc)
                        if addv:
                            eng.wait_ge(sV.h, addv)
                        last = eng.transpose(PSB[tpar][:, 256:384],
                                      combTM[par][:, 128 * kt:128 * kt + 128],
                                      P_["id128_b"][:])
                        last.then_inc(sT.h, 1)
                    t_tp = sT.bump()
                    @on("vector")
                    def _(eng, tt=tt, kt=kt, tpar=tpar, t_tp=t_tp):
                        eng.wait_ge(sT.h, t_tp)
                        eng.tensor_add(xres[kt][:, 128 * tt:128 * tt + 128],
                                       xres[kt][:, 128 * tt:128 * tt + 128],
                                       PSB[tpar][:, 256:384]).then_inc(sV.h, 1)
                    newv = sV.bump()
                    if tpar == 1:
                        addv = newv
                comb_v["last"][par] = sV.n
            barrier()

        emit_ffn(0)
        if DEV_STAGE == 2:
            emit_output()
            prog.install(block)
            return nc

        ln3bf = [abf(2048 * kt, 2048) for kt in range(KT)]
        emit_layernorm(xres, ln3bf, 16384)
        ln3_v = sV.n
        @on("gpsimd")
        def _(eng, lv=ln3_v):
            eng.wait_ge(sV.h, lv)
            for j in range(KT):
                eng.indirect_dma_start(
                    out=cc_ln3p,
                    out_offset=bass.IndirectOffsetOnAxis(P_["sc1_idx"][:, j:j + 1], 0),
                    in_=ln3bf[j], in_offset=None,
                    bounds_check=BR[TOK - 1], oob_is_err=False,
                ).then_inc(sLN3C.h, 16)
        c3 = sLN3C.bump(16 * KT)
        @on("gpsimd")
        def _(eng, c3=c3):
            eng.wait_ge(sLN3C.h, c3)
            eng.collective_compute("AllGather", ALU.bypass,
                                   replica_groups=PAIR_RG,
                                   ins=[bar_src.opt()], outs=[BARS[2].opt()]
                                   ).then_inc(sCC.h, 1)
        ag_ln3 = sCC.bump()
        barrier()

        Wf32 = [af32(8192 * i, 4096) for i in range(5)]
        ln3tile = abf(40960, 4096)
        hout_bf = abf(45056, 4096)

        NSL = 4
        chainH = {i: None for i in range(NSL)}

        def step2(en, fn):
            w = 4096 // NSL
            for hf in range(NSL):
                sem = {"vector": sV, "scalar": sS}[en]
                sl = slice(w * hf, w * hf + w)
                prev = chainH[hf]
                @on(en)
                def _(eng, fn=fn, prev=prev, sl=sl):
                    if prev is not None:
                        ps, pv = prev
                        eng.wait_ge(ps.h, pv)
                    fn(eng, sl).then_inc(sem.h, 1)
                chainH[hf] = (sem, sem.bump())

        for ct in range(4):
            @on("gpsimd")
            def _(eng, ct=ct, ag=ag_ln3, vprev=sV.n):
                eng.wait_ge(sCC.h, ag)
                eng.wait_ge(sV.h, vprev)
                for r in range(2):
                    eng.indirect_dma_start(
                        out=ln3tile[:, 2048 * r:2048 * r + 2048],
                        out_offset=None, in_=cc_ln3p,
                        in_offset=bass.IndirectOffsetOnAxis(
                            P_["ln3_idx"][:, 2 * ct + r:2 * ct + r + 1], 0),
                        bounds_check=BR[TOK - 1], oob_is_err=False,
                    ).then_inc(sL3G.h, 16)
            g_l3 = sL3G.bump(32)
            v_ct0 = sV.n
            cpv = []
            for gi in range(3):
                for fch in range(8):
                    j = gi * 8 + fch
                    slot = j % 4
                    ps_ap = PS[slot // 2][:, 512 * (slot % 2):512 * (slot % 2) + 512]
                    war = cpv[j - 4] if j >= 4 else v_ct0
                    @on("tensor")
                    def _(eng, gi=gi, ct=ct, fch=fch, ps_ap=ps_ap, g_l3=g_l3, war=war):
                        eng.wait_ge(sL3G.h, g_l3)
                        eng.wait_ge(sV.h, war)
                        last = eng.matmul(ps_ap,
                                   P_["wbd"][:, 128 * (4 * gi + ct):128 * (4 * gi + ct) + 128],
                                   ln3tile[:, 512 * fch:512 * fch + 512],
                                   start=True, stop=True)
                        last.then_inc(sT.h, 1)
                    t_g = sT.bump()
                    @on("vector")
                    def _(eng, gi=gi, fch=fch, ps_ap=ps_ap, t_g=t_g):
                        eng.wait_ge(sT.h, t_g)
                        eng.tensor_copy(Wf32[gi][:, 512 * fch:512 * fch + 512],
                                        ps_ap).then_inc(sV.h, 1)
                    cpv.append(sV.bump())
            for _i in range(NSL):
                chainH[_i] = (sV, sV.n)
            F_, I_, H_, T1, T2 = Wf32
            step2("scalar", lambda e, sl: e.activation(T1[:, sl], F_[:, sl], AF.Sigmoid))
            step2("scalar", lambda e, sl: e.activation(T2[:, sl], I_[:, sl], AF.Sigmoid))
            step2("scalar", lambda e, sl: e.activation(F_[:, sl], H_[:, sl], AF.Sigmoid))
            step2("vector", lambda e, sl: e.tensor_add(I_[:, sl], T1[:, sl], T2[:, sl]))
            step2("scalar", lambda e, sl: e.activation(I_[:, sl], I_[:, sl], AF.Ln))
            step2("scalar", lambda e, sl: e.activation(T1[:, sl], T1[:, sl], AF.Ln))
            step2("scalar", lambda e, sl: e.activation(T2[:, sl], T2[:, sl], AF.Ln))
            step2("scalar", lambda e, sl: e.activation(F_[:, sl], F_[:, sl], AF.Ln))
            step2("vector", lambda e, sl: e.tensor_sub(T1[:, sl], T1[:, sl], I_[:, sl]))
            step2("vector", lambda e, sl: e.tensor_sub(T2[:, sl], T2[:, sl], I_[:, sl]))
            step2("scalar", lambda e, sl: e.activation(T1[:, sl], T1[:, sl], AF.Exp, bias=CB[:, 1:2]))
            step2("scalar", lambda e, sl: e.activation(T2[:, sl], T2[:, sl], AF.Exp))
            step2("scalar", lambda e, sl: e.activation(F_[:, sl], F_[:, sl], AF.Exp, bias=CB[:, 1:2]))
            step2("scalar", lambda e, sl: e.activation(I_[:, sl], H_[:, sl], AF.Relu))
            step2("scalar", lambda e, sl: e.add(I_[:, sl], I_[:, sl], CB[:, 2:3]))
            step2("vector", lambda e, sl: e.tensor_single_scalar(H_[:, sl], H_[:, sl], 0.0, ALU.is_ge))
            step2("vector", lambda e, sl: e.tensor_sub(I_[:, sl], I_[:, sl], F_[:, sl]))
            step2("vector", lambda e, sl: e.tensor_mul(I_[:, sl], H_[:, sl], I_[:, sl]))
            step2("vector", lambda e, sl: e.tensor_add(I_[:, sl], I_[:, sl], F_[:, sl]))
            step2("vector", lambda e, sl: e.tensor_mul(F_[:, sl], T2[:, sl], I_[:, sl]))
            w = 4096 // NSL
            prev_scan = None
            for hf in range(NSL):
                prevc = chainH[hf]
                ini = 0.0 if hf == 0 else H_[:, w * hf - 1:w * hf]
                @on("vector")
                def _(eng, prevc=prevc, prev_scan=prev_scan, hf=hf, ini=ini):
                    ps, pv = prevc
                    eng.wait_ge(ps.h, pv)
                    if prev_scan is not None:
                        eng.wait_ge(prev_scan[0].h, prev_scan[1])
                    eng.tensor_tensor_scan(H_[:, w * hf:w * hf + w],
                                           T1[:, w * hf:w * hf + w],
                                           F_[:, w * hf:w * hf + w],
                                           ini, ALU.mult, ALU.add).then_inc(sV.h, 1)
                chainH[hf] = (sV, sV.bump())
                prev_scan = chainH[hf]
            pends = [chainH[i] for i in range(NSL)]
            @on("vector")
            def _(eng, pends=tuple(pends), hcprev=sHC.n):
                for ps_, pv_ in pends:
                    eng.wait_ge(ps_.h, pv_)
                eng.wait_ge(sHC.h, hcprev)
                eng.tensor_copy(hout_bf[:, :], H_[:, :]).then_inc(sV.h, 1)
            for _i in range(NSL):
                chainH[_i] = (sV, sV.bump()) if _i == 0 else chainH[0]
            hv = sV.n
            @on("gpsimd")
            def _(eng, ct=ct, hv=hv):
                eng.wait_ge(sV.h, hv)
                eng.indirect_dma_start(
                    out=cc_hp,
                    out_offset=bass.IndirectOffsetOnAxis(P_["sch_idx"][:, ct:ct + 1], 0),
                    in_=hout_bf[:, :], in_offset=None,
                    bounds_check=BR[D - 1], oob_is_err=False,
                ).then_inc(sHC.h, 16)
            sHC.bump(16)
        h_contrib = sHC.n
        @on("gpsimd")
        def _(eng, hc=h_contrib):
            eng.wait_ge(sHC.h, hc)
            eng.collective_compute("AllGather", ALU.bypass,
                                   replica_groups=PAIR_RG,
                                   ins=[bar_src.opt()], outs=[BARS[3].opt()]
                                   ).then_inc(sCC.h, 1)
        ag_h = sCC.bump()
        ccH_v = cc_hp.rearrange("r (h t) -> (r h) t", h=2)
        hadd = [abf(2048 * kt, 2048) for kt in range(KT)]
        @on("gpsimd")
        def _(eng, ag=ag_h, vprev=sV.n):
            eng.wait_ge(sCC.h, ag)
            eng.wait_ge(sV.h, vprev)
            for j in range(KT):
                eng.indirect_dma_start(
                    out=hadd[j], out_offset=None, in_=ccH_v,
                    in_offset=bass.IndirectOffsetOnAxis(P_["h_idx"][:, j:j + 1], 0),
                    bounds_check=BR[TOK - 1], oob_is_err=False,
                ).then_inc(sHG.h, 16)
        g_h = sHG.bump(16 * KT)
        @on("vector")
        def _(eng, g_h=g_h):
            eng.wait_ge(sHG.h, g_h)
            for kt in range(KT):
                last = eng.tensor_add(xres[kt], xres[kt], hadd[kt])
            last.then_inc(sV.h, 1)
        sV.bump()
        barrier()

        if DEV_STAGE == 3:
            emit_output()
            prog.install(block)
            return nc

        emit_ffn(1)
        emit_output()
        prog.install(block)
    return nc



_NC_CACHE = {}


def _host_inputs(inp, k):
    s, h2 = k // 2, k % 2
    bs, c2 = k // 2, k % 2
    f32 = np.float32
    d = {}
    x0 = np.asarray(inp["L0_attn_a_x"], f32)
    d["x0_my"] = np.ascontiguousarray(x0[s, h2 * TOK:(h2 + 1) * TOK, :].T)
    wqkv = np.asarray(inp["L0_attn_a_w_qkv"], f32)
    cols = np.r_[512 * h2:512 * h2 + 512,
                 1024 + 512 * h2:1024 + 512 * h2 + 512,
                 2048 + 512 * h2:2048 + 512 * h2 + 512]
    d["wqkv_my"] = np.ascontiguousarray(wqkv[:, cols])
    bq = np.asarray(inp["L0_attn_a_b_qkv"], f32)
    bq_my = bq[cols]
    d["bqkv_col"] = np.ascontiguousarray(bq_my[:1024].reshape(8, 128).T)
    d["bqkv_vrow"] = np.ascontiguousarray(bq_my[1024:1536].reshape(1, 512))
    d["w_out"] = np.ascontiguousarray(np.asarray(inp["L0_attn_a_w_out"], f32))
    d["bout_col"] = np.ascontiguousarray(
        np.asarray(inp["L0_attn_a_b_out"], f32).reshape(8, 128).T)
    d["wg_t"] = np.ascontiguousarray(
        np.asarray(inp["L0_ffn_wg"], f32).reshape(8, 128, 8).transpose(1, 0, 2)
        .reshape(128, 64))
    d["w1_my"] = np.ascontiguousarray(np.asarray(inp["L0_ffn_w1"], f32)[k])
    d["w2_my"] = np.ascontiguousarray(np.asarray(inp["L0_ffn_w2"], f32)[k])
    d["b1_col"] = np.ascontiguousarray(
        np.asarray(inp["L0_ffn_b1"], f32)[k].reshape(32, 128).T)
    d["b2_row"] = np.ascontiguousarray(
        np.asarray(inp["L0_ffn_b2"], f32)[k].reshape(1, D))
    wbd = np.zeros((12, 128, 128), f32)
    gates = [np.asarray(inp["L0_attn_b_Wf"], f32),
             np.asarray(inp["L0_attn_b_Wi"], f32),
             np.asarray(inp["L0_attn_b_Wh"], f32)]
    for gi in range(3):
        for t in range(4):
            for j in range(16):
                g = 64 * c2 + 16 * t + j
                wbd[4 * gi + t, 8 * j:8 * j + 8, 8 * j:8 * j + 8] = gates[gi][g].T
    d["wbd_my"] = np.ascontiguousarray(wbd.reshape(12 * 128, 128))
    d["consts"] = np.broadcast_to(
        np.array([1e-5, 1e-8, 0.50000001, 1.0, 0, 0, 0, 0], f32), (128, 8)).copy()
    ii = np.arange(128)
    d["tri128i_f"] = (ii[:, None] <= ii[None, :]).astype(f32)
    d["tri128i_b"] = d["tri128i_f"].astype(ml_dtypes.bfloat16)
    d["id128_f"] = np.eye(128, dtype=f32)
    d["id128_b"] = d["id128_f"].astype(ml_dtypes.bfloat16)
    r8 = np.arange(8)
    d["stri8"] = (r8[:, None] < r8[None, :]).astype(f32)
    d["tri8i"] = (r8[:, None] <= r8[None, :]).astype(f32)
    d["ones_c_f"] = np.ones((128, 1), f32)
    d["ones_c_b"] = np.ones((128, 1), ml_dtypes.bfloat16)
    d["ones_r_f"] = np.ones((1, 128), f32)
    d["ones_r_b"] = np.ones((1, 128), ml_dtypes.bfloat16)
    d["colidx8"] = np.broadcast_to(r8.astype(f32), (128, 8)).copy()
    oh = np.zeros((8, 1), f32); oh[k] = 1.0
    d["rank1h"] = oh
    d["rank2048"] = (TOK * np.arange(8, dtype=f32)).reshape(8, 1)
    d["c_iota"] = (np.arange(20)[None, :] * 128 + ii[:, None]).astype(f32)
    tt_ = np.arange(128) // 8
    ee_ = np.arange(128) % 8
    d["ones_sq"] = np.ones((128, 128), f32)
    d["tileoff_m"] = ((tt_[:, None] < tt_[None, :]) &
                      (ee_[:, None] == ee_[None, :])).astype(f32)
    d["esum_m"] = (ee_[:, None] == r8[None, :]).astype(f32)
    ch = np.arange(1024)
    ln1i = np.arange(2 * D).astype(np.int32)
    d["ln1_idx"] = np.ascontiguousarray(ln1i.reshape(16, 128).T)
    d["sc1_idx"] = np.ascontiguousarray(
        (D * h2 + np.arange(D)).astype(np.int32).reshape(8, 128).T)
    sco = np.zeros((128, 8), np.int32)
    sco[0:64, :] = 512 * h2 + 64 * np.arange(8)[None, :] + np.arange(64)[:, None]
    d["scO_idx"] = sco
    d["sch_idx"] = np.ascontiguousarray(
        (512 * c2 + np.arange(512)).astype(np.int32).reshape(4, 128).T)
    oi = (2 * ch + h2).astype(np.int32)
    d["o_idx"] = np.ascontiguousarray(oi.reshape(8, 128).T)
    l3 = np.zeros((128, 8), np.int32)
    for ct in range(4):
        for r in range(2):
            l3[:, 2 * ct + r] = 1024 * r + 512 * c2 + 128 * ct + ii
    d["ln3_idx"] = l3
    hi = (2 * ch + h2).astype(np.int32)
    d["h_idx"] = np.ascontiguousarray(hi.reshape(8, 128).T)
    return d


def kernel(**inputs):
    global _NC_CACHE
    key = DEV_STAGE
    if key not in _NC_CACHE:
        _NC_CACHE[key] = build_nc()
    nc = _NC_CACHE[key]
    in_maps = [_host_inputs(inputs, k) for k in range(NCORE)]
    res = run_bass_kernel_spmd(nc, in_maps, core_ids=list(range(NCORE)))
    out = np.empty((B, S, D), np.float32)
    for k in range(NCORE):
        s, h2 = k // 2, k % 2
        out[s, h2 * TOK:(h2 + 1) * TOK, :] = res.results[k]["out_x"].T
    return out
